# version 17 (final)
# speedup vs baseline: 1.2283x; 1.0143x over previous
import contextlib
import numpy as np
import concourse.bass as bass
import concourse.mybir as mybir
from concourse.alu_op_type import AluOpType as OP

F32 = mybir.dt.float32
BF16 = mybir.dt.bfloat16
I32 = mybir.dt.int32
AF = mybir.ActivationFunctionType

ENGS = ("tensor", "vector", "scalar", "gpsimd", "sync")
NEG = -1.0e30
QMAX = 32767.0
OFFS = 32768.0
JCOL = {"vector": 0, "scalar": 1, "gpsimd": 2, "sync": 3, "tensor": 4}


class Cfg:
    def __init__(self, T=2048, WIN=512, I_FFN=3072, HID=4096, n_cores=8,
                 upto="p4", debug_outputs=()):
        self.D = 1024
        self.CC = 8
        self.H = 16
        self.DH = 64
        self.NP = 16
        self.B = 2
        self.T = T
        self.R = self.B * T
        self.WIN = WIN
        self.SW = WIN + 128
        self.NKT = T // 128
        self.I_FFN = I_FFN
        self.IC = I_FFN // 128
        self.HID = HID
        self.HC = HID // 128
        self.n_cores = n_cores
        self.upto = upto
        self.debug_outputs = set(debug_outputs)
        assert T % 512 == 0 and self.WIN % 128 == 0
        assert I_FFN % 128 == 0 and HID % 128 == 0


class Ch:
    def __init__(self, h):
        self.h = h
        self.n = 0


class Reg:
    def __init__(self, h, off, n, p=128):
        self.h, self.off, self.n, self.p = h, off, n, p

    def __call__(self, c0=0, c1=None, p0=0, p1=None):
        c1 = self.n if c1 is None else c1
        p1 = self.p if p1 is None else p1
        return self.h[p0:p1, self.off + c0: self.off + c1]

    @property
    def ap(self):
        return self()

    def sub(self, c0, c1, p=None):
        return Reg(self.h, self.off + c0, c1 - c0, self.p if p is None else p)


class Arena:
    def __init__(self, handle, total):
        self.h = handle
        self.total = total
        self.off = 0

    def alloc(self, n):
        assert self.off + n <= self.total, f"arena overflow {self.off}+{n}>{self.total}"
        r = Reg(self.h, self.off, n)
        self.off += n
        return r

    def reset(self):
        self.off = 0


def _ap_span(ap):
    try:
        name = ap.tensor.name
    except Exception:
        return None
    try:
        off = int(ap.offset)
        dims = [list(d) for d in ap.ap]
        ext = 0
        for st, cnt in dims[1:]:
            ext += abs(int(st)) * (int(cnt) - 1)
        return (name, off, off + ext + 1)
    except Exception:
        return (name, 0, 1 << 60)


class MKBase:
    def __init__(self, cfg: Cfg):
        self.cfg = cfg
        self.ops = {e: [] for e in ENGS}
        self.stack = contextlib.ExitStack()
        self.nc = bass.Bass("TRN2", target_bir_lowering=False, debug=False)
        self.sb = {}
        self.dram = {}
        self.chs = {}
        self._w = {e: [] for e in ENGS}
        self._r = {e: [] for e in ENGS}
        self.pmax_col = 0

    def ch(self, name):
        if name not in self.chs:
            self.chs[name] = Ch(self.stack.enter_context(self.nc.semaphore(name)))
        return self.chs[name]

    NGEN = 44

    def chp(self, name):
        if not hasattr(self, "_chp_alias"):
            self._chp_alias = {}
            self._chp_next = 0
        if name not in self._chp_alias:
            i = self._chp_next
            assert i < self.NGEN, "generic channel pool exhausted"
            self._chp_next = i + 1
            self._chp_alias[name] = self.ch("g%d" % i)
        return self._chp_alias[name]

    NDCH = {"sync": 10, "gpsimd": 8, "scalar": 6}

    def dch(self, q, name):
        key = "_dch_" + q
        if not hasattr(self, key):
            setattr(self, key, ({}, [0]))
        alias, nxt = getattr(self, key)
        if name not in alias:
            i = nxt[0]
            assert i < self.NDCH[q], "dma channel pool exhausted for " + q
            nxt[0] = i + 1
            alias[name] = self.ch("d_%s_%d" % (q, i))
        return alias[name]

    def dram_t(self, name, shape, dtype, kind="Internal"):
        ap = self.nc.dram_tensor(name, list(shape), dtype, kind=kind).ap()
        self.dram[name] = ap
        return ap

    def sbuf(self, name, shape, dtype):
        h = self.stack.enter_context(self.nc.sbuf_tensor(name, list(shape), dtype))
        self.sb[name] = h
        return h

    def emit(self, eng, fn):
        fn._ph = getattr(self, "_phase", "init")
        self.ops[eng].append(fn)

    def phase(self, name):
        self._phase = name

    @staticmethod
    def _overlap(a, b):
        return a[0] == b[0] and a[1] < b[2] and b[1] < a[2]

    def _hazard(self, eng, reads, writes):
        if eng in ("tensor", "sync"):
            return
        need = False
        for s in reads:
            if s is None:
                continue
            for w in self._w[eng]:
                if self._overlap(s, w):
                    need = True
                    break
            if need:
                break
        if not need:
            for s in writes:
                if s is None:
                    continue
                for w in self._w[eng] + self._r[eng]:
                    if self._overlap(s, w):
                        need = True
                        break
                if need:
                    break
        if need:
            self.emit(eng, lambda E: E.drain())
            self._w[eng] = []
            self._r[eng] = []
        self._r[eng].extend([s for s in reads if s is not None])
        self._w[eng].extend([s for s in writes if s is not None])

    def _spans(self, aps):
        out = []
        for a in aps:
            if a is None or isinstance(a, (int, float)):
                continue
            out.append(_ap_span(a))
        return out

    def _inc(self, instr, inc):
        if inc is not None:
            ch, k = inc if isinstance(inc, tuple) else (inc, 1)
            instr.then_inc(ch.h, k)

    def _cnt(self, inc):
        if inc is not None:
            ch, k = inc if isinstance(inc, tuple) else (inc, 1)
            ch.n += k

    def _op(self, eng, reads, writes, inc, builder):
        self._hazard(eng, self._spans(reads), self._spans(writes))
        self._cnt(inc)
        def f(E, builder=builder, inc=inc):
            i = builder(E)
            self._inc(i, inc)
        self.emit(eng, f)

    def wait(self, eng, ch, at=None):
        v = ch.n if at is None else at
        self.emit(eng, lambda E, h=ch.h, v=v: E.wait_ge(h, v))

    def drain(self, eng):
        self.emit(eng, lambda E: E.drain())
        self._w[eng] = []
        self._r[eng] = []

    def seal(self, eng, inc):
        self.drain(eng)
        self._cnt(inc)
        ch, k = inc if isinstance(inc, tuple) else (inc, 1)
        if eng == "tensor":
            def f(E, h=ch.h, k=k):
                E.nop().then_inc(h, k)
        elif eng == "scalar":
            col = JCOL[eng]
            def f(E, h=ch.h, k=k, col=col):
                E.memzero(self.sb["junk"][0:1, col:col + 1]).then_inc(h, k)
        else:
            col = JCOL[eng]
            def f(E, h=ch.h, k=k, col=col):
                E.memset(self.sb["junk"][0:1, col:col + 1], 0.0).then_inc(h, k)
        self.emit(eng, f)

    def qch(self, q):
        return self.ch("q_" + q)

    NPOOL = 6

    def qpool(self, q):
        key = "_qp_" + q
        if not hasattr(self, key):
            setattr(self, key, [self.ch("qp_%s_%d" % (q, i))
                                for i in range(self.NPOOL)])
            setattr(self, "_qprr_" + q, 0)
        return getattr(self, key)

    def dma(self, q, out, in_, inc=None, transpose=False):
        if inc is None:
            pool = self.qpool(q)
            rr = getattr(self, "_qprr_" + q)
            ch = pool[rr % self.NPOOL]
            setattr(self, "_qprr_" + q, rr + 1)
            k = 16
        else:
            ch, k = inc if isinstance(inc, tuple) else (inc, 16)
        if ch.n > 0:
            self.wait(q, ch, at=ch.n)
        ch.n += k
        def f(E, out=out, in_=in_, h=ch.h, k=k, t=transpose):
            if t:
                E.dma_start(out=out, in_=in_, transpose=True).then_inc(h, k)
            else:
                E.dma_start(out=out, in_=in_).then_inc(h, k)
        self.emit(q, f)
        return (ch, ch.n)

    def wtok(self, eng, tok):
        ch, n = tok
        self.wait(eng, ch, at=n)

    def wait_queue(self, eng, q):
        for ch in self.qpool(q):
            if ch.n > 0:
                self.wait(eng, ch)
        qc = self.qch(q)
        if qc.n > 0:
            self.wait(eng, qc)

    def mm(self, out, lhsT, rhs, start, stop, inc=None, tp=None):
        self._cnt(inc)
        def f(E, out=out, l=lhsT, r=rhs, s=start, p=stop, inc=inc, tp=tp):
            i = E.matmul(out, l, r, start=s, stop=p, tile_position=tp)
            self._inc(i, inc)
        self.emit("tensor", f)

    def act(self, out, in_, func, scale=1.0, bias=None, accum=None, inc=None):
        rs = [in_, scale if not isinstance(scale, (int, float)) else None,
              bias if not isinstance(bias, (int, float)) else None]
        ws = [out] + ([accum] if accum is not None else [])
        def b(E, out=out, in_=in_, func=func, scale=scale, bias=bias, accum=accum):
            kw = {}
            if bias is not None:
                kw["bias"] = bias
            if accum is not None:
                kw["accum_out"] = accum
            return E.activation(out, in_, func, scale=scale, **kw)
        self._op("scalar", rs, ws, inc, b)

    def sign_op(self, out, in_, inc=None):
        self._op("scalar", [in_], [out], inc, lambda E, o=out, i=in_: E.sign(o, i))

    def cp(self, eng, out, in_, inc=None):
        self._op(eng, [in_], [out], inc, lambda E, o=out, i=in_: E.tensor_copy(o, i))

    def tt(self, eng, out, a, b, op, inc=None):
        self._op(eng, [a, b], [out], inc,
                 lambda E, o=out, a=a, b=b, op=op: E.tensor_tensor(o, a, b, op))

    def ts(self, eng, out, in0, s1, s2, op0, op1=None, accum=None, inc=None):
        rs = [in0] + [x for x in (s1, s2)
                      if x is not None and not isinstance(x, (int, float))]
        ws = [out] + ([accum] if accum is not None else [])
        def b(E, out=out, in0=in0, s1=s1, s2=s2, op0=op0, op1=op1, accum=accum):
            kw = {}
            if op1 is not None:
                kw["op1"] = op1
            if accum is not None:
                kw["accum_out"] = accum
            return E.tensor_scalar(out, in0, s1, s2, op0, **kw)
        self._op(eng, rs, ws, inc, b)

    def stt(self, eng, out, in0, scalar, in1, op0, op1, inc=None):
        rs = [in0, in1] + ([scalar] if not isinstance(scalar, (int, float)) else [])
        def b(E, out=out, in0=in0, s=scalar, in1=in1, op0=op0, op1=op1):
            return E.scalar_tensor_tensor(out, in0, s, in1, op0, op1)
        self._op(eng, rs, [out], inc, b)

    def scan(self, out, d0, d1, init, inc=None):
        rs = [d0, d1] + ([init] if not isinstance(init, (int, float)) else [])
        def b(E, out=out, d0=d0, d1=d1, init=init):
            return E.tensor_tensor_scan(out, d0, d1, init, OP.mult, OP.add)
        self._op("vector", rs, [out], inc, b)

    def red(self, eng, out, in_, op, absval=False, inc=None):
        def b(E, out=out, in_=in_, op=op, a=absval):
            return E.tensor_reduce(out, in_, axis=mybir.AxisListType.X, op=op,
                                   apply_absolute_value=(True if a else None))
        self._op(eng, [in_], [out], inc, b)

    def recip(self, out, in_, inc=None):
        self._op("vector", [in_], [out], inc,
                 lambda E, o=out, i=in_: E.reciprocal(o, i))

    def memset(self, eng, ap, val, inc=None):
        self._op(eng, [], [ap], inc, lambda E, a=ap, v=val: E.memset(a, v))

    def collective(self, op, ins, outs, inc=None):
        rg = [[i for i in range(self.cfg.n_cores)]]
        self._cnt(inc)
        def f(E, ins=ins, outs=outs, op=op, rg=rg, inc=inc):
            i = E.collective_compute("AllReduce", op, replica_groups=rg,
                                     ins=ins, outs=outs)
            self._inc(i, inc)
        self.emit("gpsimd", f)

    def barrier(self, name):
        pb = self.ch("bar")
        for q in ("sync", "gpsimd", "scalar"):
            self.wait_queue("gpsimd", q)
            key = "_dch_" + q
            if hasattr(self, key):
                for chx in getattr(self, key)[0].values():
                    if chx.n > 0:
                        self.wait("gpsimd", chx)
        self._chp_alias = {}
        self._chp_next = 0
        for q in ("sync", "gpsimd", "scalar"):
            key = "_dch_" + q
            if hasattr(self, key):
                getattr(self, key)[0].clear()
                getattr(self, key)[1][0] = 0
        for e in ("tensor", "vector", "scalar", "gpsimd"):
            self.seal(e, pb)
        for e in ENGS:
            self.wait(e, pb)

    def pp_col(self, i):
        return self.sb["pp"][:, i:i + 1]

    def s11c(self, i):
        return self.sb["s11"][0:1, i:i + 1]

    def pcol(self, name, cc):
        i = self.pidx[name]
        return self.sb["params"][:, i * self.cfg.CC + cc: i * self.cfg.CC + cc + 1]

    def bcast_to_pp(self, s11_idx, pp_idx):
        ps = self.psum[0:128, 3584:3585]
        cm = self.ch("ppmm")
        cf = self.ch("ppfree")
        self.wait("tensor", cf)
        self.mm(ps, self.sb["onesrow"][0:1, 0:128], self.s11c(s11_idx), True, True,
                inc=cm)
        self.wait("vector", cm)
        self.cp("vector", self.pp_col(pp_idx), ps, inc=cf)
        self.seal("vector", self.ch("ppv"))

    def track_max(self, src_ap, inc=None):
        col = self.pmax_col
        self.pmax_col += 1
        assert col < 448
        self.red("vector", self.sb["pmaxw"][:, col:col + 1], src_ap, OP.max,
                 absval=True, inc=inc)
        return col

    def finish_max(self, groups, tag):
        n = len(groups)
        cv = self.ch("fm_v" + tag)
        cm = self.ch("fm_m" + tag)
        cg = self.ch("fm_g" + tag)
        cc = self.ch("fm_c" + tag)
        self.drain("vector")
        for i, (lo, hi, si) in enumerate(groups):
            self.red("vector", self.sb["pmaxcol"][:, i:i + 1],
                     self.sb["pmaxw"][:, lo:hi], OP.max)
        self.seal("vector", cv)
        self.wait("tensor", cv)
        for i in range(n):
            self.mm(self.psum[0:1, i * 512:i * 512 + 128],
                    self.sb["pmaxcol"][:, i:i + 1], self.sb["ident"][:, :],
                    True, True, inc=cm)
        self.wait("vector", cm)
        for i in range(n):
            self.red("vector", self.sb["s11"][0:1, 24 + i:25 + i],
                     self.psum[0:1, i * 512:i * 512 + 128], OP.max)
        self.seal("vector", cv)
        self.wait("gpsimd", cv)
        self.dma("gpsimd", self.dram["ccin" + tag][0:1, 0:n],
                 self.sb["s11"][0:1, 24:24 + n], (cg, 16))
        self.wait("gpsimd", cg)
        self.collective(OP.max, [self.dram["ccin" + tag][:, :]],
                        [self.dram["ccout" + tag][:, :]], inc=cc)
        self.wait("gpsimd", cc)
        self.dma("gpsimd", self.sb["s11"][0:1, 24:24 + n],
                 self.dram["ccout" + tag][0:1, 0:n], (cg, 16))
        self.wait("vector", cg)
        for i, (lo, hi, si) in enumerate(groups):
            self.cp("vector", self.s11c(si), self.sb["s11"][0:1, 24 + i:25 + i])
        self.pmax_col = 0

    def ln_rows(self, sum_ap, sq_ap, W, mneg_out, rstd_out, invD):
        rows = self.sb["rows"]
        ms = rows[0:1, 1024:1024 + W]
        var = rows[0:1, 1536:1536 + W]
        self.ts("vector", mneg_out, sum_ap, -invD, None, OP.mult)
        self.tt("vector", ms, mneg_out, mneg_out, OP.mult)
        cv = self.ch("lnr_v")
        self.stt("vector", var, sq_ap, invD, ms, OP.mult, OP.subtract)
        self.seal("vector", cv)
        self.wait("scalar", cv)
        cs = self.ch("lnr_s")
        self.act(var, var, AF.Ln, scale=1.0, bias=1e-5)
        self.act(rstd_out, var, AF.Exp, scale=-0.5)
        self.seal("scalar", cs)
        self.wait("vector", cs)
        self.wait("tensor", cs)

    def declare(self):
        c = self.cfg
        D, R, T, I = c.D, c.R, c.T, c.I_FFN
        dbg = c.debug_outputs
        kindof = lambda n: "ExternalOutput" if n in dbg else "Internal"
        self.dram_t("xPM", [128, c.CC * R], F32, kind="ExternalInput")
        self.dram_t("wqkvPM", [128, c.CC * 3 * D], F32, kind="ExternalInput")
        self.dram_t("woutPM", [128, c.CC * D], F32, kind="ExternalInput")
        self.dram_t("wup_nPM", [128, c.IC * D], F32, kind="ExternalInput")
        self.dram_t("wgate_nPM", [128, c.IC * D], F32, kind="ExternalInput")
        self.dram_t("wdown_nPM", [128, c.CC * I], F32, kind="ExternalInput")
        for n in ("wkPM", "wvPM", "wrPM", "woPM", "wcrPM", "wsPM"):
            self.dram_t(n, [128, c.CC * D], F32, kind="ExternalInput")
        self.dram_t("wckPM", [128, c.CC * c.HID], F32, kind="ExternalInput")
        self.dram_t("wcvPM", [128, c.HC * D], F32, kind="ExternalInput")
        for n in ("ln1g", "ln1b", "ln2g", "ln2b", "td", "tf", "tmk", "tmv", "tmr",
                  "cmk", "cmr"):
            self.dram_t(n + "_in", [128, c.CC], F32, kind="ExternalInput")
        self.dram_t("ident_in", [128, 128], F32, kind="ExternalInput")
        self.dram_t("maskLR_in", [128, c.SW], F32, kind="ExternalInput")
        self.dram_t("outPM", [128, c.CC * R], F32, kind="ExternalOutput")
        scr = [("resid1", c.CC * R, F32), ("resid2", c.CC * R, F32),
               ("avscr", c.CC * R, BF16),
               ("spillA", c.IC * R, BF16), ("spillB", c.IC * R, BF16),
               ("spillC", c.IC * R, BF16), ("spillD", c.IC * R, F32),
               ("wdqA_n", c.IC * D, BF16), ("wdqB_n", c.IC * D, BF16),
               ("wdqC_n", c.CC * I, BF16),
               ("kscr", c.CC * R, F32), ("vscr", c.CC * R, F32),
               ("xprime", c.CC * R, F32), ("pre4", c.CC * R, F32),
               ("s4", c.CC * R, F32),
               ("mixkscr", c.CC * R, BF16), ("mixrscr", c.CC * R, BF16)]
        for n, w, dt in scr:
            self.dram_t(n, [128, w], dt, kind=kindof(n))
        for n in ("stat1", "stat2", "stat3", "stat4"):
            self.dram_t(n, [2, R], F32, kind=kindof(n))
        for t, w in (("1", 1), ("2", 2), ("3", 1), ("4", 1)):
            self.dram_t("ccin" + t, [1, w], F32)
            self.dram_t("ccout" + t, [1, w], F32)

        self.psum = self.stack.enter_context(
            self.nc.psum_tensor("PS", [128, 4096], F32))[:, :]
        self.sbuf("ident", [128, 128], F32)
        self.sbuf("onesrow", [1, 128], F32)
        self.sbuf("onescolb", [128, 1], BF16)
        self.sbuf("onesb64", [128, 64], BF16)
        self.sbuf("identb", [128, 128], BF16)
        self.sbuf("maskb", [128, c.SW], BF16)
        self.sbuf("params", [128, 16 * c.CC], F32)
        self.sbuf("pp", [128, 32], F32)
        self.sbuf("rows", [1, 2048], F32)
        self.sbuf("s11", [1, 32], F32)
        self.sbuf("junk", [1, 8], F32)
        self.sbuf("pmaxw", [128, 448], F32)
        self.sbuf("pmaxcol", [128, 8], F32)
        self.sbuf("dcol", [128, c.CC], F32)
        self.sbuf("constv", [128, 2], F32)
        self.nc.const_aps.register(F32, 1e-5, self.sb["constv"][:, 0:1])
        self.sbuf("tails", [128, 2 * c.CC], F32)
        self.sbuf("pers", [16, T], BF16)
        self.AB = Arena(self.sbuf("AB", [128, 61440], BF16), 61440)
        self.AF = Arena(self.sbuf("AF", [128, 13824], F32), 13824)
        self.AI = Arena(self.sbuf("AI", [128, 4096], I32), 4096)

    def bank(self, i, w=512):
        return self.psum[:, i * 512: i * 512 + w]

    def bank1(self, i, w=512):
        return self.psum[0:1, i * 512: i * 512 + w]

    def prologue(self):
        self.phase("prologue")
        c = self.cfg
        cc0 = self.ch("const")
        self.dma("sync", self.sb["ident"][:, :], self.dram["ident_in"][:, :], (cc0, 16))
        ccg = self.ch("constg")
        self.dma("gpsimd", self.sb["maskb"][:, :], self.dram["maskLR_in"][:, :],
                 (ccg, 16))
        pn = ("ln1g", "ln1b", "ln2g", "ln2b", "td", "tf", "tmk", "tmv", "tmr",
              "cmk", "cmr")
        self.pidx = {n: i for i, n in enumerate(pn)}
        for n in pn:
            i = self.pidx[n]
            self.dma("sync", self.sb["params"][:, i * c.CC:(i + 1) * c.CC],
                     self.dram[n + "_in"][:, :], (cc0, 16))
        self.wait("vector", cc0)
        self.wait("vector", self.ch("constg"))
        self.memset("vector", self.sb["onesrow"][:, :], 1.0)
        self.memset("vector", self.sb["onescolb"][:, :], 1.0)
        self.memset("vector", self.sb["onesb64"][:, :], 1.0)
        self.memset("vector", self.sb["pmaxw"][:, :], 0.0)
        self.memset("vector", self.sb["tails"][:, :], 0.0)
        self.memset("vector", self.sb["constv"][:, 0:1], 1e-5)
        self.wait("scalar", cc0)
        self.act(self.sb["identb"][:, :], self.sb["ident"][:, :], AF.Copy)
        td = self.sb["params"][:, self.pidx["td"] * c.CC:(self.pidx["td"] + 1) * c.CC]
        self.act(self.sb["dcol"][:, :], td, AF.Exp)
        self.act(self.sb["dcol"][:, :], self.sb["dcol"][:, :], AF.Exp, scale=-1.0)
        cs = self.ch("constdone")
        self.seal("scalar", cs)
        self.seal("vector", cs)
        self.const_done = cs
        for e in ("tensor", "vector", "scalar", "gpsimd"):
            self.wait(e, cs)


class MKP1(MKBase):
    def dequant_piece(self, src_name, dst_name, piece_off, seq):
        _ph = getattr(self, '_phase', 'init'); self.phase('dq')
        buf = seq % 2
        NC = 1024
        cload = self.dch("sync", "dq_ld")
        cred = self.chp("dq_red")
        cg1 = self.chp("dq_g1")
        csc = self.chp("dq_sc")
        cg2 = self.chp("dq_g2")
        cst = self.dch("sync", "dq_st")
        w = self.dq_wnat.sub(buf * NC, (buf + 1) * NC)
        wsc = self.dq_wsc.sub(buf * 64, buf * 64 + 32)
        wrc = self.dq_wsc.sub(buf * 64 + 32, buf * 64 + 64)
        wqi = self.dq_wqi.sub(buf * NC, (buf + 1) * NC)
        wqf = self.dq_wqf.sub(buf * NC, (buf + 1) * NC)
        wqb = self.dq_wqb.sub(0, NC)
        nblk = NC // 32
        if seq >= 2:
            self.wait("sync", cst, at=cst.n)
        self.dma("sync", w(), self.dram[src_name][:, piece_off:piece_off + NC],
                 (cload, 16))
        self.wait("vector", cload)
        self.red("vector", wsc(0, nblk),
                 w().rearrange("p (s e) -> p s e", e=32), OP.max, absval=True)
        self.ts("vector", wsc(0, nblk), wsc(0, nblk), 1.0 / 7.0, None, OP.mult)
        self.recip(wrc(0, nblk), wsc(0, nblk), inc=cred)
        self.wait("gpsimd", cred)
        wscap, wrcap = wsc(), wrc()
        rexp = bass.AP(wrcap.tensor, wrcap.offset,
                       [list(wrcap.ap[0]), [1, nblk], [0, 32]])
        sexp = bass.AP(wscap.tensor, wscap.offset,
                       [list(wscap.ap[0]), [1, nblk], [0, 32]])
        if seq >= 2:
            self.wait("gpsimd", cg2, at=cg2.n)
        self.tt("gpsimd", wqf().rearrange("p (s e) -> p s e", e=32),
                w().rearrange("p (s e) -> p s e", e=32), rexp, OP.mult, inc=cg1)
        self.wait("scalar", cg1)
        if seq >= 2:
            self.wait("scalar", cg2, at=cg2.n)
        self.act(wqi(), wqf(), AF.Copy, scale=1.0, bias=8.5, inc=csc)
        self.wait("vector", csc)
        if seq >= 1:
            self.wait("vector", cst, at=cst.n)
        self.stt("vector", wqf(), wqi(), -8.5, wqf(), OP.add, OP.is_gt)
        self.stt("vector", wqf(), wqi(), -8.0, wqf(), OP.add, OP.subtract)
        self.tt("vector", wqb().rearrange("p (s e) -> p s e", e=32),
                wqf().rearrange("p (s e) -> p s e", e=32), sexp, OP.mult,
                inc=cg2)
        self.wait("sync", cg2)
        tok = self.dma("sync", self.dram[dst_name][:, piece_off:piece_off + NC],
                       wqb(), (cst, 16))
        if not hasattr(self, "dq_tok"):
            self.dq_tok = {}
        self.dq_tok[seq] = tok
        self.phase(_ph)

    def dequant_plan(self):
        c = self.cfg
        plan = []
        for ic in range(c.IC):
            plan.append(("wup_nPM", "wdqA_n", ic * c.D))
            plan.append(("wgate_nPM", "wdqB_n", ic * c.D))
        for j in range((c.CC * c.I_FFN) // 1024):
            plan.append(("wdown_nPM", "wdqC_n", j * 1024))
        return plan

    def _dq_interleave(self, k=3):
        for _ in range(k):
            if self.dqi < len(self.dqplan):
                s, d, o = self.dqplan[self.dqi]
                self.dequant_piece(s, d, o, self.dqi)
                self.dqi += 1

    def p1(self):
        self.phase("p1")
        c = self.cfg
        T, CC = c.T, c.CC
        self.AB.reset(); self.AF.reset(); self.AI.reset()
        self.xstr = self.AB.alloc(2 * CC * 512)
        self.vaug = self.AB.alloc((T // 128) * 1024)
        self.qk = self.AB.alloc(2 * T)
        self.wqk = self.AB.alloc(2 * 2 * CC * 128)
        self.strips = self.AB.alloc(c.NKT * c.SW)
        self.wv_sb = self.AB.alloc(CC * c.D)
        self.avstg = self.AB.alloc(2 * 512)
        self.wslot = self.AB.alloc(2 * CC * 128)
        self.cast2 = self.AB.alloc(2 * 512)
        self.avin = self.AB.alloc(CC * 512)
        self.stile = self.AF.alloc(CC * 512)
        self.rdenA = self.AF.alloc(512)
        self.rdenB = self.AF.alloc(512)
        self.xres = self.AF.alloc(2 * 512)
        self.f2 = self.AF.alloc(2 * 512)

        wq = self.dram["wqkvPM"][:, :].rearrange("p (cc e) -> p cc e", cc=CC)
        chwv = self.dch("gpsimd", "wvld")
        self.dma("gpsimd", self.wv_sb().rearrange("p (cc e) -> p cc e", cc=CC),
                 wq[:, :, 2 * c.D:3 * c.D], (chwv, 16))
        self.dqplan = self.dequant_plan()
        self.dqi = 0
        for b in range(c.B):
            self._p1_batch(b)
        self.finish_max([(0, self.pmax_col, 0)], "1")
        self.ts("vector", self.s11c(1), self.s11c(0), 1.0 / QMAX, None, OP.mult)
        self.recip(self.s11c(2), self.s11c(0))
        self.ts("vector", self.s11c(2), self.s11c(2), QMAX, None, OP.mult)
        self.ts("vector", self.s11c(3), self.s11c(1), -OFFS, None, OP.mult)
        cvr = self.chp("r1d")
        self.seal("vector", cvr)
        self.wait("tensor", cvr)
        self.wait("vector", cvr)
        self.bcast_to_pp(2, 0)
        self.bcast_to_pp(1, 1)
        self.bcast_to_pp(3, 2)

    def _p1_batch(self, b):
        self.phase("p1qk")
        c = self.cfg
        T, CC, D = c.T, c.CC, c.D
        xr = self.dram["xPM"][:, :].rearrange("p (cc r) -> p cc r", cc=CC)
        chbat = self.chp("p1bat")
        chxs = self.dch("gpsimd", "xs")
        vps = self.chp("vps")
        vaugc = self.chp("vaugc")
        if b > 0:
            self.wait("gpsimd", chbat, at=b)
        self.wait("tensor", self.dch("gpsimd", "wvld"))

        self._xqk_ld = 0
        chqkp = self.chp("qkp")
        for tt in range(T // 128):
            xsl = tt % 2
            xreg = self.xstr.sub(xsl * CC * 512, xsl * CC * 512 + CC * 128)
            if tt >= 2:
                self.wait("gpsimd", vps, at=max(0, vps.n - 2))
            elif b > 0:
                self.wait("gpsimd", chqkp, at=chqkp.n)
            xn = self.dma("gpsimd",
                          xreg().rearrange("p (cc t) -> p cc t", cc=CC),
                          xr[:, :, b * T + tt * 128: b * T + (tt + 1) * 128],
                          (chxs, 16))
            self.wtok("tensor", xn)
            for vc in range(2):
                boff = (((tt % 2) * 2) + vc) * 512
                if vaugc.n >= 4:
                    self.wait("tensor", vaugc, at=vaugc.n - 3)
                for cci in range(CC):
                    self.mm(self.psum[:, boff:boff + 512],
                            xreg(cci * 128, (cci + 1) * 128),
                            self.wv_sb(cci * D + vc * 512, cci * D + (vc + 1) * 512),
                            cci == 0, cci == CC - 1,
                            inc=(vps if cci == CC - 1 else None))
                self.wait("vector", vps)
                self.cp("vector",
                        self.vaug(tt * 1024 + vc * 512, tt * 1024 + (vc + 1) * 512),
                        self.psum[:, boff:boff + 512], inc=vaugc)

        chqkw = self.dch("gpsimd", "qkw")
        chqke = self.chp("qke")
        wqkv = self.dram["wqkvPM"][:, :].rearrange("p (cc e) -> p cc e", cc=CC)
        for hp in range(c.H // 2):
            wbuf = hp % 2
            wdst = self.wqk
            wof = wbuf * 2 * CC * 128
            if hp >= 2:
                self.wait("gpsimd", chqke, at=chqke.n - 2 * (T // 512))
            self.dma("gpsimd",
                     wdst(wof, wof + CC * 128).rearrange("p (cc e) -> p cc e", cc=CC),
                     wqkv[:, :, hp * 128:(hp + 1) * 128], (chqkw, 16))
            self.dma("gpsimd",
                     wdst(wof + CC * 128, wof + 2 * CC * 128).rearrange("p (cc e) -> p cc e", cc=CC),
                     wqkv[:, :, D + hp * 128: D + (hp + 1) * 128], (chqkw, 16))
            self.wait("tensor", chqkw)
            self.wait("tensor", self.chp("avm"), at=self.chp("avm").n)
            nqt = T // 512
            for t4 in range(nqt):
                nld = self._xqk_ld
                self._xqk_ld = nld + 1
                xsl = nld % 2
                xreg = self.xstr.sub(xsl * CC * 512, (xsl + 1) * CC * 512)
                if nld < 2:
                    self.wait("gpsimd", vps, at=vps.n)
                else:
                    self.wait("gpsimd", chqkp, at=max(0, chqkp.n - 2))
                xn = self.dma("gpsimd",
                              xreg().rearrange("p (cc t) -> p cc t", cc=CC),
                              xr[:, :, b * T + t4 * 512: b * T + (t4 + 1) * 512],
                              (chxs, 16))
                self.wtok("tensor", xn)
                for qki in range(2):
                    idx = t4 * 2 + qki
                    boff = (6 + idx % 2) * 512
                    if chqke.n >= 2:
                        self.wait("tensor", chqke, at=chqke.n - 1)
                    for cci in range(CC):
                        self.mm(self.psum[:, boff:boff + 512],
                                wdst(wof + qki * CC * 128 + cci * 128,
                                     wof + qki * CC * 128 + (cci + 1) * 128),
                                xreg(cci * 512, (cci + 1) * 512),
                                cci == 0, cci == CC - 1,
                                inc=(chqkp if cci == CC - 1 else None))
                    self.wait("scalar", chqkp)
                    self.act(self.qk(qki * T + t4 * 512, qki * T + (t4 + 1) * 512),
                             self.psum[:, boff:boff + 512], AF.Copy, inc=chqke)
            for hh in range(2):
                self._p1_attn_head(b, hp * 2 + hh)
        self._p1_post(b)
        self.seal("tensor", chbat)

    def _p1_attn_head(self, b, h):
        self.phase("p1at")
        c = self.cfg
        T = c.T
        p0 = (h % 2) * 64
        qT = self.qk(0, T, p0, p0 + 64)
        kT = self.qk(T, 2 * T, p0, p0 + 64)
        chsc = self.chp("sc")
        chmk = self.chp("mk")
        chex = self.chp("exp")
        chpp = self.chp("persp")
        chpe = self.chp("perse")
        chavp = self.chp("avp")
        chrd = self.chp("rd")
        chavm = self.chp("avm")
        chavt = self.chp("avt")
        headidx = b * c.H + h
        self.wait("tensor", self.chp("qke"))
        self.wait("tensor", self.chp("vaugc"))
        if headidx > 0:
            self.wait("scalar", chavt, at=headidx)
        identb = self.sb["identb"]
        maskb = self.sb["maskb"]
        for j in range(c.NKT):
            k0 = j * 128
            wj = min(c.SW, T - k0)
            off = (j % 2) * 1024
            if chex.n >= 2:
                self.wait("tensor", chex, at=chex.n - 1)
            w1 = min(wj, 512)
            self.mm(self.psum[:, off:off + w1], identb[:, :], maskb[:, 0:w1],
                    True, False)
            if wj > 512:
                self.mm(self.psum[:, off + 512:off + wj], identb[:, :],
                        maskb[:, 512:wj], True, False)
            self.mm(self.psum[:, off:off + w1], kT[:, k0:k0 + 128], qT[:, k0:k0 + w1],
                    False, True, inc=(chsc if wj <= 512 else None), tp=(p0, 0))
            if wj > 512:
                self.mm(self.psum[:, off + 512:off + wj], kT[:, k0:k0 + 128],
                        qT[:, k0 + 512:k0 + wj], False, True, inc=chsc, tp=(p0, 0))
            if j == 0:
                self.wait("vector", chsc)
                self.memset("vector", self.psum[0:c.NP, off:off + wj], NEG,
                            inc=chmk)
                self.wait("scalar", chmk)
            else:
                self.wait("scalar", chsc)
            self.act(self.strips(j * c.SW, j * c.SW + wj),
                     self.psum[:, off:off + wj], AF.Exp, scale=0.125, inc=chex)
        nq = T // 512
        for g in range(nq):
            boff = (4 + g % 2) * 512
            if g >= 2:
                self.wait("tensor", chpe, at=chpe.n - 1)
            elif chavm.n > 0:
                self.wait("tensor", chavm, at=chavm.n)
            self.mm(self.psum[0:c.NP, boff:boff + 512], kT[:, 0:c.NP],
                    qT[:, g * 512:(g + 1) * 512], True, True, inc=chpp, tp=(p0, 0))
            self.wait("scalar", chpp)
            self.act(self.sb["pers"][0:c.NP, g * 512:(g + 1) * 512],
                     self.psum[0:c.NP, boff:boff + 512], AF.Exp, scale=0.125, inc=chpe)
        a0 = p0
        d0 = 64 - a0
        for g in range(nq):
            q0 = g * 512
            boff = (4 + g % 4) * 512
            if g < 2:
                self.wait("tensor", chpe)
            if chavm.n >= 4:
                self.wait("tensor", chavm, at=chavm.n - 3)
            rhsp = self.sb["pers"][0:c.NP, q0:q0 + 512]
            self.mm(self.psum[a0:a0 + 64, boff:boff + 512],
                    self.vaug(h * 64, (h + 1) * 64, 0, c.NP), rhsp,
                    True, False, tp=(0, a0))
            self.mm(self.psum[d0:d0 + 64, boff:boff + 512],
                    self.sb["onesb64"][0:c.NP, :], rhsp, True, False, tp=(0, d0))
            lo_j = max(0, (q0 - c.WIN) // 128)
            hi_j = min(c.NKT, (q0 + 512) // 128)
            js = list(range(lo_j, hi_j))
            for idx, jj in enumerate(js):
                k0 = jj * 128
                wj = min(c.SW, T - k0)
                lo = max(k0, q0)
                hi = min(k0 + wj, q0 + 512)
                if lo >= hi:
                    continue
                last = idx == len(js) - 1
                rhs = self.strips(jj * c.SW + (lo - k0), jj * c.SW + (hi - k0))
                self.mm(self.psum[a0:a0 + 64, boff + (lo - q0): boff + (hi - q0)],
                        self.vaug(jj * 1024 + h * 64, jj * 1024 + (h + 1) * 64),
                        rhs, False, last, tp=(0, a0))
                self.mm(self.psum[d0:d0 + 64, boff + (lo - q0): boff + (hi - q0)],
                        self.sb["onesb64"][:, :], rhs, False, last, tp=(0, d0),
                        inc=(chavp if last else None))
            self.wait("vector", chavp)
            rr = self.rdenA if g % 2 == 0 else self.rdenB
            self.recip(rr(0, 512, d0, d0 + 64),
                       self.psum[d0:d0 + 64, boff:boff + 512], inc=chrd)
            self.wait("scalar", chrd)
            qn = self.dma("scalar", rr(0, 512, a0, a0 + 64), rr(0, 512, d0, d0 + 64))
            self.wtok("vector", qn)
            slot = g % 2
            stg = self.avstg(slot * 512, (slot + 1) * 512, a0, a0 + 64)
            key = "avslot%d" % slot
            prev = getattr(self, key, None)
            if prev is not None:
                self.wtok("vector", prev)
            self.tt("vector", stg, self.psum[a0:a0 + 64, boff:boff + 512],
                    rr(0, 512, a0, a0 + 64), OP.mult, inc=chavm)
            self.wait("scalar", chavm, at=chavm.n)
            qn2 = self.dma("scalar",
                           self.dram["avscr"][a0:a0 + 64,
                                              (h // 2) * c.R + b * T + q0:
                                              (h // 2) * c.R + b * T + q0 + 512],
                           stg)
            setattr(self, key, qn2)
        self.seal("tensor", chavt)

    def _p1_post(self, b):
        self.phase("p1po")
        c = self.cfg
        T, CC = c.T, c.CC
        chops = self.chp("p1c_mm")
        chsadd = self.chp("p1c_add")
        chcast = self.chp("p1c_cast")
        chstmm = self.chp("p1c_stmm")
        chbc = self.chp("p1c_bc")
        chx1 = self.chp("p1c_x1")
        avld = self.dch("sync", "p1c_avld")
        wout = self.dram["woutPM"][:, :].rearrange("p (cc e) -> p cc e", cc=CC)
        self.wait_queue("sync", "scalar")
        for t4 in range(T // 512):
            r0 = b * T + t4 * 512
            if t4 > 0:
                self.wait("sync", chops, at=chops.n)
            for cci in range(CC):
                self.dma("sync", self.avin(cci * 512, (cci + 1) * 512),
                         self.dram["avscr"][:, cci * c.R + r0: cci * c.R + r0 + 512],
                         (avld, 16))
            for oc in range(CC):
                so = (oc % 2) * CC * 128
                if chops.n >= 2:
                    self.wait("gpsimd", chops, at=chops.n - 1)
                wn = self.dma("gpsimd",
                              self.wslot(so, so + CC * 128).rearrange("p (cc e) -> p cc e", cc=CC),
                              wout[:, :, oc * 128:(oc + 1) * 128])
                boff = (oc % 4) * 512
                self.wtok("tensor", wn)
                if oc == 0:
                    self.wait("tensor", avld)
                if oc >= 4:
                    self.wait("tensor", chsadd, at=chsadd.n - 3)
                for cci in range(CC):
                    self.mm(self.psum[:, boff:boff + 512],
                            self.wslot(so + cci * 128, so + (cci + 1) * 128),
                            self.avin(cci * 512, (cci + 1) * 512),
                            cci == 0, cci == CC - 1,
                            inc=(chops if cci == CC - 1 else None))
                xslot = self.xres((oc % 2) * 512, ((oc % 2) + 1) * 512)
                if chsadd.n >= 2:
                    self.wait("sync", chsadd, at=chsadd.n - 1)
                xn = self.dma("sync", xslot,
                              self.dram["xPM"][:, oc * c.R + r0: oc * c.R + r0 + 512])
                self.wait("vector", chops)
                self.wtok("vector", xn)
                sC = self.stile(oc * 512, (oc + 1) * 512)
                self.tt("vector", sC, self.psum[:, boff:boff + 512], xslot,
                        OP.add, inc=chsadd)
                self.wait("scalar", chsadd)
                if oc >= 1:
                    self.wait("scalar", chstmm, at=chstmm.n)
                sbf = self.cast2(0, 512)
                s2bf = self.cast2(512, 1024)
                self.act(sbf, sC, AF.Copy, inc=chcast)
                self.act(s2bf, sC, AF.Square, inc=chcast)
                self.wait("tensor", chcast)
                self.mm(self.bank1(4), self.sb["onescolb"][:, :], sbf,
                        oc == 0, oc == CC - 1, inc=(chstmm, 1))
                self.mm(self.bank1(5), self.sb["onescolb"][:, :], s2bf,
                        oc == 0, oc == CC - 1, inc=(chstmm, 1))
            self.wait("vector", chstmm)
            rows = self.sb["rows"]
            mneg = rows[0:1, 0:512]
            rstd = rows[0:1, 512:1024]
            self.ln_rows(self.bank1(4), self.bank1(5), 512, mneg, rstd,
                         1.0 / c.D)
            self.mm(self.bank(6), self.sb["onesrow"][:, :], mneg, True, True)
            self.mm(self.bank(7), self.sb["onesrow"][:, :], rstd, True, True, inc=chbc)
            self.wait("vector", chbc)
            for oc in range(CC):
                sC = self.stile(oc * 512, (oc + 1) * 512)
                tslot = self.f2((oc % 2) * 512, ((oc % 2) + 1) * 512)
                key = "x1slot%d" % (oc % 2)
                prev = getattr(self, key, None)
                if prev is not None:
                    self.wtok("vector", prev)
                self.tt("vector", tslot, sC, self.bank(6), OP.add)
                self.tt("vector", tslot, tslot, self.bank(7), OP.mult)
                self.track_max(tslot)
                self.seal("vector", chx1)
                self.wait("sync", chx1, at=chx1.n)
                xn = self.dma("sync",
                              self.dram["resid1"][:, oc * c.R + r0: oc * c.R + r0 + 512],
                              tslot)
                setattr(self, key, xn)
            self.wait("tensor", chx1)



NT = 1024


class MKP2(MKP1):
    def p2(self):
        self.phase("xq")
        c = self.cfg
        CC, R, IC = c.CC, c.R, c.IC
        self.barrier("p2")
        self.AB.reset(); self.AF.reset(); self.AI.reset()
        xq = self.AB.alloc(CC * R)
        wtr = self.AB.alloc(4 * CC * 128)
        spst = self.AB.alloc(8 * 512)
        self.g_bf = self.AB.alloc(2 * NT)
        self.u_bf = self.AB.alloc(2 * NT)
        self.q_bf = self.AB.alloc(2 * NT)
        ftl = self.AF.alloc(4 * 512)
        self.sg_f = self.AF.alloc(2 * NT)
        self.p_f = self.AF.alloc(2 * NT)
        self.z_f = self.AF.alloc(2 * NT)
        self.gc_f = self.AF.alloc(2 * NT)
        self.c_f = self.AF.alloc(2 * NT)
        self.z4_f = ftl
        iA = self.AI.alloc(2048)
        iB = self.AI.alloc(2048)
        itl = iA.sub(0, 1024)
        self.dq_wnat = self.z_f
        self.dq_wqf = self.gc_f
        self.dq_wsc = self.c_f.sub(0, 128)
        self.dq_wqi = iB
        self.dq_wqb = self.q_bf.sub(0, 1024)

        self.wait("scalar", self.ch("ppv"))
        self.wait("vector", self.ch("ppv"))
        chld = self.dch("sync", "f1_xld")
        chm = self.chp("f1_m")
        chq = self.chp("f1_q")
        for cci in range(CC):
            for tq in range(R // 512):
                sl = (cci * (R // 512) + tq) % 2
                fo = ftl(sl * 512, (sl + 1) * 512)
                zo = ftl((2 + sl) * 512, (3 + sl) * 512)
                io = itl(sl * 512, (sl + 1) * 512)
                if chm.n >= 2:
                    self.wait("sync", chm, at=chm.n - 1)
                self.dma("sync", fo,
                         self.dram["resid1"][:, cci * R + tq * 512:
                                             cci * R + tq * 512 + 512], (chld, 16))
                self.wait("scalar", chld)
                if chq.n >= 2:
                    self.wait("scalar", chq, at=chq.n - 1)
                self.act(zo, fo, AF.Copy, scale=self.pp_col(0), bias=32768.5)
                self.act(io, fo, AF.Copy, scale=self.pp_col(0), bias=32768.5)
                self.seal("scalar", chm)
                self.wait("vector", chm)
                self.tt("vector", zo, io, zo, OP.is_gt)
                self.stt("vector", zo, io, -OFFS, zo, OP.add, OP.subtract)
                self.ts("vector", xq(cci * R + tq * 512, cci * R + tq * 512 + 512),
                        zo, self.pp_col(1), None, OP.mult, inc=chq)

        for _ in range(4):
            sq, dn, of = self.dqplan[self.dqi]
            self.dequant_piece(sq, dn, of, self.dqi)
            self.dqi += 1

        self.phase("f1")
        self.wait("tensor", chq)
        chw = self.dch("scalar", "f1_w")
        chmmm = self.chp("f1_mm")
        chsp = self.chp("f1_sp")
        chmx = self.chp("f1_mx")
        cols = {"A": [], "B": []}
        spslots = {}
        for ec in range(2 * IC):
            which = "A" if ec % 2 == 0 else "B"
            ocl = ec // 2
            wsrc = self.dram["wdq%s_n" % which]
            sl = ec % 4
            wofs = sl * CC * 128
            if ec >= 4:
                self.wait("scalar", chmmm, at=chmmm.n - 3 * (R // 512))
            self.wtok("scalar", self.dq_tok[ec])
            wtoks = []
            for cci in range(CC):
                wtoks.append(self.dma(
                    "scalar", wtr(wofs + cci * 128, wofs + (cci + 1) * 128),
                    wsrc[:, ocl * c.D + cci * 128: ocl * c.D + (cci + 1) * 128],
                    transpose=True))
            for tk in wtoks:
                self.wtok("tensor", tk)
            for tq in range(R // 512):
                gidx = ec * (R // 512) + tq
                boff = (gidx % 4) * 512
                if chmx.n >= 4:
                    self.wait("tensor", chmx, at=chmx.n - 3)
                for cci in range(CC):
                    self.mm(self.psum[:, boff:boff + 512],
                            wtr(wofs + cci * 128, wofs + (cci + 1) * 128),
                            xq(cci * R + tq * 512, cci * R + (tq + 1) * 512),
                            cci == 0, cci == CC - 1,
                            inc=(chmmm if cci == CC - 1 else None))
                self.wait("scalar", chmmm)
                ssl = gidx % 8
                stg = spst(ssl * 512, (ssl + 1) * 512)
                prev = spslots.get(ssl)
                if prev is not None:
                    self.wtok("scalar", prev)
                self.act(stg, self.psum[:, boff:boff + 512], AF.Copy, inc=chsp)
                self.wait("vector", chsp)
                cols[which].append(self.track_max(self.psum[:, boff:boff + 512]))
                self.seal("vector", chmx)
                self.wait("gpsimd", chsp, at=chsp.n)
                spslots[ssl] = self.dma(
                    "gpsimd",
                    self.dram["spill%s" % which][:, ocl * R + tq * 512:
                                                 ocl * R + tq * 512 + 512], stg)
            while self.dqi < min(len(self.dqplan), (ec + 1) * 3 // 2 + 4):
                sq, dn, of = self.dqplan[self.dqi]
                self.dequant_piece(sq, dn, of, self.dqi)
                self.dqi += 1
        ua, ub = cols["A"][0], cols["A"][-1] + 1
        ga, gb = cols["B"][0], cols["B"][-1] + 1
        self.finish_max([(ua, ub, 4), (ga, gb, 5)], "2")
        self.ts("vector", self.s11c(6), self.s11c(4), 1.0 / QMAX, None, OP.mult)
        self.ts("vector", self.s11c(7), self.s11c(5), 1.0 / QMAX, None, OP.mult)
        self.recip(self.s11c(8), self.s11c(4))
        self.recip(self.s11c(9), self.s11c(5))
        self.ts("vector", self.s11c(8), self.s11c(8), QMAX, None, OP.mult)
        self.ts("vector", self.s11c(9), self.s11c(9), QMAX, None, OP.mult)
        self.ts("vector", self.s11c(10), self.s11c(7), -OFFS, None, OP.mult)
        cvr = self.chp("r2d")
        self.seal("vector", cvr)
        self.wait("tensor", cvr)
        self.bcast_to_pp(8, 3)
        self.bcast_to_pp(9, 4)
        self.bcast_to_pp(7, 5)
        self.bcast_to_pp(10, 6)

        self.phase("f2")
        self.mg_i = iA
        self.S_i = iB
        self.wait("scalar", chq)
        chgl = self.dch("sync", "f2_gl")
        chf2s = self.chp("f2_s")
        chf2sB = self.chp("f2_sB")
        chf2v = self.chp("f2_v")
        chf2vB = self.chp("f2_vB")
        g2lo = self.pmax_col
        self.wait_queue("sync", "gpsimd")
        self.wait("scalar", self.ch("ppv"))
        self.wait("vector", self.ch("ppv"))
        chf2vA = self.chp("f2_vA")
        chf2gA = self.chp("f2_gA")
        chf2g = self.chp("f2_g")
        ntile = (IC * R) // NT
        for i in range(ntile):
            sl = i % 2
            gb = self.g_bf(sl * NT, (sl + 1) * NT)
            mg = self.mg_i(sl * NT, (sl + 1) * NT)
            Si = self.S_i(sl * NT, (sl + 1) * NT)
            sg = self.sg_f(sl * NT, (sl + 1) * NT)
            pf = self.p_f(sl * NT, (sl + 1) * NT)
            zf = self.z_f(sl * NT, (sl + 1) * NT)
            gf = self.gc_f(sl * NT, (sl + 1) * NT)
            cf = self.c_f(sl * NT, (sl + 1) * NT)
            if i >= 2:
                self.wait("sync", chf2s, at=chf2s.n - 1)
            self.dma("sync", gb, self.dram["spillB"][:, i * NT:(i + 1) * NT],
                     (chgl, 16))
            self.wait("scalar", chgl)
            if i >= 2:
                self.wait("scalar", chf2v, at=chf2v.n - 1)
            self.act(zf, gb, AF.Copy, scale=self.pp_col(4), bias=32768.5)
            self.act(mg, gb, AF.Copy, scale=self.pp_col(4), bias=32768.5)
            self.seal("scalar", chf2s)
            self.wait("vector", chf2s)
            self.tt("vector", gf, mg, zf, OP.is_gt)
            self.stt("vector", cf, mg, -OFFS, gf, OP.add, OP.subtract,
                     inc=chf2vA)
            self.wait("scalar", chf2vA)
            self.act(sg, cf, AF.Sigmoid, scale=self.pp_col(5), bias=0.0)
            self.act(zf, sg, AF.Copy, scale=65536.0, bias=0.5)
            self.act(Si, sg, AF.Copy, scale=65536.0, bias=0.5)
            self.seal("scalar", chf2sB)
            self.wait("vector", chf2sB)
            self.tt("vector", gf, Si, zf, OP.is_gt)
            self.stt("vector", gf, Si, -0.0, gf, OP.add, OP.subtract)
            self.ts("vector", gf, gf, 65535.0, None, OP.min)
            prevsp = getattr(self, "_pfsp%d" % sl, None)
            if prevsp is not None:
                self.wtok("vector", prevsp)
            self.tt("vector", pf, cf, gf, OP.mult)
            self.track_max(pf, inc=chf2v)
            self.wait("gpsimd", chf2v, at=chf2v.n)
            setattr(self, "_pfsp%d" % sl,
                    self.dma("gpsimd", self.dram["spillD"][:, i * NT:(i + 1) * NT],
                             pf))
        self.finish_max([(g2lo, self.pmax_col, 11)], "3")
        self.tt("vector", self.s11c(12), self.s11c(11), self.s11c(7), OP.mult)
        self.ts("vector", self.s11c(12), self.s11c(12), 1.0 / (65536.0 * QMAX), None,
                OP.mult)
        self.recip(self.s11c(14), self.s11c(12))
        self.tt("vector", self.s11c(13), self.s11c(12), self.s11c(6), OP.mult)
        self.tt("vector", self.s11c(15), self.s11c(7), self.s11c(14), OP.mult)
        self.ts("vector", self.s11c(15), self.s11c(15), 1.0 / 65536.0, None, OP.mult)
        cvr = self.chp("r3d")
        self.seal("vector", cvr)
        self.wait("tensor", cvr)
        self.bcast_to_pp(15, 7)

        self.phase("f3")
        chul = self.dch("sync", "f3_ul")
        self.wait("scalar", self.ch("ppv"))
        self.wait("vector", self.ch("ppv"))
        self.wait("sync", self.chp("f2_s"))
        self.wait("sync", self.chp("f2_v"))
        chf3s = self.chp("f3_s")
        chf3vA = self.chp("f3_vA")
        chf3g = self.chp("f3_g")
        chf3v = self.chp("f3_v")
        chpl = self.dch("sync", "f3_pl")
        qlo = self.pmax_col
        self.wait_queue("sync", "gpsimd")
        self.wait("vector", self.dch("sync", "dq_st"))
        for i in range(ntile):
            sl = i % 2
            ub = self.u_bf(sl * NT, (sl + 1) * NT)
            qb = self.q_bf(sl * NT, (sl + 1) * NT)
            mg = self.mg_i(sl * NT, (sl + 1) * NT)
            Si = self.S_i(sl * NT, (sl + 1) * NT)
            pf = self.p_f(sl * NT, (sl + 1) * NT)
            zf = self.z_f(sl * NT, (sl + 1) * NT)
            gf = self.gc_f(sl * NT, (sl + 1) * NT)
            z4 = self.z4_f(sl * NT, (sl + 1) * NT)
            if i >= 2:
                self.wait("sync", chf3s, at=chf3s.n - 1)
            self.dma("sync", pf, self.dram["spillD"][:, i * NT:(i + 1) * NT],
                     (chpl, 16))
            self.dma("sync", ub, self.dram["spillA"][:, i * NT:(i + 1) * NT],
                     (chul, 16))
            self.wait("scalar", chpl)
            self.wait("scalar", chul)
            if i >= 2:
                self.wait("scalar", chf3v, at=chf3v.n - 1)
            self.act(zf, pf, AF.Copy, scale=self.pp_col(7), bias=32768.5)
            self.act(mg, pf, AF.Copy, scale=self.pp_col(7), bias=32768.5)
            self.act(z4, ub, AF.Copy, scale=self.pp_col(3), bias=32768.5)
            self.act(Si, ub, AF.Copy, scale=self.pp_col(3), bias=32768.5)
            self.seal("scalar", chf3s)
            self.wait("vector", chf3s)
            self.tt("vector", gf, mg, zf, OP.is_gt)
            self.stt("vector", zf, mg, -OFFS, gf, OP.add, OP.subtract)
            self.tt("vector", gf, Si, z4, OP.is_gt)
            self.stt("vector", z4, Si, -OFFS, gf, OP.add, OP.subtract)
            qprev = getattr(self, "_qbslot%d" % sl, None)
            if qprev is not None:
                self.wtok("vector", qprev)
            self.tt("vector", qb, zf, z4, OP.mult)
            self.track_max(qb, inc=chf3v)
            self.wait("gpsimd", chf3v, at=chf3v.n)
            setattr(self, "_qbslot%d" % sl,
                    self.dma("gpsimd", self.dram["spillC"][:, i * NT:(i + 1) * NT],
                             qb))
        self.finish_max([(qlo, self.pmax_col, 16)], "4")
        self.tt("vector", self.s11c(17), self.s11c(16), self.s11c(13), OP.mult)
        self.ts("vector", self.s11c(17), self.s11c(17), 1.0 / QMAX, None, OP.mult)
        self.recip(self.s11c(18), self.s11c(17))
        self.tt("vector", self.s11c(18), self.s11c(18), self.s11c(13), OP.mult)
        self.ts("vector", self.s11c(19), self.s11c(17), -OFFS, None, OP.mult)
        cvr = self.chp("r4d")
        self.seal("vector", cvr)
        self.wait("tensor", cvr)
        self.bcast_to_pp(18, 8)
        self.bcast_to_pp(17, 9)
        self.bcast_to_pp(19, 10)
        self._f4()

    def _f4(self):
        self.phase("f4")
        c = self.cfg
        CC, R, IC = c.CC, c.R, c.IC
        self.barrier("f4")
        self.AB.reset(); self.AF.reset(); self.AI.reset()
        self.f4_wdn = self.AB.alloc(IC * CC * 128)
        self.f4_obf = self.AB.alloc(2 * IC * 512)
        self.f4_cst = self.AB.alloc(CC * 1024)
        self.f4_qld = self.AB.alloc(2 * 512)
        self.f4_stile = self.AF.alloc(CC * 512)
        self.f4_ftl = self.AF.alloc(4 * 512)
        self.f4_zf4 = self.AF.alloc(2 * 512)
        self.f4_itl = self.AI.alloc(2 * 512)
        self.f4_rec = {}
        wdn = self.f4_wdn
        for oc in range(CC):
            for icc in range(IC):
                self.dma("scalar",
                         wdn(oc * IC * 128 + icc * 128, oc * IC * 128 + (icc + 1) * 128),
                         self.dram["wdqC_n"][:, oc * c.I_FFN + icc * 128:
                                             oc * c.I_FFN + (icc + 1) * 128],
                         transpose=True)
        self.wait("scalar", self.ch("ppv"))
        self.wait("vector", self.ch("ppv"))
        self.wait_queue("tensor", "scalar")
        self._f4_produce(0)
        for tq in range(R // 512):
            if tq + 1 < R // 512:
                self._f4_produce(tq + 1)
            self._f4_mm(tq)

    def _f4_produce(self, tq):
        c = self.cfg
        CC, R, IC = c.CC, c.R, c.IC
        rec = self.f4_rec
        r0 = tq * 512
        sl4 = tq % 2
        obf = self.f4_obf.sub(sl4 * IC * 512, (sl4 + 1) * IC * 512)
        chol = self.dch("sync", "f4_old")
        chom = self.chp("f4_om")
        choq = self.chp("f4_oq")
        if ("mm", tq - 2) in rec:
            self.wait("vector", self.chp("f4_mm"), at=rec[("mm", tq - 2)])
        for icc in range(IC):
            sl = icc % 2
            qsl = self.f4_qld(sl * 512, (sl + 1) * 512)
            io = self.f4_itl(sl * 512, (sl + 1) * 512)
            zo = self.f4_zf4(sl * 512, (sl + 1) * 512)
            if chom.n >= 2:
                self.wait("sync", chom, at=chom.n - 1)
            self.dma("sync", qsl,
                     self.dram["spillC"][:, icc * R + r0: icc * R + r0 + 512],
                     (chol, 16))
            self.wait("scalar", chol)
            if choq.n >= 2:
                self.wait("scalar", choq, at=choq.n - 1)
            self.act(zo, qsl, AF.Copy, scale=self.pp_col(8), bias=32768.5)
            self.act(io, qsl, AF.Copy, scale=self.pp_col(8), bias=32768.5)
            self.seal("scalar", chom)
            self.wait("vector", chom)
            self.tt("vector", zo, io, zo, OP.is_gt)
            self.stt("vector", zo, io, -OFFS, zo, OP.add, OP.subtract)
            self.ts("vector", obf(icc * 512, (icc + 1) * 512), zo,
                    self.pp_col(9), None, OP.mult, inc=choq)
        rec[("oq", tq)] = choq.n

    def _f4_mm(self, tq):
        c = self.cfg
        CC, R, IC = c.CC, c.R, c.IC
        rec = self.f4_rec
        r0 = tq * 512
        sl4 = tq % 2
        obf = self.f4_obf.sub(sl4 * IC * 512, (sl4 + 1) * IC * 512)
        wdn = self.f4_wdn
        cst = self.f4_cst
        stile = self.f4_stile
        ftl = self.f4_ftl
        chmm = self.chp("f4_mm")
        chadd = self.chp("f4_add")
        chcast = self.chp("f4_cast")
        chstmm = self.chp("f4_stmm")
        chbc = self.chp("f4_bc")
        chx2 = self.chp("f4_x2")
        chrow4 = self.chp("f4_row")
        chxl = self.dch("sync", "f4_xl")
        if not hasattr(self, "_f4x2slots"):
            self._f4x2slots = {}
        x2slots = self._f4x2slots
        self.wait("tensor", self.chp("f4_oq"), at=rec[("oq", tq)])
        if ("smm", tq - 1) in rec:
            self.wait("scalar", chstmm, at=rec[("smm", tq - 1)])
        for oc in range(CC):
            boff = (oc % 4) * 512
            if oc >= 4 or tq > 0:
                self.wait("tensor", chadd, at=max(0, chadd.n - 3))
            for icc in range(IC):
                self.mm(self.psum[:, boff:boff + 512],
                        wdn(oc * IC * 128 + icc * 128,
                            oc * IC * 128 + (icc + 1) * 128),
                        obf(icc * 512, (icc + 1) * 512),
                        icc == 0, icc == IC - 1,
                        inc=(chmm if icc == IC - 1 else None))
            xslot = ftl((2 + oc % 2) * 512, (3 + oc % 2) * 512)
            if chadd.n >= 2:
                self.wait("sync", chadd, at=chadd.n - 1)
            xn = self.dma("sync", xslot,
                          self.dram["resid1"][:, oc * R + r0: oc * R + r0 + 512],
                          (chxl, 16))
            self.wait("vector", chmm)
            self.wtok("vector", xn)
            sC = stile(oc * 512, (oc + 1) * 512)
            self.tt("vector", sC, self.psum[:, boff:boff + 512], xslot, OP.add,
                    inc=chadd)
            self.wait("scalar", chadd)
            cpr = oc * 1024
            self.act(cst(cpr, cpr + 512), sC, AF.Copy, inc=chcast)
            self.act(cst(cpr + 512, cpr + 1024), sC, AF.Square, inc=chcast)
        rec[("mm", tq)] = chmm.n
        self.wait("tensor", chcast, at=chcast.n)
        if ("row", tq - 1) in rec:
            self.wait("tensor", chrow4, at=rec[("row", tq - 1)])
        for oc in range(CC):
            cpr = oc * 1024
            self.mm(self.bank1(4), self.sb["onescolb"][:, :], cst(cpr, cpr + 512),
                    oc == 0, oc == CC - 1, inc=(chstmm, 1))
            self.mm(self.bank1(5), self.sb["onescolb"][:, :],
                    cst(cpr + 512, cpr + 1024), oc == 0, oc == CC - 1,
                    inc=(chstmm, 1))
        rec[("smm", tq)] = chstmm.n
        self.wait("vector", chstmm)
        if ("bc", tq - 1) in rec:
            self.wait("vector", chbc, at=rec[("bc", tq - 1)])
        rows = self.sb["rows"]
        mneg = rows[0:1, 0:512]
        rstd = rows[0:1, 512:1024]
        self.ln_rows(self.bank1(4), self.bank1(5), 512, mneg, rstd,
                     1.0 / c.D)
        self.seal("vector", chrow4)
        rec[("row", tq)] = chrow4.n
        if ("x2", tq - 1) in rec:
            self.wait("tensor", chx2, at=rec[("x2", tq - 1)])
        self.wait("tensor", chrow4, at=chrow4.n)
        self.mm(self.bank(6), self.sb["onesrow"][:, :], mneg, True, True)
        self.mm(self.bank(7), self.sb["onesrow"][:, :], rstd, True, True, inc=chbc)
        rec[("bc", tq)] = chbc.n
        self.wait("vector", chbc)
        for oc in range(CC):
            sC = stile(oc * 512, (oc + 1) * 512)
            tslot = ftl((oc % 2) * 512, ((oc % 2) + 1) * 512)
            prev = x2slots.get(oc % 2)
            if prev is not None:
                self.wtok("vector", prev)
            self.tt("vector", tslot, sC, self.bank(6), OP.add)
            self.tt("vector", tslot, tslot, self.bank(7), OP.mult, inc=chx2)
            self.wait("sync", chx2, at=chx2.n)
            x2slots[oc % 2] = self.dma(
                "sync", self.dram["resid2"][:, oc * R + r0: oc * R + r0 + 512],
                tslot)
        rec[("x2", tq)] = chx2.n


class MKFull(MKP2):
    def p3(self):
        c = self.cfg
        self.barrier("p3")
        for b in range(c.B):
            self._p3_abc(b)
        self._p3_d()

    def _p3_abc(self, b):
        self.phase("p3a")
        c = self.cfg
        CC, R, T, D = c.CC, c.R, c.T, c.D
        t0 = b * T
        self.AB.reset(); self.AF.reset(); self.AI.reset()
        sr_bf = self.AB.alloc(CC * T)
        rwkv = self.AB.alloc(CC * T)
        mixk = self.AB.alloc(CC * 512)
        mixv = self.AB.alloc(CC * 512)
        mixr = self.AB.alloc(CC * 512)
        wsl = self.AB.alloc(2 * CC * 128)
        p3f = self.AF.alloc(12288)
        ftl = self.AF.alloc(1536)
        self.sr_bf, self.rwkv = sr_bf, rwkv

        if b > 0:
            self.wait_queue("sync", "sync")
        chxl = self.dch("sync", "p3_xl")
        chcast = self.chp("p3_cast")
        chstmm = self.chp("p3_stmm")
        chrow = self.chp("p3_row")
        for tq in range(T // 512):
            r0 = t0 + tq * 512
            if chstmm.n > 0:
                self.wait("scalar", chstmm, at=chstmm.n)
            for cci in range(CC):
                sl = cci % 2
                fo = p3f(sl * 512, (sl + 1) * 512)
                if chcast.n >= 4:
                    self.wait("sync", chcast, at=chcast.n - 2)
                self.dma("sync", fo,
                         self.dram["resid2"][:, cci * R + r0: cci * R + r0 + 512],
                         (chxl, 16))
                self.wait("scalar", chxl)
                self.act(mixk(cci * 512, (cci + 1) * 512), fo, AF.Copy, inc=chcast)
                self.act(mixv(cci * 512, (cci + 1) * 512), fo, AF.Square,
                         inc=chcast)
            self.wait("tensor", chcast, at=chcast.n)
            if chrow.n > 0:
                self.wait("tensor", chrow, at=chrow.n)
            for cci in range(CC):
                self.mm(self.bank1(4), self.sb["onescolb"][:, :],
                        mixk(cci * 512, (cci + 1) * 512),
                        cci == 0, cci == CC - 1, inc=(chstmm, 1))
                self.mm(self.bank1(5), self.sb["onescolb"][:, :],
                        mixv(cci * 512, (cci + 1) * 512),
                        cci == 0, cci == CC - 1, inc=(chstmm, 1))
            self.wait("vector", chstmm)
            rows = self.sb["rows"]
            mneg = rows[0:1, 0:512]
            rstd = rows[0:1, 512:1024]
            chsr1 = self.dch("sync", "p3_strow")
            if chsr1.n > 0:
                self.wait("vector", chsr1)
                self.wait("scalar", chsr1)
            self.ln_rows(self.bank1(4), self.bank1(5), 512, mneg, rstd,
                         1.0 / c.D)
            self.seal("vector", chrow)
            self.wait("sync", chrow, at=chrow.n)
            self.dma("sync", self.dram["stat1"][0:1, r0:r0 + 512], mneg, (chsr1, 16))
            self.dma("sync", self.dram["stat1"][1:2, r0:r0 + 512], rstd, (chsr1, 16))

        self.wait_queue("sync", "sync")
        chsr1b = self.dch("sync", "p3_strow")
        if chsr1b.n > 0:
            self.wait("sync", chsr1b)
        chst1 = self.dch("sync", "p3_st1")
        chbc = self.chp("p3_bc")
        chmix = self.chp("p3_mix")
        chmixg = self.chp("p3_mixg")
        chwld = self.chp("p3_wld")
        chpmm = self.chp("p3_pmm")
        chev = self.chp("p3_ev")
        chsig = self.chp("p3_sig")
        kvst = self.chp("p3_kvst")
        self.memset("vector", self.sb["tails"][:, :], 0.0)
        for tq in range(T // 512):
            r0 = t0 + tq * 512
            self.wait("sync", chbc, at=chbc.n)
            rows = self.sb["rows"]
            self.dma("sync", rows[0:1, 0:512], self.dram["stat1"][0:1, r0:r0 + 512],
                     (chst1, 16))
            self.dma("sync", rows[0:1, 512:1024], self.dram["stat1"][1:2, r0:r0 + 512],
                     (chst1, 16))
            self.wait("tensor", chst1)
            self.wait("tensor", chmix)
            self.wait("tensor", chmixg)
            self.mm(self.bank(6), self.sb["onesrow"][:, :], rows[0:1, 0:512],
                    True, True)
            self.mm(self.bank(7), self.sb["onesrow"][:, :], rows[0:1, 512:1024],
                    True, True, inc=chbc)
            self.wait("vector", chbc)
            self.wait("gpsimd", chbc)
            for cci in range(CC):
                sl = cci % 2
                fo = p3f.sub(sl * 512, (sl + 1) * 512)
                h = p3f.sub(1024 + sl * 512, 1024 + (sl + 1) * 512)
                hh = p3f.sub(2048 + sl * 512, 2048 + (sl + 1) * 512)
                dd = p3f.sub(3072 + sl * 512, 3072 + (sl + 1) * 512)
                if chmix.n >= 6:
                    self.wait("sync", chmix, at=chmix.n)
                    self.wait("sync", chmixg, at=chmixg.n)
                self.dma("sync", fo(),
                         self.dram["resid2"][:, cci * R + r0: cci * R + r0 + 512],
                         (chxl, 16))
                self.wait("vector", chxl)
                if chmix.n >= 6:
                    self.wait("vector", chmix, at=chmix.n)
                    self.wait("vector", chmixg, at=chmixg.n)
                self.tt("vector", h(), fo(), self.bank(6), OP.add)
                self.tt("vector", h(), h(), self.bank(7), OP.mult)
                self.ts("vector", h(), h(), self.pcol("ln1g", cci),
                        self.pcol("ln1b", cci), OP.mult, OP.add)
                self.cp("vector", hh(0, 1), self.sb["tails"][:, cci:cci + 1])
                self.cp("vector", hh(1, 512), h(0, 511))
                self.cp("vector", self.sb["tails"][:, cci:cci + 1], h(511, 512))
                self.tt("vector", dd(), h(), hh(), OP.subtract)
                chhd = self.chp("p3_hd")
                self.seal("vector", chhd)
                self.wait("gpsimd", chhd)
                self.stt("vector", mixk(cci * 512, (cci + 1) * 512), dd(),
                         self.pcol("tmk", cci), hh(), OP.mult, OP.add, inc=chmix)
                self.stt("vector", mixv(cci * 512, (cci + 1) * 512), dd(),
                         self.pcol("tmv", cci), hh(), OP.mult, OP.add, inc=chmix)
                self.stt("vector", mixr(cci * 512, (cci + 1) * 512), dd(),
                         self.pcol("tmr", cci), hh(), OP.mult, OP.add, inc=chmix)
                self.seal("vector", chmix)
                self.seal("gpsimd", chmixg)
            self.wait("tensor", chmix)
            self.wait("tensor", chmixg)
            tile_i = 0
            for pi, (wname, mix) in enumerate((("wkPM", mixk), ("wvPM", mixv),
                                               ("wrPM", mixr))):
                wsrc = self.dram[wname][:, :].rearrange("p (cc e) -> p cc e", cc=CC)
                for oc in range(CC):
                    sl = tile_i % 2
                    so = sl * CC * 128
                    if chpmm.n >= 2:
                        self.wait("gpsimd", chpmm, at=chpmm.n - 1)
                    wn = self.dma("gpsimd",
                                  wsl(so, so + CC * 128).rearrange("p (cc e) -> p cc e", cc=CC),
                                  wsrc[:, :, oc * 128:(oc + 1) * 128])
                    boff = (tile_i % 4) * 512
                    self.wtok("tensor", wn)
                    btok = getattr(self, "_p3bank", {}).get(boff)
                    if btok is not None:
                        self.wtok("tensor", btok)
                    for cci in range(CC):
                        self.mm(self.psum[:, boff:boff + 512],
                                wsl(so + cci * 128, so + (cci + 1) * 128),
                                mix(cci * 512, (cci + 1) * 512),
                                cci == 0, cci == CC - 1,
                                inc=(chpmm if cci == CC - 1 else None))
                    if not hasattr(self, "_p3bank"):
                        self._p3bank = {}
                    if pi < 2:
                        self.wait("vector", chpmm)
                        ksl = p3f(4096 + (tile_i % 2) * 512,
                                  4096 + (tile_i % 2 + 1) * 512)
                        prev = getattr(self, "p3kv%d" % (tile_i % 2), None)
                        if prev is not None:
                            self.wtok("vector", prev)
                        self.cp("vector", ksl, self.psum[:, boff:boff + 512],
                                inc=chev)
                        self._p3bank[boff] = (chev, chev.n)
                        self.wait("scalar", chev, at=chev.n)
                        dst = "kscr" if pi == 0 else "vscr"
                        qn = self.dma("scalar",
                                      self.dram[dst][:, oc * R + r0: oc * R + r0 + 512],
                                      ksl)
                        setattr(self, "p3kv%d" % (tile_i % 2), qn)
                    else:
                        self.wait("scalar", chpmm)
                        self.act(sr_bf(oc * T + tq * 512, oc * T + (tq + 1) * 512),
                                 self.psum[:, boff:boff + 512], AF.Sigmoid,
                                 inc=chsig)
                        self._p3bank[boff] = (chsig, chsig.n)
                    tile_i += 1


        self.phase("p3b")
        kv_done = self.chp("p3_kvdone")
        self.wait_queue("sync", "scalar")
        self.wait("sync", self.chp("p3_mix"))
        self.wait("sync", self.chp("p3_mixg"))
        self.wait("sync", self.chp("p3_pmm"))
        self.wait("vector", self.chp("p3c_mm"))
        chkl = self.dch("sync", "p3_kl")
        chek = self.chp("p3_ek")
        chsc2 = self.chp("p3_scan")
        chy = self.chp("p3_y")
        W = T
        for cci in range(CC):
            kc = p3f.sub(0, W)
            vc = p3f.sub(W, 2 * W)
            ek = p3f.sub(2 * W, 3 * W)
            eu = p3f.sub(3 * W, 4 * W)
            ekv = p3f.sub(4 * W, 5 * W)
            euv = p3f.sub(5 * W, 6 * W)
            if cci > 0:
                self.wait("sync", chy, at=chy.n)
                self.wait("sync", chek, at=chek.n)
            self.dma("sync", kc(), self.dram["kscr"][:, cci * R + t0: cci * R + t0 + W],
                     (chkl, 16))
            self.dma("sync", vc(), self.dram["vscr"][:, cci * R + t0: cci * R + t0 + W],
                     (chkl, 16))
            self.wait("scalar", chkl)
            self.act(ek(), kc(), AF.Exp)
            self.act(eu(), kc(), AF.Exp, bias=self.pcol("tf", cci))
            self.seal("scalar", chek)
            self.wait("vector", chek)
            self.tt("vector", ekv(), ek(), vc(), OP.mult)
            self.tt("vector", euv(), eu(), vc(), OP.mult)
            dc = self.sb["dcol"][:, cci:cci + 1]
            dbc = bass.AP(dc.tensor, dc.offset, [list(dc.ap[0]), [0, W]])
            self.scan(kc(), dbc, ekv(), 0.0)
            self.scan(vc(), dbc, ek(), 0.0)
            self.cp("vector", ekv(0, 1), euv(0, 1))
            self.tt("vector", ekv(1, W), kc(0, W - 1), euv(1, W), OP.add)
            self.cp("vector", euv(0, 1), eu(0, 1))
            self.tt("vector", euv(1, W), vc(0, W - 1), eu(1, W), OP.add)
            chyA = self.chp("p3_yA")
            chrec = self.chp("p3_rec")
            self.seal("vector", chyA)
            self.wait("scalar", chyA)
            self.act(eu(), euv(), AF.Ln)
            self.act(eu(), eu(), AF.Exp, scale=-1.0, inc=chrec)
            self.wait("vector", chrec)
            self.tt("vector", ek(), ekv(), eu(), OP.mult)
            self.tt("vector", rwkv(cci * T, (cci + 1) * T), ek(),
                    sr_bf(cci * T, (cci + 1) * T), OP.mult)
            self.seal("vector", chy)

        self.phase("p3c")
        chwo = self.chp("p3c_w")
        chomm = self.chp("p3c_mm")
        self.wait("gpsimd", self.chp("p3_pmm"))
        self.wait("tensor", self.chp("p3_ev"))
        self.wait("tensor", self.chp("p3_sig"))
        self.wait("tensor", self.chp("p3_y"))
        chxl2 = self.dch("sync", "p3c_xl")
        self.wait("vector", self.chp("p3_ek"), at=self.chp("p3_ek").n)
        self.wait("vector", self.chp("p3_y"), at=self.chp("p3_y").n)
        chadd = self.chp("p3c_add")
        chcst = self.chp("p3c_cst")
        chsmm = self.chp("p3c_smm")
        chrow2 = self.chp("p3c_row")
        wo = self.dram["woPM"][:, :].rearrange("p (cc e) -> p cc e", cc=CC)
        xpslots = {}
        for tq in range(T // 512):
            r0 = t0 + tq * 512
            for oc in range(CC):
                sl = oc % 2
                so = sl * CC * 128
                if chomm.n >= 2:
                    self.wait("gpsimd", chomm, at=chomm.n - 1)
                wn = self.dma("gpsimd",
                              wsl(so, so + CC * 128).rearrange("p (cc e) -> p cc e", cc=CC),
                              wo[:, :, oc * 128:(oc + 1) * 128])
                boff = (oc % 4) * 512
                self.wtok("tensor", wn)
                if oc >= 4:
                    self.wait("tensor", chadd, at=chadd.n - 3)
                for cci in range(CC):
                    self.mm(self.psum[:, boff:boff + 512],
                            wsl(so + cci * 128, so + (cci + 1) * 128),
                            rwkv(cci * T + tq * 512, cci * T + (tq + 1) * 512),
                            cci == 0, cci == CC - 1,
                            inc=(chomm if cci == CC - 1 else None))
                xslot = ftl((oc % 2) * 512, ((oc % 2) + 1) * 512)
                xn = self.dma("sync", xslot,
                              self.dram["resid2"][:, oc * R + r0: oc * R + r0 + 512],
                              (chxl2, 16))
                self.wait("vector", chomm)
                self.wtok("vector", xn)
                xp = p3f((oc % 2) * 512, (oc % 2 + 1) * 512)

                prev = xpslots.get(oc % 2)
                if prev is not None:
                    self.wtok("vector", prev)
                self.tt("vector", xp, self.psum[:, boff:boff + 512], xslot, OP.add)
                self.seal("vector", chadd)
                self.wait("scalar", chadd)
                if chsmm.n >= 4:
                    self.wait("scalar", chsmm, at=chsmm.n - 3)
                b0 = (oc % 2) * 512
                sbfA = mixk(b0, b0 + 512)
                s2bfA = mixv(b0, b0 + 512)
                self.act(sbfA, xp, AF.Copy, inc=chcst)
                self.act(s2bfA, xp, AF.Square, inc=chcst)
                self.wait("tensor", chcst)
                self.mm(self.bank1(4), self.sb["onescolb"][:, :], sbfA,
                        oc == 0, oc == CC - 1, inc=(chsmm, 1))
                self.mm(self.bank1(5), self.sb["onescolb"][:, :], s2bfA,
                        oc == 0, oc == CC - 1, inc=(chsmm, 1))
                self.wait("sync", chadd, at=chadd.n)
                xpslots[oc % 2] = self.dma(
                    "sync", self.dram["xprime"][:, oc * R + r0: oc * R + r0 + 512], xp)
            self.wait("vector", chsmm)
            rows = self.sb["rows"]
            mneg = rows[0:1, 0:512]
            rstd = rows[0:1, 512:1024]
            chsr2 = self.dch("sync", "p3_strow")
            if chsr2.n > 0:
                self.wait("vector", chsr2)
                self.wait("scalar", chsr2)
            self.ln_rows(self.bank1(4), self.bank1(5), 512, mneg, rstd,
                         1.0 / c.D)
            self.seal("vector", chrow2)
            self.wait("sync", chrow2, at=chrow2.n)
            self.dma("sync", self.dram["stat2"][0:1, r0:r0 + 512], mneg, (chsr2, 16))
            self.dma("sync", self.dram["stat2"][1:2, r0:r0 + 512], rstd, (chsr2, 16))

    def _p3_d(self):
        self.phase("p3d")
        c = self.cfg
        CC, R, T, D = c.CC, c.R, c.T, c.D
        self.barrier("p3d")
        self.AB.reset(); self.AF.reset(); self.AI.reset()
        HHC = c.HC // 2
        hw = c.HID // 2
        self.pd_wck = self.AB.alloc(CC * hw)
        self.pd_wcv = self.AB.alloc(HHC * D)
        self.pd_wcr = self.AB.alloc(CC * D)
        self.pd_mixk = self.AB.alloc(2 * CC * 256)
        self.pd_mixr = self.AB.alloc(2 * CC * 256)
        self.pd_kkb = self.AB.alloc(2 * 256)
        self.pd_sig2 = self.AB.alloc(CC * 256)
        self.pd_cst2 = self.AB.alloc(16 * 256)
        self.pd_f = self.AF.alloc(12288)
        self.pd_rec = {}
        self.pd_dstats = None
        NTB = T // 256
        chwl = self.dch("gpsimd", "pd_wl")
        self.dma("gpsimd", self.pd_wcr().rearrange("p (cc e) -> p cc e", cc=CC),
                 self.dram["wcrPM"][:, :].rearrange("p (cc e) -> p cc e", cc=CC),
                 (chwl, 16))
        for half in range(2):
            if half > 0:
                self.wait("gpsimd", self.chp("pd_kk"))
                self.wait("gpsimd", self.chp("pd_kv"))
                self.wait("sync", self.dch("gpsimd", "pd_ms"))
                self.wait_queue("sync", "scalar")
            self.dma("gpsimd",
                     self.pd_wck().rearrange("p (cc e) -> p cc e", cc=CC),
                     self.dram["wckPM"][:, :].rearrange("p (cc e) -> p cc e", cc=CC)
                     [:, :, half * hw:(half + 1) * hw], (chwl, 16))
            self.dma("gpsimd",
                     self.pd_wcv().rearrange("p (hc e) -> p hc e", hc=HHC),
                     self.dram["wcvPM"][:, :].rearrange("p (hc e) -> p hc e", hc=c.HC)
                     [:, half * HHC:(half + 1) * HHC, :], (chwl, 16))
            self.wait("tensor", chwl)
            for b in range(c.B):
                if half == 0:
                    self.memset("vector", self.sb["tails"][:, CC:2 * CC], 0.0)
                    self._pd_S(half, b, 0)
                    self._pd_M(half, b, 0)
                else:
                    self._pd_Ml(half, b, 0)
                for tb in range(NTB):
                    if tb + 1 < NTB:
                        if half == 0:
                            self._pd_S(half, b, tb + 1)
                            self._pd_M(half, b, tb + 1)
                        else:
                            self._pd_Ml(half, b, tb + 1)
                    self._pd_K(half, b, tb)
                    self._pd_E(half, b, tb)
        self._pd_flush_stats()

    def _pd_u(self, half, b, tb):
        return (half * self.cfg.B + b) * (self.cfg.T // 256) + tb

    def _pd_S(self, half, b, tb):
        c = self.cfg
        u = self._pd_u(half, b, tb)
        rec = self.pd_rec
        r0 = b * c.T + tb * 256
        ss = tb % 2
        rows = self.sb["rows"]
        mrow = rows[0:1, ss * 256:(ss + 1) * 256]
        srow = rows[0:1, 512 + ss * 256:512 + (ss + 1) * 256]
        chst = self.dch("sync", "pd_st")
        chbc = self.chp("pd_bc")
        if ("bc", u - 2) in rec:
            self.wait("sync", chbc, at=rec[("bc", u - 2)])
        self.dma("sync", mrow, self.dram["stat2"][0:1, r0:r0 + 256], (chst, 16))
        self.dma("sync", srow, self.dram["stat2"][1:2, r0:r0 + 256], (chst, 16))
        if ("mix", u - 1) in rec:
            self.wait("tensor", self.chp("pd_mix"), at=rec[("mix", u - 1)])
        self.wait("tensor", chst)
        self.mm(self.psum[:, 6 * 512:6 * 512 + 256], self.sb["onesrow"][:, :],
                mrow, True, True)
        self.mm(self.psum[:, 7 * 512:7 * 512 + 256], self.sb["onesrow"][:, :],
                srow, True, True, inc=chbc)
        rec[("bc", u)] = chbc.n

    def _pd_M(self, half, b, tb):
        c = self.cfg
        CC, R = c.CC, c.R
        u = self._pd_u(half, b, tb)
        rec = self.pd_rec
        r0 = b * c.T + tb * 256
        sl2 = u % 2
        p3f = self.pd_f
        chbc = self.chp("pd_bc")
        chmix = self.chp("pd_mix")
        chxl = self.dch("sync", "pd_xl")
        chms = self.dch("gpsimd", "pd_ms")
        self.wait("vector", chbc, at=rec[("bc", u)])
        if ("kkend", u - 2) in rec:
            self.wait("vector", self.chp("pd_kk"), at=rec[("kkend", u - 2)])
        if ("ms", u - 2) in rec:
            self.wait("vector", chms, at=rec[("ms", u - 2)])
        xr = self.dram["xprime"][:, :].rearrange("p (cc r) -> p cc r", cc=CC)
        for cci in range(CC):
            sl = cci % 2
            xpt = p3f.sub(sl * 256, (sl + 1) * 256)
            h2 = p3f.sub(512 + sl * 256, 512 + (sl + 1) * 256)
            hh2 = p3f.sub(1024 + sl * 256, 1024 + (sl + 1) * 256)
            dd2 = p3f.sub(1536 + sl * 256, 1536 + (sl + 1) * 256)
            if cci % 2 == 0:
                if cci > 0 or u > 0:
                    self.wait("sync", chmix, at=chmix.n)
                xd = p3f.sub(0, 512)
                self.dma("sync", xd().rearrange("p (k r) -> p k r", k=2),
                         xr[:, cci:cci + 2, r0:r0 + 256], (chxl, 16))
                self.wait("vector", chxl)
            self.tt("vector", h2(), xpt(), self.psum[:, 6 * 512:6 * 512 + 256],
                    OP.add)
            self.tt("vector", h2(), h2(), self.psum[:, 7 * 512:7 * 512 + 256],
                    OP.mult)
            self.ts("vector", h2(), h2(), self.pcol("ln2g", cci),
                    self.pcol("ln2b", cci), OP.mult, OP.add)
            tcol = CC + cci
            self.cp("vector", hh2(0, 1), self.sb["tails"][:, tcol:tcol + 1])
            self.cp("vector", hh2(1, 256), h2(0, 255))
            self.cp("vector", self.sb["tails"][:, tcol:tcol + 1], h2(255, 256))
            self.tt("vector", dd2(), h2(), hh2(), OP.subtract)
            self.stt("vector", self.pd_mixk(sl2 * CC * 256 + cci * 256,
                                            sl2 * CC * 256 + (cci + 1) * 256),
                     dd2(), self.pcol("cmk", cci), hh2(), OP.mult, OP.add)
            self.stt("vector", self.pd_mixr(sl2 * CC * 256 + cci * 256,
                                            sl2 * CC * 256 + (cci + 1) * 256),
                     dd2(), self.pcol("cmr", cci), hh2(), OP.mult, OP.add)
            self.seal("vector", chmix)
        rec[("mix", u)] = chmix.n
        self.wait("gpsimd", chmix, at=chmix.n)
        self.dma("gpsimd",
                 self.dram["mixkscr"][:, :].rearrange("p (cc r) -> p cc r", cc=CC)
                 [:, :, r0:r0 + 256],
                 self.pd_mixk(sl2 * CC * 256, (sl2 + 1) * CC * 256)
                 .rearrange("p (cc r) -> p cc r", cc=CC), (chms, 16))
        self.dma("gpsimd",
                 self.dram["mixrscr"][:, :].rearrange("p (cc r) -> p cc r", cc=CC)
                 [:, :, r0:r0 + 256],
                 self.pd_mixr(sl2 * CC * 256, (sl2 + 1) * CC * 256)
                 .rearrange("p (cc r) -> p cc r", cc=CC), (chms, 16))
        rec[("ms", u)] = chms.n

    def _pd_Ml(self, half, b, tb):
        c = self.cfg
        CC = c.CC
        u = self._pd_u(half, b, tb)
        rec = self.pd_rec
        r0 = b * c.T + tb * 256
        sl2 = u % 2
        chml = self.dch("sync", "pd_ml")
        if ("kkend", u - 2) in rec:
            self.wait("sync", self.chp("pd_kk"), at=rec[("kkend", u - 2)])
        if ("cr", u - 2) in rec:
            self.wait("sync", self.chp("pd_cr"), at=rec[("cr", u - 2)])
        self.dma("sync",
                 self.pd_mixk(sl2 * CC * 256, (sl2 + 1) * CC * 256)
                 .rearrange("p (cc r) -> p cc r", cc=CC),
                 self.dram["mixkscr"][:, :].rearrange("p (cc r) -> p cc r", cc=CC)
                 [:, :, r0:r0 + 256], (chml, 16))
        self.dma("sync",
                 self.pd_mixr(sl2 * CC * 256, (sl2 + 1) * CC * 256)
                 .rearrange("p (cc r) -> p cc r", cc=CC),
                 self.dram["mixrscr"][:, :].rearrange("p (cc r) -> p cc r", cc=CC)
                 [:, :, r0:r0 + 256], (chml, 16))
        rec[("ml", u)] = chml.n

    def _pd_K(self, half, b, tb):
        c = self.cfg
        CC, D = c.CC, c.D
        u = self._pd_u(half, b, tb)
        rec = self.pd_rec
        sl2 = u % 2
        nhc = c.HC // 2
        chkk = self.chp("pd_kk")
        chkkb = self.chp("pd_kkb")
        chkv = self.chp("pd_kv")
        chml = self.dch("sync", "pd_ml")
        p3f = self.pd_f

        def emit_kk(hc):
            if hc == 0:
                if half == 0:
                    self.wait("tensor", self.chp("pd_mix"), at=rec[("mix", u)])
                else:
                    self.wait("tensor", chml, at=rec[("ml", u)])
            boff = (4 + hc % 2) * 512
            if ("kkb", u, hc - 2) in rec:
                self.wait("tensor", chkkb, at=rec[("kkb", u, hc - 2)])
            elif ("kkb", u - 1, nhc - 2 + hc) in rec:
                self.wait("tensor", chkkb, at=rec[("kkb", u - 1, nhc - 2 + hc)])
            for cci in range(CC):
                self.mm(self.psum[:, boff:boff + 256],
                        self.pd_wck(cci * (c.HID // 2) + hc * 128,
                                    cci * (c.HID // 2) + (hc + 1) * 128),
                        self.pd_mixk(sl2 * CC * 256 + cci * 256,
                                     sl2 * CC * 256 + (cci + 1) * 256),
                        cci == 0, cci == CC - 1,
                        inc=(chkk if cci == CC - 1 else None))
            rec[("kk", u, hc)] = chkk.n
            if hc == nhc - 1:
                rec[("kkend", u)] = chkk.n

        def emit_scalar(hc):
            self.wait("scalar", chkk, at=rec[("kk", u, hc)])
            r1 = p3f(2048 + (hc % 2) * 256, 2048 + (hc % 2 + 1) * 256)
            boff = (4 + hc % 2) * 512
            self.act(r1, self.psum[:, boff:boff + 256], AF.Relu)
            self.act(self.pd_kkb((hc % 2) * 256, (hc % 2 + 1) * 256), r1,
                     AF.Square, inc=chkkb)
            rec[("kkb", u, hc)] = chkkb.n

        def emit_kv(hc):
            self.wait("tensor", chkkb, at=rec[("kkb", u, hc)])
            if hc == 0:
                for key, chn in ((("fin2", u - 1), "pd_fin2"),
                                 (("fin", u - 1), "pd_fin")):
                    if key in rec:
                        self.wait("tensor", self.chp(chn), at=rec[key])
                        break
            for oc in range(CC):
                kvoff = (oc // 2) * 512 + (oc % 2) * 256
                self.mm(self.psum[:, kvoff:kvoff + 256],
                        self.pd_wcv(hc * D + oc * 128, hc * D + (oc + 1) * 128),
                        self.pd_kkb((hc % 2) * 256, (hc % 2 + 1) * 256),
                        hc == 0 and oc % 2 == 0,
                        hc == nhc - 1 and oc % 2 == 1,
                        inc=(chkv if (hc == nhc - 1 and oc == CC - 1) else None))
            if hc == nhc - 1:
                rec[("kv", u)] = chkv.n

        emit_kk(0)
        emit_scalar(0)
        emit_kk(1)
        emit_scalar(1)
        for hc in range(nhc):
            emit_kv(hc)
            if hc + 2 < nhc:
                emit_kk(hc + 2)
                emit_scalar(hc + 2)

    def _pd_E(self, half, b, tb):
        if half == 0:
            self._pd_E0(half, b, tb)
        else:
            self._pd_E1(half, b, tb)

    def _pd_E0(self, half, b, tb):
        c = self.cfg
        CC, R = c.CC, c.R
        u = self._pd_u(half, b, tb)
        rec = self.pd_rec
        r0 = b * c.T + tb * 256
        p3f = self.pd_f
        chfin = self.chp("pd_fin")
        s4r = self.dram["s4"][:, :].rearrange("p (oc r) -> p oc r", oc=CC)
        self.wait("vector", self.chp("pd_kv"), at=rec[("kv", u)])
        for ocp in range(CC // 2):
            oc = 2 * ocp
            prev = getattr(self, "_pdkvtok", None)
            if prev is not None:
                self.wtok("vector", prev)
            for k in range(2):
                kvoff = ((oc + k) // 2) * 512 + ((oc + k) % 2) * 256
                kvp = p3f(2560 + k * 256, 2560 + (k + 1) * 256)
                self.cp("vector", kvp, self.psum[:, kvoff:kvoff + 256],
                        inc=chfin)
            self.wait("scalar", chfin, at=chfin.n)
            tok = self.dma("scalar", s4r[:, oc:oc + 2, r0:r0 + 256],
                           p3f(2560, 3072).rearrange("p (k r) -> p k r", k=2))
            self._pdkvtok = tok
            self.pd_rec[("s4tok", u, ocp)] = tok
        rec[("fin", u)] = chfin.n

    def _pd_flush_stats(self):
        if self.pd_dstats is None:
            return
        uprev, cstv, r0p = self.pd_dstats
        rec = self.pd_rec
        chsmm = self.chp("pd_smm")
        chrow = self.chp("pd_row")
        chsg = self.chp("pd_sg")
        self.wait("tensor", self.chp("pd_cst"), at=cstv)
        if ("sg", uprev) in rec:
            self.wait("tensor", chsg, at=rec[("sg", uprev)])
        for oc in range(self.cfg.CC):
            sA = self.pd_cst2(oc * 512, oc * 512 + 256)
            s2A = self.pd_cst2(oc * 512 + 256, oc * 512 + 512)
            self.mm(self.psum[0:1, 6 * 512:6 * 512 + 256],
                    self.sb["onescolb"][:, :], sA, oc == 0, oc == self.cfg.CC - 1,
                    inc=(chsmm, 1))
            self.mm(self.psum[0:1, 7 * 512:7 * 512 + 256],
                    self.sb["onescolb"][:, :], s2A, oc == 0,
                    oc == self.cfg.CC - 1, inc=(chsmm, 1))
        rec[("smm", uprev)] = chsmm.n
        self.wait("vector", chsmm)
        rows = self.sb["rows"]
        mneg = rows[0:1, 1024:1280]
        rstd = rows[0:1, 1280:1536]
        chsr = self.dch("sync", "pd_st3")
        if chsr.n > 0:
            self.wait("vector", chsr)
        self.ln_rows(self.psum[0:1, 6 * 512:6 * 512 + 256],
                     self.psum[0:1, 7 * 512:7 * 512 + 256], 256, mneg, rstd,
                     1.0 / self.cfg.D)
        self.seal("vector", chrow)
        rec[("row", uprev)] = chrow.n
        self.wait("sync", chrow, at=chrow.n)
        self.dma("sync", self.dram["stat3"][0:1, r0p:r0p + 256], mneg, (chsr, 16))
        self.dma("sync", self.dram["stat3"][1:2, r0p:r0p + 256], rstd, (chsr, 16))
        self.pd_dstats = None

    def _pd_E1(self, half, b, tb):
        c = self.cfg
        CC, R, D = c.CC, c.R, c.D
        u = self._pd_u(half, b, tb)
        rec = self.pd_rec
        r0 = b * c.T + tb * 256
        sl2 = u % 2
        p3f = self.pd_f
        chcr = self.chp("pd_cr")
        chsg = self.chp("pd_sg")
        chfin2 = self.chp("pd_fin2")
        chcst = self.chp("pd_cst")
        chkvl = self.dch("sync", "pd_kvl")
        self._pd_flush_stats()
        if ("row", u - 1) in rec:
            self.wait("tensor", self.chp("pd_row"), at=rec[("row", u - 1)])
        for oc in range(CC):
            boff = (6 + oc % 2) * 512
            if oc >= 2:
                self.wait("tensor", chsg, at=chsg.n - 1)
            for cci in range(CC):
                self.mm(self.psum[:, boff:boff + 256],
                        self.pd_wcr(cci * D + oc * 128, cci * D + (oc + 1) * 128),
                        self.pd_mixr(sl2 * CC * 256 + cci * 256,
                                     sl2 * CC * 256 + (cci + 1) * 256),
                        cci == 0, cci == CC - 1,
                        inc=(chcr if cci == CC - 1 else None))
            self.wait("scalar", chcr)
            self.act(self.pd_sig2(oc * 256, (oc + 1) * 256),
                     self.psum[:, boff:boff + 256], AF.Sigmoid, inc=chsg)
        rec[("cr", u)] = chcr.n
        rec[("sg", u)] = chsg.n
        s4r = self.dram["s4"][:, :].rearrange("p (oc r) -> p oc r", oc=CC)
        x2r = self.dram["resid2"][:, :].rearrange("p (oc r) -> p oc r", oc=CC)
        xpr = self.dram["xprime"][:, :].rearrange("p (oc r) -> p oc r", oc=CC)
        p4r = self.dram["pre4"][:, :].rearrange("p (oc r) -> p oc r", oc=CC)
        self.wait("vector", chsg, at=rec[("sg", u)])
        if ("smm", u - 1) in rec:
            self.wait("scalar", self.chp("pd_smm"), at=rec[("smm", u - 1)])
        for ocp in range(CC // 2):
            oc = 2 * ocp
            if ("fin2", u - 1) in rec and ocp == 0:
                self.wait("sync", chfin2, at=rec[("fin2", u - 1)])
            if ocp >= 1:
                self.wait("sync", chfin2, at=chfin2.n)
            self.dma("sync", p3f(2560, 3072).rearrange("p (k r) -> p k r", k=2),
                     s4r[:, oc:oc + 2, r0:r0 + 256], (chkvl, 16))
            self.dma("sync", p3f(3072, 3584).rearrange("p (k r) -> p k r", k=2),
                     x2r[:, oc:oc + 2, r0:r0 + 256], (chkvl, 16))
            self.dma("sync", p3f(3584, 4096).rearrange("p (k r) -> p k r", k=2),
                     xpr[:, oc:oc + 2, r0:r0 + 256], (chkvl, 16))
            self.wait("vector", chkvl)
            ppre = getattr(self, "_pdpretok", None)
            if ppre is not None:
                self.wtok("vector", ppre)
            self.wait("vector", chcst, at=chcst.n)
            for k in range(2):
                occ = oc + k
                kvoff = (occ // 2) * 512 + (occ % 2) * 256
                kvp = p3f(2560 + k * 256, 2560 + (k + 1) * 256)
                x2t = p3f(3072 + k * 256, 3072 + (k + 1) * 256)
                xpt = p3f(3584 + k * 256, 3584 + (k + 1) * 256)
                pre = p3f(4096 + k * 256, 4096 + (k + 1) * 256)
                self.tt("vector", kvp, self.psum[:, kvoff:kvoff + 256], kvp,
                        OP.add)
                self.tt("vector", kvp, kvp,
                        self.pd_sig2(occ * 256, (occ + 1) * 256), OP.mult)
                self.tt("vector", pre, kvp, x2t, OP.add)
                self.tt("vector", pre, pre, xpt, OP.add)
                self.seal("vector", chfin2)
                self.wait("scalar", chfin2)
                cb0 = occ * 512
                self.act(self.pd_cst2(cb0, cb0 + 256), pre, AF.Copy, inc=chcst)
                self.act(self.pd_cst2(cb0 + 256, cb0 + 512), pre, AF.Square,
                         inc=chcst)
            self.wait("sync", chfin2, at=chfin2.n)
            self._pdpretok = self.dma(
                "sync", p4r[:, oc:oc + 2, r0:r0 + 256],
                p3f(4096, 4608).rearrange("p (k r) -> p k r", k=2))
        rec[("fin2", u)] = chfin2.n
        self.pd_dstats = (u, self.chp("pd_cst").n, r0)

    def p4(self):
        self.phase("p4a")
        c = self.cfg
        CC, R, D = c.CC, c.R, c.D
        self.barrier("p4")
        self.AB.reset(); self.AF.reset(); self.AI.reset()
        ws = self.AB.alloc(CC * D)
        x3b = self.AB.alloc(CC * 512)
        cst = self.AB.alloc(CC * 1024)
        stile = self.AF.alloc(CC * 512)
        x3f = self.AF.alloc(CC * 512)
        ftl = self.AF.alloc(4 * 512)
        chwl = self.dch("sync", "p4_wl")
        chsg = self.chp("p4_sgn")
        for cci in range(CC):
            sl = cci % 2
            fo = ftl(sl * 1024, (sl + 1) * 1024)
            if cci >= 2:
                self.wait("sync", chsg, at=chsg.n - 1)
            self.dma("sync", fo, self.dram["wsPM"][:, cci * D:(cci + 1) * D],
                     (chwl, 16))
            self.wait("scalar", chwl)
            self.sign_op(ws(cci * D, (cci + 1) * D), fo)
            self.seal("scalar", chsg)
        chst = self.dch("sync", "p4_st")
        chbc = self.chp("p4_bc")
        chpl = self.dch("sync", "p4_pl")
        chap = self.chp("p4_ap")
        chcb = self.chp("p4_cb")
        chmm = self.chp("p4_mm")
        chadd = self.chp("p4_add")
        chcst = self.chp("p4_cst")
        chsmm = self.chp("p4_smm")
        chrow = self.chp("p4_row")
        chs4 = self.chp("p4_s4")
        s4slots = {}
        self.wait("tensor", chsg)
        self.wait("sync", chsg)
        for tq in range(R // 512):
            r0 = tq * 512
            rows = self.sb["rows"]
            if chsmm.n > 0:
                self.wait("scalar", chsmm, at=chsmm.n)
            self.wait("sync", chbc, at=chbc.n)
            chsr4 = self.dch("sync", "p4_strow")
            if chsr4.n > 0:
                self.wait("sync", chsr4)
            self.dma("sync", rows[0:1, 0:512], self.dram["stat3"][0:1, r0:r0 + 512],
                     (chst, 16))
            self.dma("sync", rows[0:1, 512:1024], self.dram["stat3"][1:2, r0:r0 + 512],
                     (chst, 16))
            self.wait("tensor", chst)
            self.wait("tensor", chap)
            self.mm(self.bank(6), self.sb["onesrow"][:, :], rows[0:1, 0:512],
                    True, True)
            self.mm(self.bank(7), self.sb["onesrow"][:, :], rows[0:1, 512:1024],
                    True, True, inc=chbc)
            self.wait("vector", chbc)
            for cci in range(CC):
                fo = ftl((cci % 2) * 512, (cci % 2 + 1) * 512)
                if chap.n >= 2:
                    self.wait("sync", chap, at=chap.n - 1)
                self.dma("sync", fo,
                         self.dram["pre4"][:, cci * R + r0: cci * R + r0 + 512],
                         (chpl, 16))
                self.wait("vector", chpl)
                x3 = x3f(cci * 512, (cci + 1) * 512)
                self.tt("vector", x3, fo, self.bank(6), OP.add)
                self.tt("vector", x3, x3, self.bank(7), OP.mult)
                self.seal("vector", chap)
                self.wait("scalar", chap)
                self.act(x3b(cci * 512, (cci + 1) * 512), x3, AF.Copy, inc=chcb)
            self.wait("tensor", chcb)
            for oc in range(CC):
                boff = (oc % 4) * 512
                if oc >= 4:
                    self.wait("tensor", chadd, at=chadd.n - 3)
                for cci in range(CC):
                    self.mm(self.psum[:, boff:boff + 512],
                            ws(cci * D + oc * 128, cci * D + (oc + 1) * 128),
                            x3b(cci * 512, (cci + 1) * 512),
                            cci == 0, cci == CC - 1,
                            inc=(chmm if cci == CC - 1 else None))
                self.wait("vector", chmm)
                sC = stile(oc * 512, (oc + 1) * 512)
                ps4 = s4slots.get(oc)
                if ps4 is not None:
                    self.wtok("vector", ps4)
                self.tt("vector", sC, self.psum[:, boff:boff + 512],
                        x3f(oc * 512, (oc + 1) * 512), OP.add, inc=chadd)
                self.wait("scalar", chadd)
                cpr = oc * 1024
                self.act(cst(cpr, cpr + 512), sC, AF.Copy, inc=chcst)
                self.act(cst(cpr + 512, cpr + 1024), sC, AF.Square, inc=chcst)
                self.wait("sync", chadd, at=chadd.n)
                s4slots[oc] = self.dma(
                    "sync", self.dram["s4"][:, oc * R + r0: oc * R + r0 + 512], sC)
            self.wait("tensor", chcst, at=chcst.n)
            if chrow.n > 0:
                self.wait("tensor", chrow, at=chrow.n)
            for oc in range(CC):
                cpr = oc * 1024
                self.mm(self.bank1(4), self.sb["onescolb"][:, :], cst(cpr, cpr + 512),
                        oc == 0, oc == CC - 1, inc=(chsmm, 1))
                self.mm(self.bank1(5), self.sb["onescolb"][:, :],
                        cst(cpr + 512, cpr + 1024), oc == 0, oc == CC - 1,
                        inc=(chsmm, 1))
            self.wait("vector", chsmm)
            mneg = rows[0:1, 0:512]
            rstd = rows[0:1, 512:1024]
            chsr4w = self.dch("sync", "p4_strow")
            if chsr4w.n > 0:
                self.wait("vector", chsr4w)
                self.wait("scalar", chsr4w)
            self.ln_rows(self.bank1(4), self.bank1(5), 512, mneg, rstd,
                         1.0 / c.D)
            self.seal("vector", chrow)
            self.wait("sync", chrow, at=chrow.n)
            chsr4 = self.dch("sync", "p4_strow")
            self.dma("sync", self.dram["stat4"][0:1, r0:r0 + 512], mneg, (chsr4, 16))
            self.dma("sync", self.dram["stat4"][1:2, r0:r0 + 512], rstd, (chsr4, 16))
        self.phase("p4b")
        chfs = self.dch("sync", "p4b_st")
        chfb = self.chp("p4b_bc")
        chfl = self.dch("sync", "p4b_l")
        chfo = self.chp("p4b_o")
        outq = self.dch("gpsimd", "q_out")
        self.wait_queue("sync", "sync")
        chsr4b = self.dch("sync", "p4_strow")
        if chsr4b.n > 0:
            self.wait("sync", chsr4b)
        for tq in range(R // 512):
            r0 = tq * 512
            rows = self.sb["rows"]
            self.wait("sync", chfb, at=chfb.n)
            self.dma("sync", rows[0:1, 0:512], self.dram["stat4"][0:1, r0:r0 + 512],
                     (chfs, 16))
            self.dma("sync", rows[0:1, 512:1024], self.dram["stat4"][1:2, r0:r0 + 512],
                     (chfs, 16))
            self.wait("tensor", chfs)
            self.wait("tensor", chfo)
            self.mm(self.bank(6), self.sb["onesrow"][:, :], rows[0:1, 0:512],
                    True, True)
            self.mm(self.bank(7), self.sb["onesrow"][:, :], rows[0:1, 512:1024],
                    True, True, inc=chfb)
            self.wait("vector", chfb)
            for cci in range(CC):
                fo = ftl((cci % 2) * 512, (cci % 2 + 1) * 512)
                ot = ftl((2 + cci % 2) * 512, (3 + cci % 2) * 512)
                if chfo.n >= 2:
                    self.wait("sync", chfo, at=chfo.n - 1)
                self.dma("sync", fo,
                         self.dram["s4"][:, cci * R + r0: cci * R + r0 + 512],
                         (chfl, 16))
                self.wait("vector", chfl)
                prev = getattr(self, "p4o%d" % (cci % 2), None)
                if prev is not None:
                    self.wtok("vector", prev)
                self.tt("vector", ot, fo, self.bank(6), OP.add)
                self.tt("vector", ot, ot, self.bank(7), OP.mult)
                self.seal("vector", chfo)
                self.wait("gpsimd", chfo, at=chfo.n)
                qn = self.dma("gpsimd",
                              self.dram["outPM"][:, cci * R + r0: cci * R + r0 + 512],
                              ot, (outq, 16))
                setattr(self, "p4o%d" % (cci % 2), qn)
        self.wait("gpsimd", outq)

    def build(self):
        self.declare()
        self.prologue()
        up = self.cfg.upto
        self.p1()
        if up != "p1":
            self.p2()
        if up not in ("p1", "p2"):
            self.p3()
        if up not in ("p1", "p2", "p3"):
            self.p4()
        self._final_wait()
        with self.nc.Block() as block:
            for eng in ENGS:
                lst = self.ops[eng]
                def make_body(lst):
                    def body(E):
                        i = 0
                        while i < len(lst):
                            ph = getattr(lst[i], "_ph", "init")
                            j = i
                            while j < len(lst) and getattr(lst[j], "_ph", "init") == ph:
                                j += 1
                            with self.nc.named_scope(ph):
                                for f in lst[i:j]:
                                    f(E)
                            i = j
                    return body
                getattr(block, eng)(make_body(lst))
        self.stack.close()
        return self.nc

    def _final_wait(self):
        for q in ("sync", "gpsimd", "scalar"):
            self.wait_queue("gpsimd", q)


def pm(a, din=None):
    a = np.ascontiguousarray(a)
    Din, cols = a.shape
    return np.ascontiguousarray(
        a.reshape(Din // 128, 128, cols).transpose(1, 0, 2).reshape(128, -1))


def unpm(a, Din, cols):
    return np.ascontiguousarray(
        a.reshape(128, Din // 128, cols).transpose(1, 0, 2).reshape(Din, cols))


def packvec(v):
    return np.ascontiguousarray(np.asarray(v).reshape(-1).reshape(8, 128).T)


def build_mask(win):
    sw = win + 128
    m = np.zeros((128, sw), np.float32)
    p = np.arange(128)[:, None]
    f = np.arange(128)[None, :]
    m[:, 0:128] = np.where(f >= p, 0.0, NEG)
    g = np.arange(sw - win)[None, :]
    m[:, win:sw] = np.where(g <= p, 0.0, NEG)
    return m


def host_inputs(cfg, core_inputs):
    d = {}
    ci = core_inputs
    d["xPM"] = pm(ci["x"])
    d["wqkvPM"] = pm(np.ascontiguousarray(ci["W_qkv"].T))
    d["woutPM"] = pm(np.ascontiguousarray(ci["W_out"].T))
    d["wup_nPM"] = pm(ci["w_up"])
    d["wgate_nPM"] = pm(ci["w_gate"])
    d["wdown_nPM"] = pm(ci["w_down"])
    d["wkPM"] = pm(np.ascontiguousarray(ci["Wk"].T))
    d["wvPM"] = pm(np.ascontiguousarray(ci["Wv"].T))
    d["wrPM"] = pm(np.ascontiguousarray(ci["Wr"].T))
    d["woPM"] = pm(np.ascontiguousarray(ci["Wo"].T))
    d["wcrPM"] = pm(np.ascontiguousarray(ci["Wcr"].T))
    d["wsPM"] = pm(np.ascontiguousarray(ci["weight"].T))
    d["wckPM"] = pm(np.ascontiguousarray(ci["Wck"].T))
    d["wcvPM"] = pm(np.ascontiguousarray(ci["Wcv"].T))
    for n, k in (("ln1g", "ln1_g"), ("ln1b", "ln1_b"), ("ln2g", "ln2_g"),
                 ("ln2b", "ln2_b"), ("td", "time_decay"), ("tf", "time_first"),
                 ("tmk", "tm_k"), ("tmv", "tm_v"), ("tmr", "tm_r"),
                 ("cmk", "cm_k"), ("cmr", "cm_r")):
        d[n + "_in"] = packvec(ci[k])
    d["ident_in"] = np.eye(128, dtype=np.float32)
    d["maskLR_in"] = build_mask(cfg.WIN)
    return d


def pm(a):
    a = np.ascontiguousarray(a)
    Din, cols = a.shape
    return np.ascontiguousarray(
        a.reshape(Din // 128, 128, cols).transpose(1, 0, 2).reshape(128, -1))


def unpm(a, Din, cols):
    return np.ascontiguousarray(
        np.asarray(a).reshape(128, Din // 128, cols).transpose(1, 0, 2)
        .reshape(Din, cols))


def packvec(v):
    return np.ascontiguousarray(
        np.asarray(v, dtype=np.float32).reshape(-1).reshape(8, 128).T)


def build_mask(win):
    sw = win + 128
    m = np.zeros((128, sw), np.float32)
    p = np.arange(128)[:, None]
    f = np.arange(128)[None, :]
    m[:, 0:128] = np.where(f >= p, 0.0, NEG)
    g = np.arange(128)[None, :]
    m[:, win:sw] = np.where(np.arange(sw - win)[None, :] <= p, 0.0, NEG)
    return m


def kernel(L0_attn_x, L0_attn_W_qkv, L0_attn_W_out, L0_ffn_w_gate, L0_ffn_w_up,
           L0_ffn_w_down, L1_attn_ln1_g, L1_attn_ln1_b, L1_attn_ln2_g,
           L1_attn_ln2_b, L1_attn_time_decay, L1_attn_time_first, L1_attn_tm_k,
           L1_attn_tm_v, L1_attn_tm_r, L1_attn_Wk, L1_attn_Wv, L1_attn_Wr,
           L1_attn_Wo, L1_attn_cm_k, L1_attn_cm_r, L1_attn_Wck, L1_attn_Wcr,
           L1_attn_Wcv, L1_ffn_weight):
    from concourse.bass_utils import run_bass_kernel_spmd

    NCORES = 8
    cfg = Cfg(T=2048, WIN=512, I_FFN=3072, HID=4096, n_cores=NCORES, upto="p4")
    mkb = MKFull(cfg)
    nc = mkb.build()

    f32 = lambda a: np.ascontiguousarray(np.asarray(a), dtype=np.float32)
    base = {
        "wqkvPM": pm(f32(L0_attn_W_qkv).T),
        "woutPM": pm(f32(L0_attn_W_out).T),
        "wup_nPM": pm(f32(L0_ffn_w_up)),
        "wgate_nPM": pm(f32(L0_ffn_w_gate)),
        "wdown_nPM": pm(f32(L0_ffn_w_down)),
        "wkPM": pm(f32(L1_attn_Wk).T),
        "wvPM": pm(f32(L1_attn_Wv).T),
        "wrPM": pm(f32(L1_attn_Wr).T),
        "woPM": pm(f32(L1_attn_Wo).T),
        "wcrPM": pm(f32(L1_attn_Wcr).T),
        "wsPM": pm(f32(L1_ffn_weight).T),
        "wckPM": pm(f32(L1_attn_Wck).T),
        "wcvPM": pm(f32(L1_attn_Wcv).T),
        "ln1g_in": packvec(L1_attn_ln1_g), "ln1b_in": packvec(L1_attn_ln1_b),
        "ln2g_in": packvec(L1_attn_ln2_g), "ln2b_in": packvec(L1_attn_ln2_b),
        "td_in": packvec(L1_attn_time_decay), "tf_in": packvec(L1_attn_time_first),
        "tmk_in": packvec(L1_attn_tm_k), "tmv_in": packvec(L1_attn_tm_v),
        "tmr_in": packvec(L1_attn_tm_r), "cmk_in": packvec(L1_attn_cm_k),
        "cmr_in": packvec(L1_attn_cm_r),
        "ident_in": np.eye(128, dtype=np.float32),
        "maskLR_in": build_mask(cfg.WIN),
    }
    x = f32(L0_attn_x)
    in_maps = []
    for c in range(NCORES):
        m = dict(base)
        xs = x[2 * c:2 * c + 2]
        m["xPM"] = pm(np.ascontiguousarray(
            xs.transpose(2, 0, 1).reshape(1024, cfg.R)))
        in_maps.append(m)
    res = run_bass_kernel_spmd(nc, in_maps, core_ids=list(range(NCORES)))
    out = np.empty((16, 2048, 1024), np.float32)
    for c in range(NCORES):
        o = unpm(res.results[c]["outPM"], 1024, cfg.R)
        out[2 * c:2 * c + 2] = o.reshape(1024, 2, 2048).transpose(1, 2, 0)
    return out


# version 19
# speedup vs baseline: 1.2372x; 1.0073x over previous
import contextlib
import numpy as np
import concourse.bass as bass
import concourse.mybir as mybir
from concourse.alu_op_type import AluOpType as OP

F32 = mybir.dt.float32
BF16 = mybir.dt.bfloat16
I32 = mybir.dt.int32
AF = mybir.ActivationFunctionType

ENGS = ("tensor", "vector", "scalar", "gpsimd", "sync")
NEG = -1.0e30
QMAX = 32767.0
OFFS = 32768.0
JCOL = {"vector": 0, "scalar": 1, "gpsimd": 2, "sync": 3, "tensor": 4}


class Cfg:
    def __init__(self, T=2048, WIN=512, I_FFN=3072, HID=4096, n_cores=8,
                 upto="p4", debug_outputs=()):
        self.D = 1024
        self.CC = 8
        self.H = 16
        self.DH = 64
        self.NP = 16
        self.B = 2
        self.T = T
        self.R = self.B * T
        self.WIN = WIN
        self.SW = WIN + 128
        self.NKT = T // 128
        self.I_FFN = I_FFN
        self.IC = I_FFN // 128
        self.HID = HID
        self.HC = HID // 128
        self.n_cores = n_cores
        self.upto = upto
        self.debug_outputs = set(debug_outputs)
        assert T % 512 == 0 and self.WIN % 128 == 0
        assert I_FFN % 128 == 0 and HID % 128 == 0


class Ch:
    def __init__(self, h):
        self.h = h
        self.n = 0


class Reg:
    def __init__(self, h, off, n, p=128):
        self.h, self.off, self.n, self.p = h, off, n, p

    def __call__(self, c0=0, c1=None, p0=0, p1=None):
        c1 = self.n if c1 is None else c1
        p1 = self.p if p1 is None else p1
        return self.h[p0:p1, self.off + c0: self.off + c1]

    @property
    def ap(self):
        return self()

    def sub(self, c0, c1, p=None):
        return Reg(self.h, self.off + c0, c1 - c0, self.p if p is None else p)


class Arena:
    def __init__(self, handle, total):
        self.h = handle
        self.total = total
        self.off = 0

    def alloc(self, n):
        assert self.off + n <= self.total, f"arena overflow {self.off}+{n}>{self.total}"
        r = Reg(self.h, self.off, n)
        self.off += n
        return r

    def reset(self):
        self.off = 0


def _ap_span(ap):
    try:
        name = ap.tensor.name
    except Exception:
        return None
    try:
        off = int(ap.offset)
        dims = [list(d) for d in ap.ap]
        ext = 0
        for st, cnt in dims[1:]:
            ext += abs(int(st)) * (int(cnt) - 1)
        return (name, off, off + ext + 1)
    except Exception:
        return (name, 0, 1 << 60)


class MKBase:
    def __init__(self, cfg: Cfg):
        self.cfg = cfg
        self.ops = {e: [] for e in ENGS}
        self.stack = contextlib.ExitStack()
        self.nc = bass.Bass("TRN2", target_bir_lowering=False, debug=False)
        self.sb = {}
        self.dram = {}
        self.chs = {}
        self._w = {e: [] for e in ENGS}
        self._r = {e: [] for e in ENGS}
        self.pmax_col = 0

    def ch(self, name):
        if name not in self.chs:
            self.chs[name] = Ch(self.stack.enter_context(self.nc.semaphore(name)))
        return self.chs[name]

    NGEN = 44

    def chp(self, name):
        if not hasattr(self, "_chp_alias"):
            self._chp_alias = {}
            self._chp_next = 0
        if name not in self._chp_alias:
            i = self._chp_next
            assert i < self.NGEN, "generic channel pool exhausted"
            self._chp_next = i + 1
            self._chp_alias[name] = self.ch("g%d" % i)
        return self._chp_alias[name]

    NDCH = {"sync": 10, "gpsimd": 8, "scalar": 6}

    def dch(self, q, name):
        key = "_dch_" + q
        if not hasattr(self, key):
            setattr(self, key, ({}, [0]))
        alias, nxt = getattr(self, key)
        if name not in alias:
            i = nxt[0]
            assert i < self.NDCH[q], "dma channel pool exhausted for " + q
            nxt[0] = i + 1
            alias[name] = self.ch("d_%s_%d" % (q, i))
        return alias[name]

    def dram_t(self, name, shape, dtype, kind="Internal"):
        ap = self.nc.dram_tensor(name, list(shape), dtype, kind=kind).ap()
        self.dram[name] = ap
        return ap

    def sbuf(self, name, shape, dtype):
        h = self.stack.enter_context(self.nc.sbuf_tensor(name, list(shape), dtype))
        self.sb[name] = h
        return h

    def emit(self, eng, fn):
        fn._ph = getattr(self, "_phase", "init")
        self.ops[eng].append(fn)

    def phase(self, name):
        self._phase = name

    @staticmethod
    def _overlap(a, b):
        return a[0] == b[0] and a[1] < b[2] and b[1] < a[2]

    def _hazard(self, eng, reads, writes):
        if eng in ("tensor", "sync"):
            return
        need = False
        for s in reads:
            if s is None:
                continue
            for w in self._w[eng]:
                if self._overlap(s, w):
                    need = True
                    break
            if need:
                break
        if not need:
            for s in writes:
                if s is None:
                    continue
                for w in self._w[eng] + self._r[eng]:
                    if self._overlap(s, w):
                        need = True
                        break
                if need:
                    break
        if need:
            self.emit(eng, lambda E: E.drain())
            self._w[eng] = []
            self._r[eng] = []
        self._r[eng].extend([s for s in reads if s is not None])
        self._w[eng].extend([s for s in writes if s is not None])

    def _spans(self, aps):
        out = []
        for a in aps:
            if a is None or isinstance(a, (int, float)):
                continue
            out.append(_ap_span(a))
        return out

    def _inc(self, instr, inc):
        if inc is not None:
            ch, k = inc if isinstance(inc, tuple) else (inc, 1)
            instr.then_inc(ch.h, k)

    def _cnt(self, inc):
        if inc is not None:
            ch, k = inc if isinstance(inc, tuple) else (inc, 1)
            ch.n += k

    def _op(self, eng, reads, writes, inc, builder):
        self._hazard(eng, self._spans(reads), self._spans(writes))
        self._cnt(inc)
        def f(E, builder=builder, inc=inc):
            i = builder(E)
            self._inc(i, inc)
        self.emit(eng, f)

    def wait(self, eng, ch, at=None):
        v = ch.n if at is None else at
        self.emit(eng, lambda E, h=ch.h, v=v: E.wait_ge(h, v))

    def drain(self, eng):
        self.emit(eng, lambda E: E.drain())
        self._w[eng] = []
        self._r[eng] = []

    def seal(self, eng, inc):
        self.drain(eng)
        self._cnt(inc)
        ch, k = inc if isinstance(inc, tuple) else (inc, 1)
        if eng == "tensor":
            def f(E, h=ch.h, k=k):
                E.nop().then_inc(h, k)
        elif eng == "scalar":
            col = JCOL[eng]
            def f(E, h=ch.h, k=k, col=col):
                E.memzero(self.sb["junk"][0:1, col:col + 1]).then_inc(h, k)
        else:
            col = JCOL[eng]
            def f(E, h=ch.h, k=k, col=col):
                E.memset(self.sb["junk"][0:1, col:col + 1], 0.0).then_inc(h, k)
        self.emit(eng, f)

    def qch(self, q):
        return self.ch("q_" + q)

    NPOOL = 6

    def qpool(self, q):
        key = "_qp_" + q
        if not hasattr(self, key):
            setattr(self, key, [self.ch("qp_%s_%d" % (q, i))
                                for i in range(self.NPOOL)])
            setattr(self, "_qprr_" + q, 0)
        return getattr(self, key)

    def dma(self, q, out, in_, inc=None, transpose=False):
        if inc is None:
            pool = self.qpool(q)
            rr = getattr(self, "_qprr_" + q)
            ch = pool[rr % self.NPOOL]
            setattr(self, "_qprr_" + q, rr + 1)
            k = 16
        else:
            ch, k = inc if isinstance(inc, tuple) else (inc, 16)
        if ch.n > 0:
            self.wait(q, ch, at=ch.n)
        ch.n += k
        def f(E, out=out, in_=in_, h=ch.h, k=k, t=transpose):
            if t:
                E.dma_start(out=out, in_=in_, transpose=True).then_inc(h, k)
            else:
                E.dma_start(out=out, in_=in_).then_inc(h, k)
        self.emit(q, f)
        return (ch, ch.n)

    def wtok(self, eng, tok):
        ch, n = tok
        self.wait(eng, ch, at=n)

    def wait_queue(self, eng, q):
        for ch in self.qpool(q):
            if ch.n > 0:
                self.wait(eng, ch)
        qc = self.qch(q)
        if qc.n > 0:
            self.wait(eng, qc)

    def mm(self, out, lhsT, rhs, start, stop, inc=None, tp=None):
        self._cnt(inc)
        def f(E, out=out, l=lhsT, r=rhs, s=start, p=stop, inc=inc, tp=tp):
            i = E.matmul(out, l, r, start=s, stop=p, tile_position=tp)
            self._inc(i, inc)
        self.emit("tensor", f)

    def act(self, out, in_, func, scale=1.0, bias=None, accum=None, inc=None):
        rs = [in_, scale if not isinstance(scale, (int, float)) else None,
              bias if not isinstance(bias, (int, float)) else None]
        ws = [out] + ([accum] if accum is not None else [])
        def b(E, out=out, in_=in_, func=func, scale=scale, bias=bias, accum=accum):
            kw = {}
            if bias is not None:
                kw["bias"] = bias
            if accum is not None:
                kw["accum_out"] = accum
            return E.activation(out, in_, func, scale=scale, **kw)
        self._op("scalar", rs, ws, inc, b)

    def sign_op(self, out, in_, inc=None):
        self._op("scalar", [in_], [out], inc, lambda E, o=out, i=in_: E.sign(o, i))

    def cp(self, eng, out, in_, inc=None):
        self._op(eng, [in_], [out], inc, lambda E, o=out, i=in_: E.tensor_copy(o, i))

    def tt(self, eng, out, a, b, op, inc=None):
        self._op(eng, [a, b], [out], inc,
                 lambda E, o=out, a=a, b=b, op=op: E.tensor_tensor(o, a, b, op))

    def ts(self, eng, out, in0, s1, s2, op0, op1=None, accum=None, inc=None):
        rs = [in0] + [x for x in (s1, s2)
                      if x is not None and not isinstance(x, (int, float))]
        ws = [out] + ([accum] if accum is not None else [])
        def b(E, out=out, in0=in0, s1=s1, s2=s2, op0=op0, op1=op1, accum=accum):
            kw = {}
            if op1 is not None:
                kw["op1"] = op1
            if accum is not None:
                kw["accum_out"] = accum
            return E.tensor_scalar(out, in0, s1, s2, op0, **kw)
        self._op(eng, rs, ws, inc, b)

    def stt(self, eng, out, in0, scalar, in1, op0, op1, inc=None):
        rs = [in0, in1] + ([scalar] if not isinstance(scalar, (int, float)) else [])
        def b(E, out=out, in0=in0, s=scalar, in1=in1, op0=op0, op1=op1):
            return E.scalar_tensor_tensor(out, in0, s, in1, op0, op1)
        self._op(eng, rs, [out], inc, b)

    def scan(self, out, d0, d1, init, inc=None):
        rs = [d0, d1] + ([init] if not isinstance(init, (int, float)) else [])
        def b(E, out=out, d0=d0, d1=d1, init=init):
            return E.tensor_tensor_scan(out, d0, d1, init, OP.mult, OP.add)
        self._op("vector", rs, [out], inc, b)

    def red(self, eng, out, in_, op, absval=False, inc=None):
        def b(E, out=out, in_=in_, op=op, a=absval):
            return E.tensor_reduce(out, in_, axis=mybir.AxisListType.X, op=op,
                                   apply_absolute_value=(True if a else None))
        self._op(eng, [in_], [out], inc, b)

    def recip(self, out, in_, inc=None):
        self._op("vector", [in_], [out], inc,
                 lambda E, o=out, i=in_: E.reciprocal(o, i))

    def memset(self, eng, ap, val, inc=None):
        self._op(eng, [], [ap], inc, lambda E, a=ap, v=val: E.memset(a, v))

    def collective(self, op, ins, outs, inc=None):
        rg = [[i for i in range(self.cfg.n_cores)]]
        self._cnt(inc)
        def f(E, ins=ins, outs=outs, op=op, rg=rg, inc=inc):
            i = E.collective_compute("AllReduce", op, replica_groups=rg,
                                     ins=ins, outs=outs)
            self._inc(i, inc)
        self.emit("gpsimd", f)

    def barrier(self, name):
        pb = self.ch("bar")
        for q in ("sync", "gpsimd", "scalar"):
            self.wait_queue("gpsimd", q)
            key = "_dch_" + q
            if hasattr(self, key):
                for chx in getattr(self, key)[0].values():
                    if chx.n > 0:
                        self.wait("gpsimd", chx)
        self._chp_alias = {}
        self._chp_next = 0
        for q in ("sync", "gpsimd", "scalar"):
            key = "_dch_" + q
            if hasattr(self, key):
                getattr(self, key)[0].clear()
                getattr(self, key)[1][0] = 0
        for e in ("tensor", "vector", "scalar", "gpsimd"):
            self.seal(e, pb)
        for e in ENGS:
            self.wait(e, pb)

    def pp_col(self, i):
        return self.sb["pp"][:, i:i + 1]

    def s11c(self, i):
        return self.sb["s11"][0:1, i:i + 1]

    def pcol(self, name, cc):
        i = self.pidx[name]
        return self.sb["params"][:, i * self.cfg.CC + cc: i * self.cfg.CC + cc + 1]

    def bcast_to_pp(self, s11_idx, pp_idx):
        ps = self.psum[0:128, 3584:3585]
        cm = self.ch("ppmm")
        cf = self.ch("ppfree")
        self.wait("tensor", cf)
        self.mm(ps, self.sb["onesrow"][0:1, 0:128], self.s11c(s11_idx), True, True,
                inc=cm)
        self.wait("vector", cm)
        self.cp("vector", self.pp_col(pp_idx), ps, inc=cf)
        self.seal("vector", self.ch("ppv"))

    def track_max(self, src_ap, inc=None):
        col = self.pmax_col
        self.pmax_col += 1
        assert col < 448
        self.red("vector", self.sb["pmaxw"][:, col:col + 1], src_ap, OP.max,
                 absval=True, inc=inc)
        return col

    def finish_max(self, groups, tag):
        n = len(groups)
        cv = self.ch("fm_v" + tag)
        cm = self.ch("fm_m" + tag)
        cg = self.ch("fm_g" + tag)
        cc = self.ch("fm_c" + tag)
        self.drain("vector")
        for i, (lo, hi, si) in enumerate(groups):
            self.red("vector", self.sb["pmaxcol"][:, i:i + 1],
                     self.sb["pmaxw"][:, lo:hi], OP.max)
        self.seal("vector", cv)
        self.wait("tensor", cv)
        for i in range(n):
            self.mm(self.psum[0:1, i * 512:i * 512 + 128],
                    self.sb["pmaxcol"][:, i:i + 1], self.sb["ident"][:, :],
                    True, True, inc=cm)
        self.wait("vector", cm)
        for i in range(n):
            self.red("vector", self.sb["s11"][0:1, 24 + i:25 + i],
                     self.psum[0:1, i * 512:i * 512 + 128], OP.max)
        self.seal("vector", cv)
        self.wait("gpsimd", cv)
        self.dma("gpsimd", self.dram["ccin" + tag][0:1, 0:n],
                 self.sb["s11"][0:1, 24:24 + n], (cg, 16))
        self.wait("gpsimd", cg)
        self.collective(OP.max, [self.dram["ccin" + tag][:, :]],
                        [self.dram["ccout" + tag][:, :]], inc=cc)
        self.wait("gpsimd", cc)
        self.dma("gpsimd", self.sb["s11"][0:1, 24:24 + n],
                 self.dram["ccout" + tag][0:1, 0:n], (cg, 16))
        self.wait("vector", cg)
        for i, (lo, hi, si) in enumerate(groups):
            self.cp("vector", self.s11c(si), self.sb["s11"][0:1, 24 + i:25 + i])
        self.pmax_col = 0

    def ln_rows(self, sum_ap, sq_ap, W, mneg_out, rstd_out, invD):
        rows = self.sb["rows"]
        ms = rows[0:1, 1024:1024 + W]
        var = rows[0:1, 1536:1536 + W]
        self.ts("vector", mneg_out, sum_ap, -invD, None, OP.mult)
        self.tt("vector", ms, mneg_out, mneg_out, OP.mult)
        cv = self.ch("lnr_v")
        self.stt("vector", var, sq_ap, invD, ms, OP.mult, OP.subtract)
        self.seal("vector", cv)
        self.wait("scalar", cv)
        cs = self.ch("lnr_s")
        self.act(var, var, AF.Ln, scale=1.0, bias=1e-5)
        self.act(rstd_out, var, AF.Exp, scale=-0.5)
        self.seal("scalar", cs)
        self.wait("vector", cs)
        self.wait("tensor", cs)

    def declare(self):
        c = self.cfg
        D, R, T, I = c.D, c.R, c.T, c.I_FFN
        dbg = c.debug_outputs
        kindof = lambda n: "ExternalOutput" if n in dbg else "Internal"
        self.dram_t("xPM", [128, c.CC * R], F32, kind="ExternalInput")
        self.dram_t("wqkvPM", [128, c.CC * 3 * D], F32, kind="ExternalInput")
        self.dram_t("woutPM", [128, c.CC * D], F32, kind="ExternalInput")
        self.dram_t("wup_nPM", [128, c.IC * D], F32, kind="ExternalInput")
        self.dram_t("wgate_nPM", [128, c.IC * D], F32, kind="ExternalInput")
        self.dram_t("wdown_nPM", [128, c.CC * I], F32, kind="ExternalInput")
        for n in ("wkPM", "wvPM", "wrPM", "woPM", "wcrPM", "wsPM"):
            self.dram_t(n, [128, c.CC * D], F32, kind="ExternalInput")
        self.dram_t("wckPM", [128, c.CC * c.HID], F32, kind="ExternalInput")
        self.dram_t("wcvPM", [128, c.HC * D], F32, kind="ExternalInput")
        for n in ("ln1g", "ln1b", "ln2g", "ln2b", "td", "tf", "tmk", "tmv", "tmr",
                  "cmk", "cmr"):
            self.dram_t(n + "_in", [128, c.CC], F32, kind="ExternalInput")
        self.dram_t("ident_in", [128, 128], F32, kind="ExternalInput")
        self.dram_t("maskLR_in", [128, c.SW], F32, kind="ExternalInput")
        self.dram_t("outPM", [128, c.CC * R], F32, kind="ExternalOutput")
        scr = [("resid1", c.CC * R, F32), ("resid2", c.CC * R, F32),
               ("avscr", c.CC * R, BF16),
               ("spillA", c.IC * R, BF16), ("spillB", c.IC * R, BF16),
               ("spillC", c.IC * R, BF16), ("spillD", c.IC * R, F32),
               ("wdqA_n", c.IC * D, BF16), ("wdqB_n", c.IC * D, BF16),
               ("wdqC_n", c.CC * I, BF16),
               ("kscr", c.CC * R, F32), ("vscr", c.CC * R, F32),
               ("xprime", c.CC * R, F32), ("pre4", c.CC * R, F32),
               ("s4", c.CC * R, F32),
               ("mixkscr", c.CC * R, BF16), ("mixrscr", c.CC * R, BF16)]
        for n, w, dt in scr:
            self.dram_t(n, [128, w], dt, kind=kindof(n))
        for n in ("stat1", "stat2", "stat3", "stat4"):
            self.dram_t(n, [2, R], F32, kind=kindof(n))
        for t, w in (("1", 1), ("2", 2), ("3", 1), ("4", 1)):
            self.dram_t("ccin" + t, [1, w], F32)
            self.dram_t("ccout" + t, [1, w], F32)

        self.psum = self.stack.enter_context(
            self.nc.psum_tensor("PS", [128, 4096], F32))[:, :]
        self.sbuf("ident", [128, 128], F32)
        self.sbuf("onesrow", [1, 128], F32)
        self.sbuf("onescolb", [128, 1], BF16)
        self.sbuf("onesb64", [128, 64], BF16)
        self.sbuf("identb", [128, 128], BF16)
        self.sbuf("maskb", [128, c.SW], BF16)
        self.sbuf("params", [128, 16 * c.CC], F32)
        self.sbuf("pp", [128, 32], F32)
        self.sbuf("rows", [1, 2048], F32)
        self.sbuf("s11", [1, 32], F32)
        self.sbuf("junk", [1, 8], F32)
        self.sbuf("pmaxw", [128, 448], F32)
        self.sbuf("pmaxcol", [128, 8], F32)
        self.sbuf("dcol", [128, c.CC], F32)
        self.sbuf("constv", [128, 2], F32)
        self.nc.const_aps.register(F32, 1e-5, self.sb["constv"][:, 0:1])
        self.sbuf("tails", [128, 2 * c.CC], F32)
        self.sbuf("pers", [16, T], BF16)
        self.AB = Arena(self.sbuf("AB", [128, 61440], BF16), 61440)
        self.AF = Arena(self.sbuf("AF", [128, 13824], F32), 13824)
        self.AI = Arena(self.sbuf("AI", [128, 4096], I32), 4096)

    def bank(self, i, w=512):
        return self.psum[:, i * 512: i * 512 + w]

    def bank1(self, i, w=512):
        return self.psum[0:1, i * 512: i * 512 + w]

    def prologue(self):
        self.phase("prologue")
        c = self.cfg
        cc0 = self.ch("const")
        self.dma("sync", self.sb["ident"][:, :], self.dram["ident_in"][:, :], (cc0, 16))
        ccg = self.ch("constg")
        self.dma("gpsimd", self.sb["maskb"][:, :], self.dram["maskLR_in"][:, :],
                 (ccg, 16))
        pn = ("ln1g", "ln1b", "ln2g", "ln2b", "td", "tf", "tmk", "tmv", "tmr",
              "cmk", "cmr")
        self.pidx = {n: i for i, n in enumerate(pn)}
        for n in pn:
            i = self.pidx[n]
            self.dma("sync", self.sb["params"][:, i * c.CC:(i + 1) * c.CC],
                     self.dram[n + "_in"][:, :], (cc0, 16))
        self.wait("vector", cc0)
        self.wait("vector", self.ch("constg"))
        self.memset("vector", self.sb["onesrow"][:, :], 1.0)
        self.memset("vector", self.sb["onescolb"][:, :], 1.0)
        self.memset("vector", self.sb["onesb64"][:, :], 1.0)
        self.memset("vector", self.sb["pmaxw"][:, :], 0.0)
        self.memset("vector", self.sb["tails"][:, :], 0.0)
        self.memset("vector", self.sb["constv"][:, 0:1], 1e-5)
        self.wait("scalar", cc0)
        self.act(self.sb["identb"][:, :], self.sb["ident"][:, :], AF.Copy)
        td = self.sb["params"][:, self.pidx["td"] * c.CC:(self.pidx["td"] + 1) * c.CC]
        self.act(self.sb["dcol"][:, :], td, AF.Exp)
        self.act(self.sb["dcol"][:, :], self.sb["dcol"][:, :], AF.Exp, scale=-1.0)
        cs = self.ch("constdone")
        self.seal("scalar", cs)
        self.seal("vector", cs)
        self.const_done = cs
        for e in ("tensor", "vector", "scalar", "gpsimd"):
            self.wait(e, cs)


class MKP1(MKBase):
    def dequant_piece(self, src_name, dst_name, piece_off, seq):
        _ph = getattr(self, '_phase', 'init'); self.phase('dq')
        buf = seq % 2
        NC = 1024
        cload = self.dch("sync", "dq_ld")
        cred = self.chp("dq_red")
        cg1 = self.chp("dq_g1")
        csc = self.chp("dq_sc")
        cg2 = self.chp("dq_g2")
        cst = self.dch("sync", "dq_st")
        w = self.dq_wnat.sub(buf * NC, (buf + 1) * NC)
        wsc = self.dq_wsc.sub(buf * 64, buf * 64 + 32)
        wrc = self.dq_wsc.sub(buf * 64 + 32, buf * 64 + 64)
        wqi = self.dq_wqi.sub(buf * NC, (buf + 1) * NC)
        wqf = self.dq_wqf.sub(buf * NC, (buf + 1) * NC)
        wqb = self.dq_wqb.sub(0, NC)
        nblk = NC // 32
        if seq >= 2:
            self.wait("sync", cst, at=cst.n)
        self.dma("sync", w(), self.dram[src_name][:, piece_off:piece_off + NC],
                 (cload, 16))
        self.wait("vector", cload)
        self.red("vector", wsc(0, nblk),
                 w().rearrange("p (s e) -> p s e", e=32), OP.max, absval=True)
        self.ts("vector", wsc(0, nblk), wsc(0, nblk), 1.0 / 7.0, None, OP.mult)
        self.recip(wrc(0, nblk), wsc(0, nblk), inc=cred)
        self.wait("gpsimd", cred)
        wscap, wrcap = wsc(), wrc()
        rexp = bass.AP(wrcap.tensor, wrcap.offset,
                       [list(wrcap.ap[0]), [1, nblk], [0, 32]])
        sexp = bass.AP(wscap.tensor, wscap.offset,
                       [list(wscap.ap[0]), [1, nblk], [0, 32]])
        if seq >= 2:
            self.wait("gpsimd", cg2, at=cg2.n)
        self.tt("gpsimd", wqf().rearrange("p (s e) -> p s e", e=32),
                w().rearrange("p (s e) -> p s e", e=32), rexp, OP.mult, inc=cg1)
        self.wait("scalar", cg1)
        if seq >= 2:
            self.wait("scalar", cg2, at=cg2.n)
        self.act(wqi(), wqf(), AF.Copy, scale=1.0, bias=8.5, inc=csc)
        self.wait("vector", csc)
        if seq >= 1:
            self.wait("vector", cst, at=cst.n)
        self.stt("vector", wqf(), wqi(), -8.5, wqf(), OP.add, OP.is_gt)
        self.stt("vector", wqf(), wqi(), -8.0, wqf(), OP.add, OP.subtract)
        self.tt("vector", wqb().rearrange("p (s e) -> p s e", e=32),
                wqf().rearrange("p (s e) -> p s e", e=32), sexp, OP.mult,
                inc=cg2)
        self.wait("sync", cg2)
        tok = self.dma("sync", self.dram[dst_name][:, piece_off:piece_off + NC],
                       wqb(), (cst, 16))
        if not hasattr(self, "dq_tok"):
            self.dq_tok = {}
        self.dq_tok[seq] = tok
        self.phase(_ph)

    def dequant_plan(self):
        c = self.cfg
        plan = []
        for ic in range(c.IC):
            plan.append(("wup_nPM", "wdqA_n", ic * c.D))
            plan.append(("wgate_nPM", "wdqB_n", ic * c.D))
        for j in range((c.CC * c.I_FFN) // 1024):
            plan.append(("wdown_nPM", "wdqC_n", j * 1024))
        return plan

    def _dq_interleave(self, k=3):
        for _ in range(k):
            if self.dqi < len(self.dqplan):
                s, d, o = self.dqplan[self.dqi]
                self.dequant_piece(s, d, o, self.dqi)
                self.dqi += 1

    def p1(self):
        self.phase("p1")
        c = self.cfg
        T, CC = c.T, c.CC
        self.AB.reset(); self.AF.reset(); self.AI.reset()
        self.xstr = self.AB.alloc(2 * CC * 512)
        self.vaug = self.AB.alloc((T // 128) * 1024)
        self.qk = self.AB.alloc(2 * T)
        self.wqk = self.AB.alloc(2 * 2 * CC * 128)
        self.strips = self.AB.alloc(c.NKT * c.SW)
        self.wv_sb = self.AB.alloc(CC * c.D)
        self.avstg = self.AB.alloc(2 * 512)
        self.wslot = self.AB.alloc(2 * CC * 128)
        self.cast2 = self.AB.alloc(2 * 1024)
        self.avin = self.AB.alloc(CC * 512)
        self.stile = self.AF.alloc(CC * 512)
        self.rdenA = self.AF.alloc(512)
        self.rdenB = self.AF.alloc(512)
        self.xres = self.AF.alloc(2 * 512)
        self.f2 = self.AF.alloc(2 * 512)

        wq = self.dram["wqkvPM"][:, :].rearrange("p (cc e) -> p cc e", cc=CC)
        chwv = self.dch("gpsimd", "wvld")
        self.dma("gpsimd", self.wv_sb().rearrange("p (cc e) -> p cc e", cc=CC),
                 wq[:, :, 2 * c.D:3 * c.D], (chwv, 16))
        self.dqplan = self.dequant_plan()
        self.dqi = 0
        for b in range(c.B):
            self._p1_batch(b)
        self.finish_max([(0, self.pmax_col, 0)], "1")
        self.ts("vector", self.s11c(1), self.s11c(0), 1.0 / QMAX, None, OP.mult)
        self.recip(self.s11c(2), self.s11c(0))
        self.ts("vector", self.s11c(2), self.s11c(2), QMAX, None, OP.mult)
        self.ts("vector", self.s11c(3), self.s11c(1), -OFFS, None, OP.mult)
        cvr = self.chp("r1d")
        self.seal("vector", cvr)
        self.wait("tensor", cvr)
        self.wait("vector", cvr)
        self.bcast_to_pp(2, 0)
        self.bcast_to_pp(1, 1)
        self.bcast_to_pp(3, 2)

    def _p1_batch(self, b):
        self.phase("p1qk")
        c = self.cfg
        T, CC, D = c.T, c.CC, c.D
        xr = self.dram["xPM"][:, :].rearrange("p (cc r) -> p cc r", cc=CC)
        chbat = self.chp("p1bat")
        chxs = self.dch("gpsimd", "xs")
        vps = self.chp("vps")
        vaugc = self.chp("vaugc")
        if b > 0:
            self.wait("gpsimd", chbat, at=b)
        self.wait("tensor", self.dch("gpsimd", "wvld"))

        self._xqk_ld = 0
        chqkp = self.chp("qkp")
        for tt in range(T // 128):
            xsl = tt % 2
            xreg = self.xstr.sub(xsl * CC * 512, xsl * CC * 512 + CC * 128)
            if tt >= 2:
                self.wait("gpsimd", vps, at=max(0, vps.n - 2))
            elif b > 0:
                self.wait("gpsimd", chqkp, at=chqkp.n)
            xn = self.dma("gpsimd",
                          xreg().rearrange("p (cc t) -> p cc t", cc=CC),
                          xr[:, :, b * T + tt * 128: b * T + (tt + 1) * 128],
                          (chxs, 16))
            self.wtok("tensor", xn)
            for vc in range(2):
                boff = (((tt % 2) * 2) + vc) * 512
                if vaugc.n >= 4:
                    self.wait("tensor", vaugc, at=vaugc.n - 3)
                for cci in range(CC):
                    self.mm(self.psum[:, boff:boff + 512],
                            xreg(cci * 128, (cci + 1) * 128),
                            self.wv_sb(cci * D + vc * 512, cci * D + (vc + 1) * 512),
                            cci == 0, cci == CC - 1,
                            inc=(vps if cci == CC - 1 else None))
                self.wait("vector", vps)
                self.cp("vector",
                        self.vaug(tt * 1024 + vc * 512, tt * 1024 + (vc + 1) * 512),
                        self.psum[:, boff:boff + 512], inc=vaugc)

        chqkw = self.dch("gpsimd", "qkw")
        chqke = self.chp("qke")
        wqkv = self.dram["wqkvPM"][:, :].rearrange("p (cc e) -> p cc e", cc=CC)
        for hp in range(c.H // 2):
            wbuf = hp % 2
            wdst = self.wqk
            wof = wbuf * 2 * CC * 128
            if hp >= 2:
                self.wait("gpsimd", chqke, at=chqke.n - 2 * (T // 512))
            self.dma("gpsimd",
                     wdst(wof, wof + CC * 128).rearrange("p (cc e) -> p cc e", cc=CC),
                     wqkv[:, :, hp * 128:(hp + 1) * 128], (chqkw, 16))
            self.dma("gpsimd",
                     wdst(wof + CC * 128, wof + 2 * CC * 128).rearrange("p (cc e) -> p cc e", cc=CC),
                     wqkv[:, :, D + hp * 128: D + (hp + 1) * 128], (chqkw, 16))
            self.wait("tensor", chqkw)
            self.wait("tensor", self.chp("avm"), at=self.chp("avm").n)
            nqt = T // 512
            for t4 in range(nqt):
                nld = self._xqk_ld
                self._xqk_ld = nld + 1
                xsl = nld % 2
                xreg = self.xstr.sub(xsl * CC * 512, (xsl + 1) * CC * 512)
                if nld < 2:
                    self.wait("gpsimd", vps, at=vps.n)
                else:
                    self.wait("gpsimd", chqkp, at=max(0, chqkp.n - 2))
                xn = self.dma("gpsimd",
                              xreg().rearrange("p (cc t) -> p cc t", cc=CC),
                              xr[:, :, b * T + t4 * 512: b * T + (t4 + 1) * 512],
                              (chxs, 16))
                self.wtok("tensor", xn)
                for qki in range(2):
                    idx = t4 * 2 + qki
                    boff = (6 + idx % 2) * 512
                    if chqke.n >= 2:
                        self.wait("tensor", chqke, at=chqke.n - 1)
                    for cci in range(CC):
                        self.mm(self.psum[:, boff:boff + 512],
                                wdst(wof + qki * CC * 128 + cci * 128,
                                     wof + qki * CC * 128 + (cci + 1) * 128),
                                xreg(cci * 512, (cci + 1) * 512),
                                cci == 0, cci == CC - 1,
                                inc=(chqkp if cci == CC - 1 else None))
                    self.wait("scalar", chqkp)
                    self.act(self.qk(qki * T + t4 * 512, qki * T + (t4 + 1) * 512),
                             self.psum[:, boff:boff + 512], AF.Copy, inc=chqke)
            for hh in range(2):
                self._p1_attn_head(b, hp * 2 + hh)
        self._p1_post(b)
        self.seal("tensor", chbat)

    def _p1_attn_head(self, b, h):
        self.phase("p1at")
        c = self.cfg
        T = c.T
        p0 = (h % 2) * 64
        qT = self.qk(0, T, p0, p0 + 64)
        kT = self.qk(T, 2 * T, p0, p0 + 64)
        chsc = self.chp("sc")
        chmk = self.chp("mk")
        chex = self.chp("exp")
        chpp = self.chp("persp")
        chpe = self.chp("perse")
        chavp = self.chp("avp")
        chrd = self.chp("rd")
        chavm = self.chp("avm")
        chavt = self.chp("avt")
        headidx = b * c.H + h
        self.wait("tensor", self.chp("qke"))
        self.wait("tensor", self.chp("vaugc"))
        if headidx > 0:
            self.wait("scalar", chavt, at=headidx)
        identb = self.sb["identb"]
        maskb = self.sb["maskb"]
        for j in range(c.NKT):
            k0 = j * 128
            wj = min(c.SW, T - k0)
            off = (j % 2) * 1024
            if chex.n >= 2:
                self.wait("tensor", chex, at=chex.n - 1)
            w1 = min(wj, 512)
            self.mm(self.psum[:, off:off + w1], identb[:, :], maskb[:, 0:w1],
                    True, False)
            if wj > 512:
                self.mm(self.psum[:, off + 512:off + wj], identb[:, :],
                        maskb[:, 512:wj], True, False)
            self.mm(self.psum[:, off:off + w1], kT[:, k0:k0 + 128], qT[:, k0:k0 + w1],
                    False, True, inc=(chsc if wj <= 512 else None), tp=(p0, 0))
            if wj > 512:
                self.mm(self.psum[:, off + 512:off + wj], kT[:, k0:k0 + 128],
                        qT[:, k0 + 512:k0 + wj], False, True, inc=chsc, tp=(p0, 0))
            if j == 0:
                self.wait("vector", chsc)
                self.memset("vector", self.psum[0:c.NP, off:off + wj], NEG,
                            inc=chmk)
                self.wait("scalar", chmk)
            else:
                self.wait("scalar", chsc)
            self.act(self.strips(j * c.SW, j * c.SW + wj),
                     self.psum[:, off:off + wj], AF.Exp, scale=0.125, inc=chex)
        nq = T // 512
        for g in range(nq):
            boff = (4 + g % 2) * 512
            if g >= 2:
                self.wait("tensor", chpe, at=chpe.n - 1)
            elif chavm.n > 0:
                self.wait("tensor", chavm, at=chavm.n)
            self.mm(self.psum[0:c.NP, boff:boff + 512], kT[:, 0:c.NP],
                    qT[:, g * 512:(g + 1) * 512], True, True, inc=chpp, tp=(p0, 0))
            self.wait("scalar", chpp)
            self.act(self.sb["pers"][0:c.NP, g * 512:(g + 1) * 512],
                     self.psum[0:c.NP, boff:boff + 512], AF.Exp, scale=0.125, inc=chpe)
        a0 = p0
        d0 = 64 - a0
        for g in range(nq):
            q0 = g * 512
            boff = (4 + g % 4) * 512
            if g < 2:
                self.wait("tensor", chpe)
            if chavm.n >= 4:
                self.wait("tensor", chavm, at=chavm.n - 3)
            rhsp = self.sb["pers"][0:c.NP, q0:q0 + 512]
            self.mm(self.psum[a0:a0 + 64, boff:boff + 512],
                    self.vaug(h * 64, (h + 1) * 64, 0, c.NP), rhsp,
                    True, False, tp=(0, a0))
            self.mm(self.psum[d0:d0 + 64, boff:boff + 512],
                    self.sb["onesb64"][0:c.NP, :], rhsp, True, False, tp=(0, d0))
            lo_j = max(0, (q0 - c.WIN) // 128)
            hi_j = min(c.NKT, (q0 + 512) // 128)
            js = list(range(lo_j, hi_j))
            for idx, jj in enumerate(js):
                k0 = jj * 128
                wj = min(c.SW, T - k0)
                lo = max(k0, q0)
                hi = min(k0 + wj, q0 + 512)
                if lo >= hi:
                    continue
                last = idx == len(js) - 1
                rhs = self.strips(jj * c.SW + (lo - k0), jj * c.SW + (hi - k0))
                self.mm(self.psum[a0:a0 + 64, boff + (lo - q0): boff + (hi - q0)],
                        self.vaug(jj * 1024 + h * 64, jj * 1024 + (h + 1) * 64),
                        rhs, False, last, tp=(0, a0))
                self.mm(self.psum[d0:d0 + 64, boff + (lo - q0): boff + (hi - q0)],
                        self.sb["onesb64"][:, :], rhs, False, last, tp=(0, d0),
                        inc=(chavp if last else None))
            self.wait("vector", chavp)
            rr = self.rdenA if g % 2 == 0 else self.rdenB
            self.recip(rr(0, 512, d0, d0 + 64),
                       self.psum[d0:d0 + 64, boff:boff + 512], inc=chrd)
            self.wait("scalar", chrd)
            qn = self.dma("scalar", rr(0, 512, a0, a0 + 64), rr(0, 512, d0, d0 + 64))
            self.wtok("vector", qn)
            slot = g % 2
            stg = self.avstg(slot * 512, (slot + 1) * 512, a0, a0 + 64)
            key = "avslot%d" % slot
            prev = getattr(self, key, None)
            if prev is not None:
                self.wtok("vector", prev)
            self.tt("vector", stg, self.psum[a0:a0 + 64, boff:boff + 512],
                    rr(0, 512, a0, a0 + 64), OP.mult, inc=chavm)
            self.wait("scalar", chavm, at=chavm.n)
            qn2 = self.dma("scalar",
                           self.dram["avscr"][a0:a0 + 64,
                                              (h // 2) * c.R + b * T + q0:
                                              (h // 2) * c.R + b * T + q0 + 512],
                           stg)
            setattr(self, key, qn2)
        self.seal("tensor", chavt)

    def _p1_post(self, b):
        self.phase("p1po")
        c = self.cfg
        T, CC = c.T, c.CC
        chops = self.chp("p1c_mm")
        chsadd = self.chp("p1c_add")
        chcast = self.chp("p1c_cast")
        chstmm = self.chp("p1c_stmm")
        chbc = self.chp("p1c_bc")
        chx1 = self.chp("p1c_x1")
        avld = self.dch("sync", "p1c_avld")
        wout = self.dram["woutPM"][:, :].rearrange("p (cc e) -> p cc e", cc=CC)
        self.wait_queue("sync", "scalar")
        for t4 in range(T // 512):
            r0 = b * T + t4 * 512
            if t4 > 0:
                self.wait("sync", chops, at=chops.n)
            for cci in range(CC):
                self.dma("sync", self.avin(cci * 512, (cci + 1) * 512),
                         self.dram["avscr"][:, cci * c.R + r0: cci * c.R + r0 + 512],
                         (avld, 16))
            for oc in range(CC):
                so = (oc % 2) * CC * 128
                if chops.n >= 2:
                    self.wait("gpsimd", chops, at=chops.n - 1)
                wn = self.dma("gpsimd",
                              self.wslot(so, so + CC * 128).rearrange("p (cc e) -> p cc e", cc=CC),
                              wout[:, :, oc * 128:(oc + 1) * 128])
                boff = (oc % 4) * 512
                self.wtok("tensor", wn)
                if oc == 0:
                    self.wait("tensor", avld)
                if oc >= 4:
                    self.wait("tensor", chsadd, at=chsadd.n - 3)
                for cci in range(CC):
                    self.mm(self.psum[:, boff:boff + 512],
                            self.wslot(so + cci * 128, so + (cci + 1) * 128),
                            self.avin(cci * 512, (cci + 1) * 512),
                            cci == 0, cci == CC - 1,
                            inc=(chops if cci == CC - 1 else None))
                xslot = self.xres((oc % 2) * 512, ((oc % 2) + 1) * 512)
                if chsadd.n >= 2:
                    self.wait("sync", chsadd, at=chsadd.n - 1)
                xn = self.dma("sync", xslot,
                              self.dram["xPM"][:, oc * c.R + r0: oc * c.R + r0 + 512])
                self.wait("vector", chops)
                self.wtok("vector", xn)
                sC = self.stile(oc * 512, (oc + 1) * 512)
                self.tt("vector", sC, self.psum[:, boff:boff + 512], xslot,
                        OP.add, inc=chsadd)
                self.wait("scalar", chsadd)
                if oc >= 2:
                    self.wait("scalar", chstmm, at=chstmm.n - 2)
                cb = (oc % 2) * 1024
                sbf = self.cast2(cb, cb + 512)
                s2bf = self.cast2(cb + 512, cb + 1024)
                self.act(sbf, sC, AF.Copy, inc=chcast)
                self.act(s2bf, sC, AF.Square, inc=chcast)
                self.wait("tensor", chcast)
                self.mm(self.bank1(4), self.sb["onescolb"][:, :], sbf,
                        oc == 0, oc == CC - 1, inc=(chstmm, 1))
                self.mm(self.bank1(5), self.sb["onescolb"][:, :], s2bf,
                        oc == 0, oc == CC - 1, inc=(chstmm, 1))
            self.wait("vector", chstmm)
            rows = self.sb["rows"]
            mneg = rows[0:1, 0:512]
            rstd = rows[0:1, 512:1024]
            self.ln_rows(self.bank1(4), self.bank1(5), 512, mneg, rstd,
                         1.0 / c.D)
            self.mm(self.bank(6), self.sb["onesrow"][:, :], mneg, True, True)
            self.mm(self.bank(7), self.sb["onesrow"][:, :], rstd, True, True, inc=chbc)
            self.wait("vector", chbc)
            for oc in range(CC):
                sC = self.stile(oc * 512, (oc + 1) * 512)
                tslot = self.f2((oc % 2) * 512, ((oc % 2) + 1) * 512)
                key = "x1slot%d" % (oc % 2)
                prev = getattr(self, key, None)
                if prev is not None:
                    self.wtok("vector", prev)
                self.tt("vector", tslot, sC, self.bank(6), OP.add)
                self.tt("vector", tslot, tslot, self.bank(7), OP.mult)
                self.track_max(tslot)
                self.seal("vector", chx1)
                self.wait("sync", chx1, at=chx1.n)
                xn = self.dma("sync",
                              self.dram["resid1"][:, oc * c.R + r0: oc * c.R + r0 + 512],
                              tslot)
                setattr(self, key, xn)
            self.wait("tensor", chx1)



NT = 1024


class MKP2(MKP1):
    def p2(self):
        self.phase("xq")
        c = self.cfg
        CC, R, IC = c.CC, c.R, c.IC
        self.barrier("p2")
        self.AB.reset(); self.AF.reset(); self.AI.reset()
        xq = self.AB.alloc(CC * R)
        wtr = self.AB.alloc(4 * CC * 128)
        spst = self.AB.alloc(8 * 512)
        self.g_bf = self.AB.alloc(2 * NT)
        self.u_bf = self.AB.alloc(2 * NT)
        self.q_bf = self.AB.alloc(2 * NT)
        ftl = self.AF.alloc(4 * 512)
        self.sg_f = self.AF.alloc(2 * NT)
        self.p_f = self.AF.alloc(2 * NT)
        self.z_f = self.AF.alloc(2 * NT)
        self.gc_f = self.AF.alloc(2 * NT)
        self.c_f = self.AF.alloc(2 * NT)
        self.z4_f = ftl
        iA = self.AI.alloc(2048)
        iB = self.AI.alloc(2048)
        itl = iA.sub(0, 1024)
        self.dq_wnat = self.z_f
        self.dq_wqf = self.gc_f
        self.dq_wsc = self.c_f.sub(0, 128)
        self.dq_wqi = iB
        self.dq_wqb = self.q_bf.sub(0, 1024)

        self.wait("scalar", self.ch("ppv"))
        self.wait("vector", self.ch("ppv"))
        chld = self.dch("sync", "f1_xld")
        chm = self.chp("f1_m")
        chq = self.chp("f1_q")
        for cci in range(CC):
            for tq in range(R // 512):
                sl = (cci * (R // 512) + tq) % 2
                fo = ftl(sl * 512, (sl + 1) * 512)
                zo = ftl((2 + sl) * 512, (3 + sl) * 512)
                io = itl(sl * 512, (sl + 1) * 512)
                if chm.n >= 2:
                    self.wait("sync", chm, at=chm.n - 1)
                self.dma("sync", fo,
                         self.dram["resid1"][:, cci * R + tq * 512:
                                             cci * R + tq * 512 + 512], (chld, 16))
                self.wait("scalar", chld)
                if chq.n >= 2:
                    self.wait("scalar", chq, at=chq.n - 1)
                self.act(zo, fo, AF.Copy, scale=self.pp_col(0), bias=32768.5)
                self.act(io, fo, AF.Copy, scale=self.pp_col(0), bias=32768.5)
                self.seal("scalar", chm)
                self.wait("vector", chm)
                self.tt("vector", zo, io, zo, OP.is_gt)
                self.stt("vector", zo, io, -OFFS, zo, OP.add, OP.subtract)
                self.ts("vector", xq(cci * R + tq * 512, cci * R + tq * 512 + 512),
                        zo, self.pp_col(1), None, OP.mult, inc=chq)

        for _ in range(4):
            sq, dn, of = self.dqplan[self.dqi]
            self.dequant_piece(sq, dn, of, self.dqi)
            self.dqi += 1

        self.phase("f1")
        self.wait("tensor", chq)
        chw = self.dch("scalar", "f1_w")
        chmmm = self.chp("f1_mm")
        chsp = self.chp("f1_sp")
        chmx = self.chp("f1_mx")
        cols = {"A": [], "B": []}
        spslots = {}
        for ec in range(2 * IC):
            which = "A" if ec % 2 == 0 else "B"
            ocl = ec // 2
            wsrc = self.dram["wdq%s_n" % which]
            sl = ec % 4
            wofs = sl * CC * 128
            if ec >= 4:
                self.wait("scalar", chmmm, at=chmmm.n - 3 * (R // 512))
            self.wtok("scalar", self.dq_tok[ec])
            wtoks = []
            for cci in range(CC):
                wtoks.append(self.dma(
                    "scalar", wtr(wofs + cci * 128, wofs + (cci + 1) * 128),
                    wsrc[:, ocl * c.D + cci * 128: ocl * c.D + (cci + 1) * 128],
                    transpose=True))
            for tk in wtoks:
                self.wtok("tensor", tk)
            for tq in range(R // 512):
                gidx = ec * (R // 512) + tq
                boff = (gidx % 4) * 512
                if chmx.n >= 4:
                    self.wait("tensor", chmx, at=chmx.n - 3)
                for cci in range(CC):
                    self.mm(self.psum[:, boff:boff + 512],
                            wtr(wofs + cci * 128, wofs + (cci + 1) * 128),
                            xq(cci * R + tq * 512, cci * R + (tq + 1) * 512),
                            cci == 0, cci == CC - 1,
                            inc=(chmmm if cci == CC - 1 else None))
                self.wait("scalar", chmmm)
                ssl = gidx % 8
                stg = spst(ssl * 512, (ssl + 1) * 512)
                prev = spslots.get(ssl)
                if prev is not None:
                    self.wtok("scalar", prev)
                self.act(stg, self.psum[:, boff:boff + 512], AF.Copy, inc=chsp)
                self.wait("vector", chsp)
                cols[which].append(self.track_max(self.psum[:, boff:boff + 512]))
                self.seal("vector", chmx)
                self.wait("gpsimd", chsp, at=chsp.n)
                spslots[ssl] = self.dma(
                    "gpsimd",
                    self.dram["spill%s" % which][:, ocl * R + tq * 512:
                                                 ocl * R + tq * 512 + 512], stg)
            while self.dqi < min(len(self.dqplan), (ec + 1) * 3 // 2 + 4):
                sq, dn, of = self.dqplan[self.dqi]
                self.dequant_piece(sq, dn, of, self.dqi)
                self.dqi += 1
        ua, ub = cols["A"][0], cols["A"][-1] + 1
        ga, gb = cols["B"][0], cols["B"][-1] + 1
        self.finish_max([(ua, ub, 4), (ga, gb, 5)], "2")
        self.ts("vector", self.s11c(6), self.s11c(4), 1.0 / QMAX, None, OP.mult)
        self.ts("vector", self.s11c(7), self.s11c(5), 1.0 / QMAX, None, OP.mult)
        self.recip(self.s11c(8), self.s11c(4))
        self.recip(self.s11c(9), self.s11c(5))
        self.ts("vector", self.s11c(8), self.s11c(8), QMAX, None, OP.mult)
        self.ts("vector", self.s11c(9), self.s11c(9), QMAX, None, OP.mult)
        self.ts("vector", self.s11c(10), self.s11c(7), -OFFS, None, OP.mult)
        cvr = self.chp("r2d")
        self.seal("vector", cvr)
        self.wait("tensor", cvr)
        self.bcast_to_pp(8, 3)
        self.bcast_to_pp(9, 4)
        self.bcast_to_pp(7, 5)
        self.bcast_to_pp(10, 6)

        self.phase("f2")
        self.mg_i = iA
        self.S_i = iB
        self.wait("scalar", chq)
        chgl = self.dch("sync", "f2_gl")
        chf2s = self.chp("f2_s")
        chf2sB = self.chp("f2_sB")
        chf2v = self.chp("f2_v")
        chf2vB = self.chp("f2_vB")
        g2lo = self.pmax_col
        self.wait_queue("sync", "gpsimd")
        self.wait("scalar", self.ch("ppv"))
        self.wait("vector", self.ch("ppv"))
        chf2vA = self.chp("f2_vA")
        chf2gA = self.chp("f2_gA")
        chf2g = self.chp("f2_g")
        ntile = (IC * R) // NT
        for i in range(ntile):
            sl = i % 2
            gb = self.g_bf(sl * NT, (sl + 1) * NT)
            mg = self.mg_i(sl * NT, (sl + 1) * NT)
            Si = self.S_i(sl * NT, (sl + 1) * NT)
            sg = self.sg_f(sl * NT, (sl + 1) * NT)
            pf = self.p_f(sl * NT, (sl + 1) * NT)
            zf = self.z_f(sl * NT, (sl + 1) * NT)
            gf = self.gc_f(sl * NT, (sl + 1) * NT)
            cf = self.c_f(sl * NT, (sl + 1) * NT)
            if i >= 2:
                self.wait("sync", chf2s, at=chf2s.n - 1)
            self.dma("sync", gb, self.dram["spillB"][:, i * NT:(i + 1) * NT],
                     (chgl, 16))
            self.wait("scalar", chgl)
            if i >= 2:
                self.wait("scalar", chf2v, at=chf2v.n - 1)
            self.act(zf, gb, AF.Copy, scale=self.pp_col(4), bias=32768.5)
            self.act(mg, gb, AF.Copy, scale=self.pp_col(4), bias=32768.5)
            self.seal("scalar", chf2s)
            self.wait("vector", chf2s)
            self.tt("vector", gf, mg, zf, OP.is_gt)
            self.stt("vector", cf, mg, -OFFS, gf, OP.add, OP.subtract,
                     inc=chf2vA)
            self.wait("scalar", chf2vA)
            self.act(sg, cf, AF.Sigmoid, scale=self.pp_col(5), bias=0.0)
            self.act(zf, sg, AF.Copy, scale=65536.0, bias=0.5)
            self.act(Si, sg, AF.Copy, scale=65536.0, bias=0.5)
            self.seal("scalar", chf2sB)
            self.wait("vector", chf2sB)
            self.tt("vector", gf, Si, zf, OP.is_gt)
            self.stt("vector", gf, Si, -0.0, gf, OP.add, OP.subtract)
            self.ts("vector", gf, gf, 65535.0, None, OP.min)
            prevsp = getattr(self, "_pfsp%d" % sl, None)
            if prevsp is not None:
                self.wtok("vector", prevsp)
            self.tt("vector", pf, cf, gf, OP.mult)
            self.track_max(pf, inc=chf2v)
            self.wait("gpsimd", chf2v, at=chf2v.n)
            setattr(self, "_pfsp%d" % sl,
                    self.dma("gpsimd", self.dram["spillD"][:, i * NT:(i + 1) * NT],
                             pf))
        self.finish_max([(g2lo, self.pmax_col, 11)], "3")
        self.tt("vector", self.s11c(12), self.s11c(11), self.s11c(7), OP.mult)
        self.ts("vector", self.s11c(12), self.s11c(12), 1.0 / (65536.0 * QMAX), None,
                OP.mult)
        self.recip(self.s11c(14), self.s11c(12))
        self.tt("vector", self.s11c(13), self.s11c(12), self.s11c(6), OP.mult)
        self.tt("vector", self.s11c(15), self.s11c(7), self.s11c(14), OP.mult)
        self.ts("vector", self.s11c(15), self.s11c(15), 1.0 / 65536.0, None, OP.mult)
        cvr = self.chp("r3d")
        self.seal("vector", cvr)
        self.wait("tensor", cvr)
        self.bcast_to_pp(15, 7)

        self.phase("f3")
        chul = self.dch("sync", "f3_ul")
        self.wait("scalar", self.ch("ppv"))
        self.wait("vector", self.ch("ppv"))
        self.wait("sync", self.chp("f2_s"))
        self.wait("sync", self.chp("f2_v"))
        chf3s = self.chp("f3_s")
        chf3vA = self.chp("f3_vA")
        chf3g = self.chp("f3_g")
        chf3v = self.chp("f3_v")
        chpl = self.dch("sync", "f3_pl")
        qlo = self.pmax_col
        self.wait_queue("sync", "gpsimd")
        self.wait("vector", self.dch("sync", "dq_st"))
        for i in range(ntile):
            sl = i % 2
            ub = self.u_bf(sl * NT, (sl + 1) * NT)
            qb = self.q_bf(sl * NT, (sl + 1) * NT)
            mg = self.mg_i(sl * NT, (sl + 1) * NT)
            Si = self.S_i(sl * NT, (sl + 1) * NT)
            pf = self.p_f(sl * NT, (sl + 1) * NT)
            zf = self.z_f(sl * NT, (sl + 1) * NT)
            gf = self.gc_f(sl * NT, (sl + 1) * NT)
            z4 = self.z4_f(sl * NT, (sl + 1) * NT)
            if i >= 2:
                self.wait("sync", chf3s, at=chf3s.n - 1)
            self.dma("sync", pf, self.dram["spillD"][:, i * NT:(i + 1) * NT],
                     (chpl, 16))
            self.dma("sync", ub, self.dram["spillA"][:, i * NT:(i + 1) * NT],
                     (chul, 16))
            self.wait("scalar", chpl)
            self.wait("scalar", chul)
            if i >= 2:
                self.wait("scalar", chf3v, at=chf3v.n - 1)
            self.act(zf, pf, AF.Copy, scale=self.pp_col(7), bias=32768.5)
            self.act(mg, pf, AF.Copy, scale=self.pp_col(7), bias=32768.5)
            self.act(z4, ub, AF.Copy, scale=self.pp_col(3), bias=32768.5)
            self.act(Si, ub, AF.Copy, scale=self.pp_col(3), bias=32768.5)
            self.seal("scalar", chf3s)
            self.wait("vector", chf3s)
            self.tt("vector", gf, mg, zf, OP.is_gt)
            self.stt("vector", zf, mg, -OFFS, gf, OP.add, OP.subtract)
            self.tt("vector", gf, Si, z4, OP.is_gt)
            self.stt("vector", z4, Si, -OFFS, gf, OP.add, OP.subtract)
            qprev = getattr(self, "_qbslot%d" % sl, None)
            if qprev is not None:
                self.wtok("vector", qprev)
            self.tt("vector", qb, zf, z4, OP.mult)
            self.track_max(qb, inc=chf3v)
            self.wait("gpsimd", chf3v, at=chf3v.n)
            setattr(self, "_qbslot%d" % sl,
                    self.dma("gpsimd", self.dram["spillC"][:, i * NT:(i + 1) * NT],
                             qb))
        self.finish_max([(qlo, self.pmax_col, 16)], "4")
        self.tt("vector", self.s11c(17), self.s11c(16), self.s11c(13), OP.mult)
        self.ts("vector", self.s11c(17), self.s11c(17), 1.0 / QMAX, None, OP.mult)
        self.recip(self.s11c(18), self.s11c(17))
        self.tt("vector", self.s11c(18), self.s11c(18), self.s11c(13), OP.mult)
        self.ts("vector", self.s11c(19), self.s11c(17), -OFFS, None, OP.mult)
        cvr = self.chp("r4d")
        self.seal("vector", cvr)
        self.wait("tensor", cvr)
        self.bcast_to_pp(18, 8)
        self.bcast_to_pp(17, 9)
        self.bcast_to_pp(19, 10)
        self._f4()

    def _f4(self):
        self.phase("f4")
        c = self.cfg
        CC, R, IC = c.CC, c.R, c.IC
        self.barrier("f4")
        self.AB.reset(); self.AF.reset(); self.AI.reset()
        self.f4_wdn = self.AB.alloc(IC * CC * 128)
        self.f4_obf = self.AB.alloc(2 * IC * 512)
        self.f4_cst = self.AB.alloc(CC * 1024)
        self.f4_qld = self.AB.alloc(2 * 512)
        self.f4_stile = self.AF.alloc(CC * 512)
        self.f4_ftl = self.AF.alloc(4 * 512)
        self.f4_zf4 = self.AF.alloc(2 * 512)
        self.f4_itl = self.AI.alloc(2 * 512)
        self.f4_rec = {}
        wdn = self.f4_wdn
        for oc in range(CC):
            for icc in range(IC):
                self.dma("scalar",
                         wdn(oc * IC * 128 + icc * 128, oc * IC * 128 + (icc + 1) * 128),
                         self.dram["wdqC_n"][:, oc * c.I_FFN + icc * 128:
                                             oc * c.I_FFN + (icc + 1) * 128],
                         transpose=True)
        self.wait("scalar", self.ch("ppv"))
        self.wait("vector", self.ch("ppv"))
        self.wait_queue("tensor", "scalar")
        self._f4_produce(0)
        for tq in range(R // 512):
            if tq + 1 < R // 512:
                self._f4_produce(tq + 1)
            self._f4_mm(tq)

    def _f4_produce(self, tq):
        c = self.cfg
        CC, R, IC = c.CC, c.R, c.IC
        rec = self.f4_rec
        r0 = tq * 512
        sl4 = tq % 2
        obf = self.f4_obf.sub(sl4 * IC * 512, (sl4 + 1) * IC * 512)
        chol = self.dch("sync", "f4_old")
        chom = self.chp("f4_om")
        choq = self.chp("f4_oq")
        if ("mm", tq - 2) in rec:
            self.wait("vector", self.chp("f4_mm"), at=rec[("mm", tq - 2)])
        for icc in range(IC):
            sl = icc % 2
            qsl = self.f4_qld(sl * 512, (sl + 1) * 512)
            io = self.f4_itl(sl * 512, (sl + 1) * 512)
            zo = self.f4_zf4(sl * 512, (sl + 1) * 512)
            if chom.n >= 2:
                self.wait("sync", chom, at=chom.n - 1)
            self.dma("sync", qsl,
                     self.dram["spillC"][:, icc * R + r0: icc * R + r0 + 512],
                     (chol, 16))
            self.wait("scalar", chol)
            if choq.n >= 2:
                self.wait("scalar", choq, at=choq.n - 1)
            self.act(zo, qsl, AF.Copy, scale=self.pp_col(8), bias=32768.5)
            self.act(io, qsl, AF.Copy, scale=self.pp_col(8), bias=32768.5)
            self.seal("scalar", chom)
            self.wait("vector", chom)
            self.tt("vector", zo, io, zo, OP.is_gt)
            self.stt("vector", zo, io, -OFFS, zo, OP.add, OP.subtract)
            self.ts("vector", obf(icc * 512, (icc + 1) * 512), zo,
                    self.pp_col(9), None, OP.mult, inc=choq)
        rec[("oq", tq)] = choq.n

    def _f4_mm(self, tq):
        c = self.cfg
        CC, R, IC = c.CC, c.R, c.IC
        rec = self.f4_rec
        r0 = tq * 512
        sl4 = tq % 2
        obf = self.f4_obf.sub(sl4 * IC * 512, (sl4 + 1) * IC * 512)
        wdn = self.f4_wdn
        cst = self.f4_cst
        stile = self.f4_stile
        ftl = self.f4_ftl
        chmm = self.chp("f4_mm")
        chadd = self.chp("f4_add")
        chcast = self.chp("f4_cast")
        chstmm = self.chp("f4_stmm")
        chbc = self.chp("f4_bc")
        chx2 = self.chp("f4_x2")
        chrow4 = self.chp("f4_row")
        chxl = self.dch("sync", "f4_xl")
        if not hasattr(self, "_f4x2slots"):
            self._f4x2slots = {}
        x2slots = self._f4x2slots
        self.wait("tensor", self.chp("f4_oq"), at=rec[("oq", tq)])
        if ("smm", tq - 1) in rec:
            self.wait("scalar", chstmm, at=rec[("smm", tq - 1)])
        for oc in range(CC):
            boff = (oc % 4) * 512
            if oc >= 4 or tq > 0:
                self.wait("tensor", chadd, at=max(0, chadd.n - 3))
            for icc in range(IC):
                self.mm(self.psum[:, boff:boff + 512],
                        wdn(oc * IC * 128 + icc * 128,
                            oc * IC * 128 + (icc + 1) * 128),
                        obf(icc * 512, (icc + 1) * 512),
                        icc == 0, icc == IC - 1,
                        inc=(chmm if icc == IC - 1 else None))
            xslot = ftl((2 + oc % 2) * 512, (3 + oc % 2) * 512)
            if chadd.n >= 2:
                self.wait("sync", chadd, at=chadd.n - 1)
            xn = self.dma("sync", xslot,
                          self.dram["resid1"][:, oc * R + r0: oc * R + r0 + 512],
                          (chxl, 16))
            self.wait("vector", chmm)
            self.wtok("vector", xn)
            sC = stile(oc * 512, (oc + 1) * 512)
            self.tt("vector", sC, self.psum[:, boff:boff + 512], xslot, OP.add,
                    inc=chadd)
            self.wait("scalar", chadd)
            cpr = oc * 1024
            self.act(cst(cpr, cpr + 512), sC, AF.Copy, inc=chcast)
            self.act(cst(cpr + 512, cpr + 1024), sC, AF.Square, inc=chcast)
        rec[("mm", tq)] = chmm.n
        self.wait("tensor", chcast, at=chcast.n)
        if ("row", tq - 1) in rec:
            self.wait("tensor", chrow4, at=rec[("row", tq - 1)])
        for oc in range(CC):
            cpr = oc * 1024
            self.mm(self.bank1(4), self.sb["onescolb"][:, :], cst(cpr, cpr + 512),
                    oc == 0, oc == CC - 1, inc=(chstmm, 1))
            self.mm(self.bank1(5), self.sb["onescolb"][:, :],
                    cst(cpr + 512, cpr + 1024), oc == 0, oc == CC - 1,
                    inc=(chstmm, 1))
        rec[("smm", tq)] = chstmm.n
        self.wait("vector", chstmm)
        if ("bc", tq - 1) in rec:
            self.wait("vector", chbc, at=rec[("bc", tq - 1)])
        rows = self.sb["rows"]
        mneg = rows[0:1, 0:512]
        rstd = rows[0:1, 512:1024]
        self.ln_rows(self.bank1(4), self.bank1(5), 512, mneg, rstd,
                     1.0 / c.D)
        self.seal("vector", chrow4)
        rec[("row", tq)] = chrow4.n
        if ("x2", tq - 1) in rec:
            self.wait("tensor", chx2, at=rec[("x2", tq - 1)])
        self.wait("tensor", chrow4, at=chrow4.n)
        self.mm(self.bank(6), self.sb["onesrow"][:, :], mneg, True, True)
        self.mm(self.bank(7), self.sb["onesrow"][:, :], rstd, True, True, inc=chbc)
        rec[("bc", tq)] = chbc.n
        self.wait("vector", chbc)
        for oc in range(CC):
            sC = stile(oc * 512, (oc + 1) * 512)
            tslot = ftl((oc % 2) * 512, ((oc % 2) + 1) * 512)
            prev = x2slots.get(oc % 2)
            if prev is not None:
                self.wtok("vector", prev)
            self.tt("vector", tslot, sC, self.bank(6), OP.add)
            self.tt("vector", tslot, tslot, self.bank(7), OP.mult, inc=chx2)
            self.wait("sync", chx2, at=chx2.n)
            x2slots[oc % 2] = self.dma(
                "sync", self.dram["resid2"][:, oc * R + r0: oc * R + r0 + 512],
                tslot)
        rec[("x2", tq)] = chx2.n


class MKFull(MKP2):
    def p3(self):
        c = self.cfg
        self.barrier("p3")
        for b in range(c.B):
            self._p3_abc(b)
        self._p3_d()

    def _p3_abc(self, b):
        self.phase("p3a")
        c = self.cfg
        CC, R, T, D = c.CC, c.R, c.T, c.D
        t0 = b * T
        self.AB.reset(); self.AF.reset(); self.AI.reset()
        sr_bf = self.AB.alloc(CC * T)
        rwkv = self.AB.alloc(CC * T)
        mixk = self.AB.alloc(CC * 512)
        mixv = self.AB.alloc(CC * 512)
        mixr = self.AB.alloc(CC * 512)
        wsl = self.AB.alloc(2 * CC * 128)
        p3f = self.AF.alloc(12288)
        ftl = self.AF.alloc(1536)
        self.sr_bf, self.rwkv = sr_bf, rwkv

        if b > 0:
            self.wait_queue("sync", "sync")
        chxl = self.dch("sync", "p3_xl")
        chcast = self.chp("p3_cast")
        chstmm = self.chp("p3_stmm")
        chrow = self.chp("p3_row")
        for tq in range(T // 512):
            r0 = t0 + tq * 512
            if chstmm.n > 0:
                self.wait("scalar", chstmm, at=chstmm.n)
            for cci in range(CC):
                sl = cci % 2
                fo = p3f(sl * 512, (sl + 1) * 512)
                if chcast.n >= 4:
                    self.wait("sync", chcast, at=chcast.n - 2)
                self.dma("sync", fo,
                         self.dram["resid2"][:, cci * R + r0: cci * R + r0 + 512],
                         (chxl, 16))
                self.wait("scalar", chxl)
                self.act(mixk(cci * 512, (cci + 1) * 512), fo, AF.Copy, inc=chcast)
                self.act(mixv(cci * 512, (cci + 1) * 512), fo, AF.Square,
                         inc=chcast)
            self.wait("tensor", chcast, at=chcast.n)
            if chrow.n > 0:
                self.wait("tensor", chrow, at=chrow.n)
            for cci in range(CC):
                self.mm(self.bank1(4), self.sb["onescolb"][:, :],
                        mixk(cci * 512, (cci + 1) * 512),
                        cci == 0, cci == CC - 1, inc=(chstmm, 1))
                self.mm(self.bank1(5), self.sb["onescolb"][:, :],
                        mixv(cci * 512, (cci + 1) * 512),
                        cci == 0, cci == CC - 1, inc=(chstmm, 1))
            self.wait("vector", chstmm)
            rows = self.sb["rows"]
            mneg = rows[0:1, 0:512]
            rstd = rows[0:1, 512:1024]
            chsr1 = self.dch("sync", "p3_strow")
            if chsr1.n > 0:
                self.wait("vector", chsr1)
                self.wait("scalar", chsr1)
            self.ln_rows(self.bank1(4), self.bank1(5), 512, mneg, rstd,
                         1.0 / c.D)
            self.seal("vector", chrow)
            self.wait("sync", chrow, at=chrow.n)
            self.dma("sync", self.dram["stat1"][0:1, r0:r0 + 512], mneg, (chsr1, 16))
            self.dma("sync", self.dram["stat1"][1:2, r0:r0 + 512], rstd, (chsr1, 16))

        self.wait_queue("sync", "sync")
        chsr1b = self.dch("sync", "p3_strow")
        if chsr1b.n > 0:
            self.wait("sync", chsr1b)
        chst1 = self.dch("sync", "p3_st1")
        chbc = self.chp("p3_bc")
        chmix = self.chp("p3_mix")
        chmixg = self.chp("p3_mixg")
        chwld = self.chp("p3_wld")
        chpmm = self.chp("p3_pmm")
        chev = self.chp("p3_ev")
        chsig = self.chp("p3_sig")
        kvst = self.chp("p3_kvst")
        self.memset("vector", self.sb["tails"][:, :], 0.0)
        for tq in range(T // 512):
            r0 = t0 + tq * 512
            self.wait("sync", chbc, at=chbc.n)
            rows = self.sb["rows"]
            self.dma("sync", rows[0:1, 0:512], self.dram["stat1"][0:1, r0:r0 + 512],
                     (chst1, 16))
            self.dma("sync", rows[0:1, 512:1024], self.dram["stat1"][1:2, r0:r0 + 512],
                     (chst1, 16))
            self.wait("tensor", chst1)
            self.wait("tensor", chmix)
            self.wait("tensor", chmixg)
            self.mm(self.bank(6), self.sb["onesrow"][:, :], rows[0:1, 0:512],
                    True, True)
            self.mm(self.bank(7), self.sb["onesrow"][:, :], rows[0:1, 512:1024],
                    True, True, inc=chbc)
            self.wait("vector", chbc)
            self.wait("gpsimd", chbc)
            for cci in range(CC):
                sl = cci % 2
                fo = p3f.sub(sl * 512, (sl + 1) * 512)
                h = p3f.sub(1024 + sl * 512, 1024 + (sl + 1) * 512)
                hh = p3f.sub(2048 + sl * 512, 2048 + (sl + 1) * 512)
                dd = p3f.sub(3072 + sl * 512, 3072 + (sl + 1) * 512)
                if chmix.n >= 6:
                    self.wait("sync", chmix, at=chmix.n)
                    self.wait("sync", chmixg, at=chmixg.n)
                self.dma("sync", fo(),
                         self.dram["resid2"][:, cci * R + r0: cci * R + r0 + 512],
                         (chxl, 16))
                self.wait("vector", chxl)
                if chmix.n >= 6:
                    self.wait("vector", chmix, at=chmix.n)
                    self.wait("vector", chmixg, at=chmixg.n)
                self.tt("vector", h(), fo(), self.bank(6), OP.add)
                self.tt("vector", h(), h(), self.bank(7), OP.mult)
                self.ts("vector", h(), h(), self.pcol("ln1g", cci),
                        self.pcol("ln1b", cci), OP.mult, OP.add)
                self.cp("vector", hh(0, 1), self.sb["tails"][:, cci:cci + 1])
                self.cp("vector", hh(1, 512), h(0, 511))
                self.cp("vector", self.sb["tails"][:, cci:cci + 1], h(511, 512))
                self.tt("vector", dd(), h(), hh(), OP.subtract)
                chhd = self.chp("p3_hd")
                self.seal("vector", chhd)
                self.wait("gpsimd", chhd)
                self.stt("vector", mixk(cci * 512, (cci + 1) * 512), dd(),
                         self.pcol("tmk", cci), hh(), OP.mult, OP.add, inc=chmix)
                self.stt("vector", mixv(cci * 512, (cci + 1) * 512), dd(),
                         self.pcol("tmv", cci), hh(), OP.mult, OP.add, inc=chmix)
                self.stt("vector", mixr(cci * 512, (cci + 1) * 512), dd(),
                         self.pcol("tmr", cci), hh(), OP.mult, OP.add, inc=chmix)
                self.seal("vector", chmix)
                self.seal("gpsimd", chmixg)
            self.wait("tensor", chmix)
            self.wait("tensor", chmixg)
            tile_i = 0
            for pi, (wname, mix) in enumerate((("wkPM", mixk), ("wvPM", mixv),
                                               ("wrPM", mixr))):
                wsrc = self.dram[wname][:, :].rearrange("p (cc e) -> p cc e", cc=CC)
                for oc in range(CC):
                    sl = tile_i % 2
                    so = sl * CC * 128
                    if chpmm.n >= 2:
                        self.wait("gpsimd", chpmm, at=chpmm.n - 1)
                    wn = self.dma("gpsimd",
                                  wsl(so, so + CC * 128).rearrange("p (cc e) -> p cc e", cc=CC),
                                  wsrc[:, :, oc * 128:(oc + 1) * 128])
                    boff = (tile_i % 4) * 512
                    self.wtok("tensor", wn)
                    btok = getattr(self, "_p3bank", {}).get(boff)
                    if btok is not None:
                        self.wtok("tensor", btok)
                    for cci in range(CC):
                        self.mm(self.psum[:, boff:boff + 512],
                                wsl(so + cci * 128, so + (cci + 1) * 128),
                                mix(cci * 512, (cci + 1) * 512),
                                cci == 0, cci == CC - 1,
                                inc=(chpmm if cci == CC - 1 else None))
                    if not hasattr(self, "_p3bank"):
                        self._p3bank = {}
                    if pi < 2:
                        self.wait("vector", chpmm)
                        ksl = p3f(4096 + (tile_i % 2) * 512,
                                  4096 + (tile_i % 2 + 1) * 512)
                        prev = getattr(self, "p3kv%d" % (tile_i % 2), None)
                        if prev is not None:
                            self.wtok("vector", prev)
                        self.cp("vector", ksl, self.psum[:, boff:boff + 512],
                                inc=chev)
                        self._p3bank[boff] = (chev, chev.n)
                        self.wait("scalar", chev, at=chev.n)
                        dst = "kscr" if pi == 0 else "vscr"
                        qn = self.dma("scalar",
                                      self.dram[dst][:, oc * R + r0: oc * R + r0 + 512],
                                      ksl)
                        setattr(self, "p3kv%d" % (tile_i % 2), qn)
                    else:
                        self.wait("scalar", chpmm)
                        self.act(sr_bf(oc * T + tq * 512, oc * T + (tq + 1) * 512),
                                 self.psum[:, boff:boff + 512], AF.Sigmoid,
                                 inc=chsig)
                        self._p3bank[boff] = (chsig, chsig.n)
                    tile_i += 1


        self.phase("p3b")
        kv_done = self.chp("p3_kvdone")
        self.wait_queue("sync", "scalar")
        self.wait("sync", self.chp("p3_mix"))
        self.wait("sync", self.chp("p3_mixg"))
        self.wait("sync", self.chp("p3_pmm"))
        self.wait("vector", self.chp("p3c_mm"))
        chkl = self.dch("sync", "p3_kl")
        chek = self.chp("p3_ek")
        chsc2 = self.chp("p3_scan")
        chy = self.chp("p3_y")
        W = T
        for cci in range(CC):
            kc = p3f.sub(0, W)
            vc = p3f.sub(W, 2 * W)
            ek = p3f.sub(2 * W, 3 * W)
            eu = p3f.sub(3 * W, 4 * W)
            ekv = p3f.sub(4 * W, 5 * W)
            euv = p3f.sub(5 * W, 6 * W)
            if cci > 0:
                self.wait("sync", chy, at=chy.n)
                self.wait("sync", chek, at=chek.n)
            self.dma("sync", kc(), self.dram["kscr"][:, cci * R + t0: cci * R + t0 + W],
                     (chkl, 16))
            self.dma("sync", vc(), self.dram["vscr"][:, cci * R + t0: cci * R + t0 + W],
                     (chkl, 16))
            self.wait("scalar", chkl)
            self.act(ek(), kc(), AF.Exp)
            self.act(eu(), kc(), AF.Exp, bias=self.pcol("tf", cci))
            self.seal("scalar", chek)
            self.wait("vector", chek)
            self.tt("vector", ekv(), ek(), vc(), OP.mult)
            self.tt("vector", euv(), eu(), vc(), OP.mult)
            dc = self.sb["dcol"][:, cci:cci + 1]
            dbc = bass.AP(dc.tensor, dc.offset, [list(dc.ap[0]), [0, W]])
            self.scan(kc(), dbc, ekv(), 0.0)
            self.scan(vc(), dbc, ek(), 0.0)
            self.cp("vector", ekv(0, 1), euv(0, 1))
            self.tt("vector", ekv(1, W), kc(0, W - 1), euv(1, W), OP.add)
            self.cp("vector", euv(0, 1), eu(0, 1))
            self.tt("vector", euv(1, W), vc(0, W - 1), eu(1, W), OP.add)
            chyA = self.chp("p3_yA")
            chrec = self.chp("p3_rec")
            self.seal("vector", chyA)
            self.wait("scalar", chyA)
            self.act(eu(), euv(), AF.Ln)
            self.act(eu(), eu(), AF.Exp, scale=-1.0, inc=chrec)
            self.wait("vector", chrec)
            self.tt("vector", ek(), ekv(), eu(), OP.mult)
            self.tt("vector", rwkv(cci * T, (cci + 1) * T), ek(),
                    sr_bf(cci * T, (cci + 1) * T), OP.mult)
            self.seal("vector", chy)

        self.phase("p3c")
        chwo = self.chp("p3c_w")
        chomm = self.chp("p3c_mm")
        self.wait("gpsimd", self.chp("p3_pmm"))
        self.wait("tensor", self.chp("p3_ev"))
        self.wait("tensor", self.chp("p3_sig"))
        self.wait("tensor", self.chp("p3_y"))
        chxl2 = self.dch("sync", "p3c_xl")
        self.wait("vector", self.chp("p3_ek"), at=self.chp("p3_ek").n)
        self.wait("vector", self.chp("p3_y"), at=self.chp("p3_y").n)
        chadd = self.chp("p3c_add")
        chcst = self.chp("p3c_cst")
        chsmm = self.chp("p3c_smm")
        chrow2 = self.chp("p3c_row")
        wo = self.dram["woPM"][:, :].rearrange("p (cc e) -> p cc e", cc=CC)
        xpslots = {}
        for tq in range(T // 512):
            r0 = t0 + tq * 512
            for oc in range(CC):
                sl = oc % 2
                so = sl * CC * 128
                if chomm.n >= 2:
                    self.wait("gpsimd", chomm, at=chomm.n - 1)
                wn = self.dma("gpsimd",
                              wsl(so, so + CC * 128).rearrange("p (cc e) -> p cc e", cc=CC),
                              wo[:, :, oc * 128:(oc + 1) * 128])
                boff = (oc % 4) * 512
                self.wtok("tensor", wn)
                if oc >= 4:
                    self.wait("tensor", chadd, at=chadd.n - 3)
                for cci in range(CC):
                    self.mm(self.psum[:, boff:boff + 512],
                            wsl(so + cci * 128, so + (cci + 1) * 128),
                            rwkv(cci * T + tq * 512, cci * T + (tq + 1) * 512),
                            cci == 0, cci == CC - 1,
                            inc=(chomm if cci == CC - 1 else None))
                xslot = ftl((oc % 2) * 512, ((oc % 2) + 1) * 512)
                xn = self.dma("sync", xslot,
                              self.dram["resid2"][:, oc * R + r0: oc * R + r0 + 512],
                              (chxl2, 16))
                self.wait("vector", chomm)
                self.wtok("vector", xn)
                xp = p3f((oc % 2) * 512, (oc % 2 + 1) * 512)

                prev = xpslots.get(oc % 2)
                if prev is not None:
                    self.wtok("vector", prev)
                self.tt("vector", xp, self.psum[:, boff:boff + 512], xslot, OP.add)
                self.seal("vector", chadd)
                self.wait("scalar", chadd)
                if chsmm.n >= 4:
                    self.wait("scalar", chsmm, at=chsmm.n - 3)
                b0 = (oc % 2) * 512
                sbfA = mixk(b0, b0 + 512)
                s2bfA = mixv(b0, b0 + 512)
                self.act(sbfA, xp, AF.Copy, inc=chcst)
                self.act(s2bfA, xp, AF.Square, inc=chcst)
                self.wait("tensor", chcst)
                self.mm(self.bank1(4), self.sb["onescolb"][:, :], sbfA,
                        oc == 0, oc == CC - 1, inc=(chsmm, 1))
                self.mm(self.bank1(5), self.sb["onescolb"][:, :], s2bfA,
                        oc == 0, oc == CC - 1, inc=(chsmm, 1))
                self.wait("sync", chadd, at=chadd.n)
                xpslots[oc % 2] = self.dma(
                    "sync", self.dram["xprime"][:, oc * R + r0: oc * R + r0 + 512], xp)
            self.wait("vector", chsmm)
            rows = self.sb["rows"]
            mneg = rows[0:1, 0:512]
            rstd = rows[0:1, 512:1024]
            chsr2 = self.dch("sync", "p3_strow")
            if chsr2.n > 0:
                self.wait("vector", chsr2)
                self.wait("scalar", chsr2)
            self.ln_rows(self.bank1(4), self.bank1(5), 512, mneg, rstd,
                         1.0 / c.D)
            self.seal("vector", chrow2)
            self.wait("sync", chrow2, at=chrow2.n)
            self.dma("sync", self.dram["stat2"][0:1, r0:r0 + 512], mneg, (chsr2, 16))
            self.dma("sync", self.dram["stat2"][1:2, r0:r0 + 512], rstd, (chsr2, 16))

    def _p3_d(self):
        self.phase("p3d")
        c = self.cfg
        CC, R, T, D = c.CC, c.R, c.T, c.D
        self.barrier("p3d")
        self.AB.reset(); self.AF.reset(); self.AI.reset()
        HHC = c.HC // 2
        hw = c.HID // 2
        self.pd_wck = self.AB.alloc(CC * hw)
        self.pd_wcv = self.AB.alloc(HHC * D)
        self.pd_wcr = self.AB.alloc(CC * D)
        self.pd_mixk = self.AB.alloc(2 * CC * 256)
        self.pd_mixr = self.AB.alloc(2 * CC * 256)
        self.pd_kkb = self.AB.alloc(2 * 256)
        self.pd_sig2 = self.AB.alloc(CC * 256)
        self.pd_cst2 = self.AB.alloc(16 * 256)
        self.pd_f = self.AF.alloc(12288)
        self.pd_rec = {}
        self.pd_dstats = None
        NTB = T // 256
        chwl = self.dch("gpsimd", "pd_wl")
        self.dma("gpsimd", self.pd_wcr().rearrange("p (cc e) -> p cc e", cc=CC),
                 self.dram["wcrPM"][:, :].rearrange("p (cc e) -> p cc e", cc=CC),
                 (chwl, 16))
        for half in range(2):
            if half > 0:
                self.wait("gpsimd", self.chp("pd_kk"))
                self.wait("gpsimd", self.chp("pd_kv"))
                self.wait("sync", self.dch("gpsimd", "pd_ms"))
                self.wait_queue("sync", "scalar")
            self.dma("gpsimd",
                     self.pd_wck().rearrange("p (cc e) -> p cc e", cc=CC),
                     self.dram["wckPM"][:, :].rearrange("p (cc e) -> p cc e", cc=CC)
                     [:, :, half * hw:(half + 1) * hw], (chwl, 16))
            self.dma("gpsimd",
                     self.pd_wcv().rearrange("p (hc e) -> p hc e", hc=HHC),
                     self.dram["wcvPM"][:, :].rearrange("p (hc e) -> p hc e", hc=c.HC)
                     [:, half * HHC:(half + 1) * HHC, :], (chwl, 16))
            self.wait("tensor", chwl)
            for b in range(c.B):
                if half == 0:
                    self.memset("vector", self.sb["tails"][:, CC:2 * CC], 0.0)
                    self._pd_S(half, b, 0)
                    self._pd_M(half, b, 0)
                else:
                    self._pd_Ml(half, b, 0)
                for tb in range(NTB):
                    if tb + 1 < NTB:
                        if half == 0:
                            self._pd_S(half, b, tb + 1)
                            self._pd_M(half, b, tb + 1)
                        else:
                            self._pd_Ml(half, b, tb + 1)
                    self._pd_K(half, b, tb)
                    self._pd_E(half, b, tb)
        self._pd_flush_stats()

    def _pd_u(self, half, b, tb):
        return (half * self.cfg.B + b) * (self.cfg.T // 256) + tb

    def _pd_S(self, half, b, tb):
        c = self.cfg
        u = self._pd_u(half, b, tb)
        rec = self.pd_rec
        r0 = b * c.T + tb * 256
        ss = tb % 2
        rows = self.sb["rows"]
        mrow = rows[0:1, ss * 256:(ss + 1) * 256]
        srow = rows[0:1, 512 + ss * 256:512 + (ss + 1) * 256]
        chst = self.dch("sync", "pd_st")
        chbc = self.chp("pd_bc")
        if ("bc", u - 2) in rec:
            self.wait("sync", chbc, at=rec[("bc", u - 2)])
        self.dma("sync", mrow, self.dram["stat2"][0:1, r0:r0 + 256], (chst, 16))
        self.dma("sync", srow, self.dram["stat2"][1:2, r0:r0 + 256], (chst, 16))
        if ("mix", u - 1) in rec:
            self.wait("tensor", self.chp("pd_mix"), at=rec[("mix", u - 1)])
        self.wait("tensor", chst)
        self.mm(self.psum[:, 6 * 512:6 * 512 + 256], self.sb["onesrow"][:, :],
                mrow, True, True)
        self.mm(self.psum[:, 7 * 512:7 * 512 + 256], self.sb["onesrow"][:, :],
                srow, True, True, inc=chbc)
        rec[("bc", u)] = chbc.n

    def _pd_M(self, half, b, tb):
        c = self.cfg
        CC, R = c.CC, c.R
        u = self._pd_u(half, b, tb)
        rec = self.pd_rec
        r0 = b * c.T + tb * 256
        sl2 = u % 2
        p3f = self.pd_f
        chbc = self.chp("pd_bc")
        chmix = self.chp("pd_mix")
        chxl = self.dch("sync", "pd_xl")
        chms = self.dch("gpsimd", "pd_ms")
        self.wait("vector", chbc, at=rec[("bc", u)])
        if ("kkend", u - 2) in rec:
            self.wait("vector", self.chp("pd_kk"), at=rec[("kkend", u - 2)])
        if ("ms", u - 2) in rec:
            self.wait("vector", chms, at=rec[("ms", u - 2)])
        xr = self.dram["xprime"][:, :].rearrange("p (cc r) -> p cc r", cc=CC)
        for cci in range(CC):
            sl = cci % 2
            xpt = p3f.sub(sl * 256, (sl + 1) * 256)
            h2 = p3f.sub(512 + sl * 256, 512 + (sl + 1) * 256)
            hh2 = p3f.sub(1024 + sl * 256, 1024 + (sl + 1) * 256)
            dd2 = p3f.sub(1536 + sl * 256, 1536 + (sl + 1) * 256)
            if cci % 2 == 0:
                if cci > 0 or u > 0:
                    self.wait("sync", chmix, at=chmix.n)
                xd = p3f.sub(0, 512)
                self.dma("sync", xd().rearrange("p (k r) -> p k r", k=2),
                         xr[:, cci:cci + 2, r0:r0 + 256], (chxl, 16))
                self.wait("vector", chxl)
            self.tt("vector", h2(), xpt(), self.psum[:, 6 * 512:6 * 512 + 256],
                    OP.add)
            self.tt("vector", h2(), h2(), self.psum[:, 7 * 512:7 * 512 + 256],
                    OP.mult)
            self.ts("vector", h2(), h2(), self.pcol("ln2g", cci),
                    self.pcol("ln2b", cci), OP.mult, OP.add)
            tcol = CC + cci
            self.cp("vector", hh2(0, 1), self.sb["tails"][:, tcol:tcol + 1])
            self.cp("vector", hh2(1, 256), h2(0, 255))
            self.cp("vector", self.sb["tails"][:, tcol:tcol + 1], h2(255, 256))
            self.tt("vector", dd2(), h2(), hh2(), OP.subtract)
            self.stt("vector", self.pd_mixk(sl2 * CC * 256 + cci * 256,
                                            sl2 * CC * 256 + (cci + 1) * 256),
                     dd2(), self.pcol("cmk", cci), hh2(), OP.mult, OP.add)
            self.stt("vector", self.pd_mixr(sl2 * CC * 256 + cci * 256,
                                            sl2 * CC * 256 + (cci + 1) * 256),
                     dd2(), self.pcol("cmr", cci), hh2(), OP.mult, OP.add)
            self.seal("vector", chmix)
        rec[("mix", u)] = chmix.n
        self.wait("gpsimd", chmix, at=chmix.n)
        self.dma("gpsimd",
                 self.dram["mixkscr"][:, :].rearrange("p (cc r) -> p cc r", cc=CC)
                 [:, :, r0:r0 + 256],
                 self.pd_mixk(sl2 * CC * 256, (sl2 + 1) * CC * 256)
                 .rearrange("p (cc r) -> p cc r", cc=CC), (chms, 16))
        self.dma("gpsimd",
                 self.dram["mixrscr"][:, :].rearrange("p (cc r) -> p cc r", cc=CC)
                 [:, :, r0:r0 + 256],
                 self.pd_mixr(sl2 * CC * 256, (sl2 + 1) * CC * 256)
                 .rearrange("p (cc r) -> p cc r", cc=CC), (chms, 16))
        rec[("ms", u)] = chms.n

    def _pd_Ml(self, half, b, tb):
        c = self.cfg
        CC = c.CC
        u = self._pd_u(half, b, tb)
        rec = self.pd_rec
        r0 = b * c.T + tb * 256
        sl2 = u % 2
        chml = self.dch("sync", "pd_ml")
        if ("kkend", u - 2) in rec:
            self.wait("sync", self.chp("pd_kk"), at=rec[("kkend", u - 2)])
        if ("cr", u - 2) in rec:
            self.wait("sync", self.chp("pd_cr"), at=rec[("cr", u - 2)])
        self.dma("sync",
                 self.pd_mixk(sl2 * CC * 256, (sl2 + 1) * CC * 256)
                 .rearrange("p (cc r) -> p cc r", cc=CC),
                 self.dram["mixkscr"][:, :].rearrange("p (cc r) -> p cc r", cc=CC)
                 [:, :, r0:r0 + 256], (chml, 16))
        self.dma("sync",
                 self.pd_mixr(sl2 * CC * 256, (sl2 + 1) * CC * 256)
                 .rearrange("p (cc r) -> p cc r", cc=CC),
                 self.dram["mixrscr"][:, :].rearrange("p (cc r) -> p cc r", cc=CC)
                 [:, :, r0:r0 + 256], (chml, 16))
        rec[("ml", u)] = chml.n

    def _pd_K(self, half, b, tb):
        c = self.cfg
        CC, D = c.CC, c.D
        u = self._pd_u(half, b, tb)
        rec = self.pd_rec
        sl2 = u % 2
        nhc = c.HC // 2
        chkk = self.chp("pd_kk")
        chkkb = self.chp("pd_kkb")
        chkv = self.chp("pd_kv")
        chml = self.dch("sync", "pd_ml")
        p3f = self.pd_f

        def emit_kk(hc):
            if hc == 0:
                if half == 0:
                    self.wait("tensor", self.chp("pd_mix"), at=rec[("mix", u)])
                else:
                    self.wait("tensor", chml, at=rec[("ml", u)])
            boff = (4 + hc % 2) * 512
            if ("kkb", u, hc - 2) in rec:
                self.wait("tensor", chkkb, at=rec[("kkb", u, hc - 2)])
            elif ("kkb", u - 1, nhc - 2 + hc) in rec:
                self.wait("tensor", chkkb, at=rec[("kkb", u - 1, nhc - 2 + hc)])
            for cci in range(CC):
                self.mm(self.psum[:, boff:boff + 256],
                        self.pd_wck(cci * (c.HID // 2) + hc * 128,
                                    cci * (c.HID // 2) + (hc + 1) * 128),
                        self.pd_mixk(sl2 * CC * 256 + cci * 256,
                                     sl2 * CC * 256 + (cci + 1) * 256),
                        cci == 0, cci == CC - 1,
                        inc=(chkk if cci == CC - 1 else None))
            rec[("kk", u, hc)] = chkk.n
            if hc == nhc - 1:
                rec[("kkend", u)] = chkk.n

        def emit_scalar(hc):
            self.wait("scalar", chkk, at=rec[("kk", u, hc)])
            r1 = p3f(2048 + (hc % 2) * 256, 2048 + (hc % 2 + 1) * 256)
            boff = (4 + hc % 2) * 512
            self.act(r1, self.psum[:, boff:boff + 256], AF.Relu)
            self.act(self.pd_kkb((hc % 2) * 256, (hc % 2 + 1) * 256), r1,
                     AF.Square, inc=chkkb)
            rec[("kkb", u, hc)] = chkkb.n

        def emit_kv(hc):
            self.wait("tensor", chkkb, at=rec[("kkb", u, hc)])
            if hc == 0:
                for key, chn in ((("fin2", u - 1), "pd_fin2"),
                                 (("fin", u - 1), "pd_fin")):
                    if key in rec:
                        self.wait("tensor", self.chp(chn), at=rec[key])
                        break
            for oc in range(CC):
                kvoff = (oc // 2) * 512 + (oc % 2) * 256
                self.mm(self.psum[:, kvoff:kvoff + 256],
                        self.pd_wcv(hc * D + oc * 128, hc * D + (oc + 1) * 128),
                        self.pd_kkb((hc % 2) * 256, (hc % 2 + 1) * 256),
                        hc == 0 and oc % 2 == 0,
                        hc == nhc - 1 and oc % 2 == 1,
                        inc=(chkv if (hc == nhc - 1 and oc == CC - 1) else None))
            if hc == nhc - 1:
                rec[("kv", u)] = chkv.n

        emit_kk(0)
        emit_scalar(0)
        emit_kk(1)
        emit_scalar(1)
        for hc in range(nhc):
            emit_kv(hc)
            if hc + 2 < nhc:
                emit_kk(hc + 2)
                emit_scalar(hc + 2)

    def _pd_E(self, half, b, tb):
        if half == 0:
            self._pd_E0(half, b, tb)
        else:
            self._pd_E1(half, b, tb)

    def _pd_E0(self, half, b, tb):
        c = self.cfg
        CC, R = c.CC, c.R
        u = self._pd_u(half, b, tb)
        rec = self.pd_rec
        r0 = b * c.T + tb * 256
        p3f = self.pd_f
        chfin = self.chp("pd_fin")
        s4r = self.dram["s4"][:, :].rearrange("p (oc r) -> p oc r", oc=CC)
        self.wait("vector", self.chp("pd_kv"), at=rec[("kv", u)])
        for ocp in range(CC // 2):
            oc = 2 * ocp
            prev = getattr(self, "_pdkvtok", None)
            if prev is not None:
                self.wtok("vector", prev)
            for k in range(2):
                kvoff = ((oc + k) // 2) * 512 + ((oc + k) % 2) * 256
                kvp = p3f(2560 + k * 256, 2560 + (k + 1) * 256)
                self.cp("vector", kvp, self.psum[:, kvoff:kvoff + 256],
                        inc=chfin)
            self.wait("scalar", chfin, at=chfin.n)
            tok = self.dma("scalar", s4r[:, oc:oc + 2, r0:r0 + 256],
                           p3f(2560, 3072).rearrange("p (k r) -> p k r", k=2))
            self._pdkvtok = tok
            self.pd_rec[("s4tok", u, ocp)] = tok
        rec[("fin", u)] = chfin.n

    def _pd_flush_stats(self):
        if self.pd_dstats is None:
            return
        uprev, cstv, r0p = self.pd_dstats
        rec = self.pd_rec
        chsmm = self.chp("pd_smm")
        chrow = self.chp("pd_row")
        chsg = self.chp("pd_sg")
        self.wait("tensor", self.chp("pd_cst"), at=cstv)
        if ("sg", uprev) in rec:
            self.wait("tensor", chsg, at=rec[("sg", uprev)])
        for oc in range(self.cfg.CC):
            sA = self.pd_cst2(oc * 512, oc * 512 + 256)
            s2A = self.pd_cst2(oc * 512 + 256, oc * 512 + 512)
            self.mm(self.psum[0:1, 6 * 512:6 * 512 + 256],
                    self.sb["onescolb"][:, :], sA, oc == 0, oc == self.cfg.CC - 1,
                    inc=(chsmm, 1))
            self.mm(self.psum[0:1, 7 * 512:7 * 512 + 256],
                    self.sb["onescolb"][:, :], s2A, oc == 0,
                    oc == self.cfg.CC - 1, inc=(chsmm, 1))
        rec[("smm", uprev)] = chsmm.n
        self.wait("vector", chsmm)
        rows = self.sb["rows"]
        mneg = rows[0:1, 1024:1280]
        rstd = rows[0:1, 1280:1536]
        chsr = self.dch("sync", "pd_st3")
        if chsr.n > 0:
            self.wait("vector", chsr)
        self.ln_rows(self.psum[0:1, 6 * 512:6 * 512 + 256],
                     self.psum[0:1, 7 * 512:7 * 512 + 256], 256, mneg, rstd,
                     1.0 / self.cfg.D)
        self.seal("vector", chrow)
        rec[("row", uprev)] = chrow.n
        self.wait("sync", chrow, at=chrow.n)
        self.dma("sync", self.dram["stat3"][0:1, r0p:r0p + 256], mneg, (chsr, 16))
        self.dma("sync", self.dram["stat3"][1:2, r0p:r0p + 256], rstd, (chsr, 16))
        self.pd_dstats = None

    def _pd_E1(self, half, b, tb):
        c = self.cfg
        CC, R, D = c.CC, c.R, c.D
        u = self._pd_u(half, b, tb)
        rec = self.pd_rec
        r0 = b * c.T + tb * 256
        sl2 = u % 2
        p3f = self.pd_f
        chcr = self.chp("pd_cr")
        chsg = self.chp("pd_sg")
        chfin2 = self.chp("pd_fin2")
        chcst = self.chp("pd_cst")
        chkvl = self.dch("sync", "pd_kvl")
        self._pd_flush_stats()
        if ("row", u - 1) in rec:
            self.wait("tensor", self.chp("pd_row"), at=rec[("row", u - 1)])
        for oc in range(CC):
            boff = (6 + oc % 2) * 512
            if oc >= 2:
                self.wait("tensor", chsg, at=chsg.n - 1)
            for cci in range(CC):
                self.mm(self.psum[:, boff:boff + 256],
                        self.pd_wcr(cci * D + oc * 128, cci * D + (oc + 1) * 128),
                        self.pd_mixr(sl2 * CC * 256 + cci * 256,
                                     sl2 * CC * 256 + (cci + 1) * 256),
                        cci == 0, cci == CC - 1,
                        inc=(chcr if cci == CC - 1 else None))
            self.wait("scalar", chcr)
            self.act(self.pd_sig2(oc * 256, (oc + 1) * 256),
                     self.psum[:, boff:boff + 256], AF.Sigmoid, inc=chsg)
        rec[("cr", u)] = chcr.n
        rec[("sg", u)] = chsg.n
        s4r = self.dram["s4"][:, :].rearrange("p (oc r) -> p oc r", oc=CC)
        x2r = self.dram["resid2"][:, :].rearrange("p (oc r) -> p oc r", oc=CC)
        xpr = self.dram["xprime"][:, :].rearrange("p (oc r) -> p oc r", oc=CC)
        p4r = self.dram["pre4"][:, :].rearrange("p (oc r) -> p oc r", oc=CC)
        self.wait("vector", chsg, at=rec[("sg", u)])
        if ("smm", u - 1) in rec:
            self.wait("scalar", self.chp("pd_smm"), at=rec[("smm", u - 1)])
        for ocp in range(CC // 2):
            oc = 2 * ocp
            if ("fin2", u - 1) in rec and ocp == 0:
                self.wait("sync", chfin2, at=rec[("fin2", u - 1)])
            if ocp >= 1:
                self.wait("sync", chfin2, at=chfin2.n)
            self.dma("sync", p3f(2560, 3072).rearrange("p (k r) -> p k r", k=2),
                     s4r[:, oc:oc + 2, r0:r0 + 256], (chkvl, 16))
            self.dma("sync", p3f(3072, 3584).rearrange("p (k r) -> p k r", k=2),
                     x2r[:, oc:oc + 2, r0:r0 + 256], (chkvl, 16))
            self.dma("sync", p3f(3584, 4096).rearrange("p (k r) -> p k r", k=2),
                     xpr[:, oc:oc + 2, r0:r0 + 256], (chkvl, 16))
            self.wait("vector", chkvl)
            ppre = getattr(self, "_pdpretok", None)
            if ppre is not None:
                self.wtok("vector", ppre)
            self.wait("vector", chcst, at=chcst.n)
            for k in range(2):
                occ = oc + k
                kvoff = (occ // 2) * 512 + (occ % 2) * 256
                kvp = p3f(2560 + k * 256, 2560 + (k + 1) * 256)
                x2t = p3f(3072 + k * 256, 3072 + (k + 1) * 256)
                xpt = p3f(3584 + k * 256, 3584 + (k + 1) * 256)
                pre = p3f(4096 + k * 256, 4096 + (k + 1) * 256)
                self.tt("vector", kvp, self.psum[:, kvoff:kvoff + 256], kvp,
                        OP.add)
                self.tt("vector", kvp, kvp,
                        self.pd_sig2(occ * 256, (occ + 1) * 256), OP.mult)
                self.tt("vector", pre, kvp, x2t, OP.add)
                self.tt("vector", pre, pre, xpt, OP.add)
                self.seal("vector", chfin2)
                self.wait("scalar", chfin2)
                cb0 = occ * 512
                self.act(self.pd_cst2(cb0, cb0 + 256), pre, AF.Copy, inc=chcst)
                self.act(self.pd_cst2(cb0 + 256, cb0 + 512), pre, AF.Square,
                         inc=chcst)
            self.wait("sync", chfin2, at=chfin2.n)
            self._pdpretok = self.dma(
                "sync", p4r[:, oc:oc + 2, r0:r0 + 256],
                p3f(4096, 4608).rearrange("p (k r) -> p k r", k=2))
        rec[("fin2", u)] = chfin2.n
        self.pd_dstats = (u, self.chp("pd_cst").n, r0)

    def p4(self):
        self.phase("p4a")
        c = self.cfg
        CC, R, D = c.CC, c.R, c.D
        self.barrier("p4")
        self.AB.reset(); self.AF.reset(); self.AI.reset()
        ws = self.AB.alloc(CC * D)
        x3b = self.AB.alloc(CC * 512)
        cst = self.AB.alloc(CC * 1024)
        stile = self.AF.alloc(CC * 512)
        x3f = self.AF.alloc(CC * 512)
        ftl = self.AF.alloc(4 * 512)
        chwl = self.dch("sync", "p4_wl")
        chsg = self.chp("p4_sgn")
        for cci in range(CC):
            sl = cci % 2
            fo = ftl(sl * 1024, (sl + 1) * 1024)
            if cci >= 2:
                self.wait("sync", chsg, at=chsg.n - 1)
            self.dma("sync", fo, self.dram["wsPM"][:, cci * D:(cci + 1) * D],
                     (chwl, 16))
            self.wait("scalar", chwl)
            self.sign_op(ws(cci * D, (cci + 1) * D), fo)
            self.seal("scalar", chsg)
        chst = self.dch("sync", "p4_st")
        chbc = self.chp("p4_bc")
        chpl = self.dch("sync", "p4_pl")
        chap = self.chp("p4_ap")
        chcb = self.chp("p4_cb")
        chmm = self.chp("p4_mm")
        chadd = self.chp("p4_add")
        chcst = self.chp("p4_cst")
        chsmm = self.chp("p4_smm")
        chrow = self.chp("p4_row")
        chs4 = self.chp("p4_s4")
        s4slots = {}
        self.wait("tensor", chsg)
        self.wait("sync", chsg)
        for tq in range(R // 512):
            r0 = tq * 512
            rows = self.sb["rows"]
            if chsmm.n > 0:
                self.wait("scalar", chsmm, at=chsmm.n)
            self.wait("sync", chbc, at=chbc.n)
            chsr4 = self.dch("sync", "p4_strow")
            if chsr4.n > 0:
                self.wait("sync", chsr4)
            self.dma("sync", rows[0:1, 0:512], self.dram["stat3"][0:1, r0:r0 + 512],
                     (chst, 16))
            self.dma("sync", rows[0:1, 512:1024], self.dram["stat3"][1:2, r0:r0 + 512],
                     (chst, 16))
            self.wait("tensor", chst)
            self.wait("tensor", chap)
            self.mm(self.bank(6), self.sb["onesrow"][:, :], rows[0:1, 0:512],
                    True, True)
            self.mm(self.bank(7), self.sb["onesrow"][:, :], rows[0:1, 512:1024],
                    True, True, inc=chbc)
            self.wait("vector", chbc)
            for cci in range(CC):
                fo = ftl((cci % 2) * 512, (cci % 2 + 1) * 512)
                if chap.n >= 2:
                    self.wait("sync", chap, at=chap.n - 1)
                self.dma("sync", fo,
                         self.dram["pre4"][:, cci * R + r0: cci * R + r0 + 512],
                         (chpl, 16))
                self.wait("vector", chpl)
                x3 = x3f(cci * 512, (cci + 1) * 512)
                self.tt("vector", x3, fo, self.bank(6), OP.add)
                self.tt("vector", x3, x3, self.bank(7), OP.mult)
                self.seal("vector", chap)
                self.wait("scalar", chap)
                self.act(x3b(cci * 512, (cci + 1) * 512), x3, AF.Copy, inc=chcb)
            self.wait("tensor", chcb)
            for oc in range(CC):
                boff = (oc % 4) * 512
                if oc >= 4:
                    self.wait("tensor", chadd, at=chadd.n - 3)
                for cci in range(CC):
                    self.mm(self.psum[:, boff:boff + 512],
                            ws(cci * D + oc * 128, cci * D + (oc + 1) * 128),
                            x3b(cci * 512, (cci + 1) * 512),
                            cci == 0, cci == CC - 1,
                            inc=(chmm if cci == CC - 1 else None))
                self.wait("vector", chmm)
                sC = stile(oc * 512, (oc + 1) * 512)
                ps4 = s4slots.get(oc)
                if ps4 is not None:
                    self.wtok("vector", ps4)
                self.tt("vector", sC, self.psum[:, boff:boff + 512],
                        x3f(oc * 512, (oc + 1) * 512), OP.add, inc=chadd)
                self.wait("scalar", chadd)
                cpr = oc * 1024
                self.act(cst(cpr, cpr + 512), sC, AF.Copy, inc=chcst)
                self.act(cst(cpr + 512, cpr + 1024), sC, AF.Square, inc=chcst)
                self.wait("sync", chadd, at=chadd.n)
                s4slots[oc] = self.dma(
                    "sync", self.dram["s4"][:, oc * R + r0: oc * R + r0 + 512], sC)
            self.wait("tensor", chcst, at=chcst.n)
            if chrow.n > 0:
                self.wait("tensor", chrow, at=chrow.n)
            for oc in range(CC):
                cpr = oc * 1024
                self.mm(self.bank1(4), self.sb["onescolb"][:, :], cst(cpr, cpr + 512),
                        oc == 0, oc == CC - 1, inc=(chsmm, 1))
                self.mm(self.bank1(5), self.sb["onescolb"][:, :],
                        cst(cpr + 512, cpr + 1024), oc == 0, oc == CC - 1,
                        inc=(chsmm, 1))
            self.wait("vector", chsmm)
            mneg = rows[0:1, 0:512]
            rstd = rows[0:1, 512:1024]
            chsr4w = self.dch("sync", "p4_strow")
            if chsr4w.n > 0:
                self.wait("vector", chsr4w)
                self.wait("scalar", chsr4w)
            self.ln_rows(self.bank1(4), self.bank1(5), 512, mneg, rstd,
                         1.0 / c.D)
            self.seal("vector", chrow)
            self.wait("sync", chrow, at=chrow.n)
            chsr4 = self.dch("sync", "p4_strow")
            self.dma("sync", self.dram["stat4"][0:1, r0:r0 + 512], mneg, (chsr4, 16))
            self.dma("sync", self.dram["stat4"][1:2, r0:r0 + 512], rstd, (chsr4, 16))
        self.phase("p4b")
        chfs = self.dch("sync", "p4b_st")
        chfb = self.chp("p4b_bc")
        chfl = self.dch("sync", "p4b_l")
        chfo = self.chp("p4b_o")
        outq = self.dch("gpsimd", "q_out")
        self.wait_queue("sync", "sync")
        chsr4b = self.dch("sync", "p4_strow")
        if chsr4b.n > 0:
            self.wait("sync", chsr4b)
        for tq in range(R // 512):
            r0 = tq * 512
            rows = self.sb["rows"]
            self.wait("sync", chfb, at=chfb.n)
            self.dma("sync", rows[0:1, 0:512], self.dram["stat4"][0:1, r0:r0 + 512],
                     (chfs, 16))
            self.dma("sync", rows[0:1, 512:1024], self.dram["stat4"][1:2, r0:r0 + 512],
                     (chfs, 16))
            self.wait("tensor", chfs)
            self.wait("tensor", chfo)
            self.mm(self.bank(6), self.sb["onesrow"][:, :], rows[0:1, 0:512],
                    True, True)
            self.mm(self.bank(7), self.sb["onesrow"][:, :], rows[0:1, 512:1024],
                    True, True, inc=chfb)
            self.wait("vector", chfb)
            for cci in range(CC):
                fo = ftl((cci % 2) * 512, (cci % 2 + 1) * 512)
                ot = ftl((2 + cci % 2) * 512, (3 + cci % 2) * 512)
                if chfo.n >= 2:
                    self.wait("sync", chfo, at=chfo.n - 1)
                self.dma("sync", fo,
                         self.dram["s4"][:, cci * R + r0: cci * R + r0 + 512],
                         (chfl, 16))
                self.wait("vector", chfl)
                prev = getattr(self, "p4o%d" % (cci % 2), None)
                if prev is not None:
                    self.wtok("vector", prev)
                self.tt("vector", ot, fo, self.bank(6), OP.add)
                self.tt("vector", ot, ot, self.bank(7), OP.mult)
                self.seal("vector", chfo)
                self.wait("gpsimd", chfo, at=chfo.n)
                qn = self.dma("gpsimd",
                              self.dram["outPM"][:, cci * R + r0: cci * R + r0 + 512],
                              ot, (outq, 16))
                setattr(self, "p4o%d" % (cci % 2), qn)
        self.wait("gpsimd", outq)

    def build(self):
        self.declare()
        self.prologue()
        up = self.cfg.upto
        self.p1()
        if up != "p1":
            self.p2()
        if up not in ("p1", "p2"):
            self.p3()
        if up not in ("p1", "p2", "p3"):
            self.p4()
        self._final_wait()
        with self.nc.Block() as block:
            for eng in ENGS:
                lst = self.ops[eng]
                def make_body(lst):
                    def body(E):
                        i = 0
                        while i < len(lst):
                            ph = getattr(lst[i], "_ph", "init")
                            j = i
                            while j < len(lst) and getattr(lst[j], "_ph", "init") == ph:
                                j += 1
                            with self.nc.named_scope(ph):
                                for f in lst[i:j]:
                                    f(E)
                            i = j
                    return body
                getattr(block, eng)(make_body(lst))
        self.stack.close()
        return self.nc

    def _final_wait(self):
        for q in ("sync", "gpsimd", "scalar"):
            self.wait_queue("gpsimd", q)


def pm(a, din=None):
    a = np.ascontiguousarray(a)
    Din, cols = a.shape
    return np.ascontiguousarray(
        a.reshape(Din // 128, 128, cols).transpose(1, 0, 2).reshape(128, -1))


def unpm(a, Din, cols):
    return np.ascontiguousarray(
        a.reshape(128, Din // 128, cols).transpose(1, 0, 2).reshape(Din, cols))


def packvec(v):
    return np.ascontiguousarray(np.asarray(v).reshape(-1).reshape(8, 128).T)


def build_mask(win):
    sw = win + 128
    m = np.zeros((128, sw), np.float32)
    p = np.arange(128)[:, None]
    f = np.arange(128)[None, :]
    m[:, 0:128] = np.where(f >= p, 0.0, NEG)
    g = np.arange(sw - win)[None, :]
    m[:, win:sw] = np.where(g <= p, 0.0, NEG)
    return m


def host_inputs(cfg, core_inputs):
    d = {}
    ci = core_inputs
    d["xPM"] = pm(ci["x"])
    d["wqkvPM"] = pm(np.ascontiguousarray(ci["W_qkv"].T))
    d["woutPM"] = pm(np.ascontiguousarray(ci["W_out"].T))
    d["wup_nPM"] = pm(ci["w_up"])
    d["wgate_nPM"] = pm(ci["w_gate"])
    d["wdown_nPM"] = pm(ci["w_down"])
    d["wkPM"] = pm(np.ascontiguousarray(ci["Wk"].T))
    d["wvPM"] = pm(np.ascontiguousarray(ci["Wv"].T))
    d["wrPM"] = pm(np.ascontiguousarray(ci["Wr"].T))
    d["woPM"] = pm(np.ascontiguousarray(ci["Wo"].T))
    d["wcrPM"] = pm(np.ascontiguousarray(ci["Wcr"].T))
    d["wsPM"] = pm(np.ascontiguousarray(ci["weight"].T))
    d["wckPM"] = pm(np.ascontiguousarray(ci["Wck"].T))
    d["wcvPM"] = pm(np.ascontiguousarray(ci["Wcv"].T))
    for n, k in (("ln1g", "ln1_g"), ("ln1b", "ln1_b"), ("ln2g", "ln2_g"),
                 ("ln2b", "ln2_b"), ("td", "time_decay"), ("tf", "time_first"),
                 ("tmk", "tm_k"), ("tmv", "tm_v"), ("tmr", "tm_r"),
                 ("cmk", "cm_k"), ("cmr", "cm_r")):
        d[n + "_in"] = packvec(ci[k])
    d["ident_in"] = np.eye(128, dtype=np.float32)
    d["maskLR_in"] = build_mask(cfg.WIN)
    return d


def pm(a):
    a = np.ascontiguousarray(a)
    Din, cols = a.shape
    return np.ascontiguousarray(
        a.reshape(Din // 128, 128, cols).transpose(1, 0, 2).reshape(128, -1))


def unpm(a, Din, cols):
    return np.ascontiguousarray(
        np.asarray(a).reshape(128, Din // 128, cols).transpose(1, 0, 2)
        .reshape(Din, cols))


def packvec(v):
    return np.ascontiguousarray(
        np.asarray(v, dtype=np.float32).reshape(-1).reshape(8, 128).T)


def build_mask(win):
    sw = win + 128
    m = np.zeros((128, sw), np.float32)
    p = np.arange(128)[:, None]
    f = np.arange(128)[None, :]
    m[:, 0:128] = np.where(f >= p, 0.0, NEG)
    g = np.arange(128)[None, :]
    m[:, win:sw] = np.where(np.arange(sw - win)[None, :] <= p, 0.0, NEG)
    return m


def kernel(L0_attn_x, L0_attn_W_qkv, L0_attn_W_out, L0_ffn_w_gate, L0_ffn_w_up,
           L0_ffn_w_down, L1_attn_ln1_g, L1_attn_ln1_b, L1_attn_ln2_g,
           L1_attn_ln2_b, L1_attn_time_decay, L1_attn_time_first, L1_attn_tm_k,
           L1_attn_tm_v, L1_attn_tm_r, L1_attn_Wk, L1_attn_Wv, L1_attn_Wr,
           L1_attn_Wo, L1_attn_cm_k, L1_attn_cm_r, L1_attn_Wck, L1_attn_Wcr,
           L1_attn_Wcv, L1_ffn_weight):
    from concourse.bass_utils import run_bass_kernel_spmd

    NCORES = 8
    cfg = Cfg(T=2048, WIN=512, I_FFN=3072, HID=4096, n_cores=NCORES, upto="p4")
    mkb = MKFull(cfg)
    nc = mkb.build()

    f32 = lambda a: np.ascontiguousarray(np.asarray(a), dtype=np.float32)
    base = {
        "wqkvPM": pm(f32(L0_attn_W_qkv).T),
        "woutPM": pm(f32(L0_attn_W_out).T),
        "wup_nPM": pm(f32(L0_ffn_w_up)),
        "wgate_nPM": pm(f32(L0_ffn_w_gate)),
        "wdown_nPM": pm(f32(L0_ffn_w_down)),
        "wkPM": pm(f32(L1_attn_Wk).T),
        "wvPM": pm(f32(L1_attn_Wv).T),
        "wrPM": pm(f32(L1_attn_Wr).T),
        "woPM": pm(f32(L1_attn_Wo).T),
        "wcrPM": pm(f32(L1_attn_Wcr).T),
        "wsPM": pm(f32(L1_ffn_weight).T),
        "wckPM": pm(f32(L1_attn_Wck).T),
        "wcvPM": pm(f32(L1_attn_Wcv).T),
        "ln1g_in": packvec(L1_attn_ln1_g), "ln1b_in": packvec(L1_attn_ln1_b),
        "ln2g_in": packvec(L1_attn_ln2_g), "ln2b_in": packvec(L1_attn_ln2_b),
        "td_in": packvec(L1_attn_time_decay), "tf_in": packvec(L1_attn_time_first),
        "tmk_in": packvec(L1_attn_tm_k), "tmv_in": packvec(L1_attn_tm_v),
        "tmr_in": packvec(L1_attn_tm_r), "cmk_in": packvec(L1_attn_cm_k),
        "cmr_in": packvec(L1_attn_cm_r),
        "ident_in": np.eye(128, dtype=np.float32),
        "maskLR_in": build_mask(cfg.WIN),
    }
    x = f32(L0_attn_x)
    in_maps = []
    for c in range(NCORES):
        m = dict(base)
        xs = x[2 * c:2 * c + 2]
        m["xPM"] = pm(np.ascontiguousarray(
            xs.transpose(2, 0, 1).reshape(1024, cfg.R)))
        in_maps.append(m)
    res = run_bass_kernel_spmd(nc, in_maps, core_ids=list(range(NCORES)))
    out = np.empty((16, 2048, 1024), np.float32)
    for c in range(NCORES):
        o = unpm(res.results[c]["outPM"], 1024, cfg.R)
        out[2 * c:2 * c + 2] = o.reshape(1024, 2, 2048).transpose(1, 2, 0)
    return out
